# version 18
# baseline (speedup 1.0000x reference)
import numpy as np
from contextlib import ExitStack

import concourse.bass as bass
import concourse.mybir as mybir

F32 = mybir.dt.float32
BF16 = mybir.dt.bfloat16
AF = mybir.ActivationFunctionType
ALU = mybir.AluOpType
AX = mybir.AxisListType

T = 2048
D = 1024
DC = 8
S = 2
NT = 4
PAIRS = 8
SC = 16
FC = 16
NE = 8
TH = 1024
EPS = 1e-5

ENGS = ("sync", "tensor", "vector", "scalar", "gpsimd")
MERGE_SEQ = True



def _chunk128(a):
    n = a.shape[0] // 128
    return np.ascontiguousarray(a.reshape(n, 128, -1).transpose(1, 0, 2).reshape(128, -1))


def _colvec(a):
    n = a.shape[0] // 128
    return np.ascontiguousarray(a.reshape(n, 128).T)


def host_prep(inp, n_cores=8):
    f = lambda a: np.ascontiguousarray(a.astype(np.float32))
    c = {}
    for nm in ("wq", "wk", "wv"):
        w = np.asarray(inp[f"L0_attn_{nm}"])
        wT = w.transpose(2, 0, 1).reshape(D, D)
        c[nm + "_h"] = f(np.stack([_chunk128(wT[:, p * 128:(p + 1) * 128]) for p in range(PAIRS)]))
    woT = np.asarray(inp["L0_attn_wo"]).T
    c["wo_h"] = f(np.stack([_chunk128(woT[:, j * 128:(j + 1) * 128]) for j in range(8)]))
    for nm, key in (("wr", "L0_ffn_W_r"), ("wkk", "L0_ffn_W_k"), ("wvv", "L0_ffn_W_v")):
        wT = np.asarray(inp[key]).T
        c[nm + "_h"] = f(np.stack([_chunk128(wT[:, j * 128:(j + 1) * 128]) for j in range(8)]))
    c["mu_r_h"] = f(_colvec(np.asarray(inp["L0_ffn_mu_r"])))
    c["mu_k_h"] = f(_colvec(np.asarray(inp["L0_ffn_mu_k"])))
    c["bre_h"] = f(_chunk128(np.asarray(inp["L1_attn_B_re"])))
    c["bim_h"] = f(_chunk128(np.asarray(inp["L1_attn_B_im"])))
    c["cre_h"] = f(_chunk128(np.asarray(inp["L1_attn_C_re"]).T))
    c["cim_h"] = f(_chunk128(np.asarray(inp["L1_attn_C_im"]).T))
    c["dt_h"] = f(_chunk128(np.asarray(inp["L1_attn_D"]).T))
    for nm, key in (("vre", "L1_attn_x_re"), ("vim", "L1_attn_x_im"), ("vnu", "L1_attn_nu_log"),
                    ("vth", "L1_attn_theta_log"), ("vga", "L1_attn_gamma_log")):
        c[nm + "_h"] = f(_colvec(np.asarray(inp[key])))
    c["gw_h"] = f(_chunk128(np.asarray(inp["L1_ffn_gate_w"]).T))
    c["ebias_h"] = f(np.tile(np.asarray(inp["L1_ffn_expert_bias"]), (128, 16)))
    wuT = np.asarray(inp["L1_ffn_w_up"]).transpose(0, 2, 1)
    c["wu_h"] = f(np.stack([
        np.stack([_chunk128(wuT[e][:, fs * 128:(fs + 1) * 128]) for fs in range(FC)])
        for e in range(NE)]))
    wdT = np.asarray(inp["L1_ffn_w_down"]).transpose(0, 2, 1)
    c["wd_h"] = f(np.stack([_chunk128(wdT[e]) for e in range(NE)]))
    c["eye_h"] = np.eye(128, dtype=np.float32)
    c["tri_h"] = np.triu(np.ones((128, 128), dtype=np.float32))
    ind2 = np.zeros((128, 2), dtype=np.float32); ind2[0:64, 0] = 1.0; ind2[64:128, 1] = 1.0
    c["ind2_h"] = ind2
    ind2t = np.zeros((2, 256), dtype=np.float32); ind2t[0, 0:128] = 1.0; ind2t[1, 128:256] = 1.0
    c["ind2t_h"] = ind2t
    ind8 = np.zeros((8, 8 * 128), dtype=np.float32)
    for e in range(8):
        ind8[e, e * 128:(e + 1) * 128] = 1.0
    c["ind8_h"] = ind8
    c["ones_h"] = np.ones((128, 128), dtype=np.float32)
    x = np.asarray(inp["L0_attn_x"])
    xs = []
    for ci in range(n_cores):
        sl = x[2 * ci:2 * ci + 2]
        a = sl.transpose(0, 2, 1).reshape(S, DC, 128, T).transpose(0, 2, 1, 3).reshape(S, 128, DC * T)
        xs.append(f(a))
    return c, xs


def unprep_x(y):
    y = np.asarray(y)
    n = y.shape[0]
    return y.reshape(n, 128, DC, T).transpose(0, 2, 1, 3).reshape(n, D, T).transpose(0, 2, 1)



class Sem:
    def __init__(self, h):
        self.h = h
        self.n = 0

    def inc(self, k=1):
        self.n += k
        return self.n


class Builder:
    def __init__(self, nc, stack):
        self.nc = nc
        self.stack = stack
        self.tap_list = []
        self._ctgt = {}
        self._ck = 0
        self.q = {e: [] for e in ENGS}
        e = stack.enter_context
        self.sW = Sem(e(nc.semaphore("sW")))
        self.sW2 = Sem(e(nc.semaphore("sW2")))
        self.sW3 = Sem(e(nc.semaphore("sW3")))
        self.sW4 = Sem(e(nc.semaphore("sW4")))
        self._wsems = (self.sW, self.sW2, self.sW3, self.sW4)
        self._plc = {0: (0, 0, 0, 0)}
        self.sX = Sem(e(nc.semaphore("sX")))
        self.sPE = Sem(e(nc.semaphore("sPE")))
        self.sACT = Sem(e(nc.semaphore("sACT")))
        self.sDVE = Sem(e(nc.semaphore("sDVE")))
        self.sPL = Sem(e(nc.semaphore("sPL")))
        self._alloc()
        self.d = {}
        self._declare_dram()

    def _alloc(self):
        e = self.stack.enter_context
        nc = self.nc
        self.x_sb = e(nc.sbuf_tensor("x_sb", [128, DC * T], F32))
        self.xn = e(nc.sbuf_tensor("xn", [128, DC * T], BF16))
        self.big1 = e(nc.sbuf_tensor("big1", [128, DC * T], BF16))
        self.big2 = e(nc.sbuf_tensor("big2", [128, DC * T], BF16))
        self.wring = e(nc.sbuf_tensor("wring", [128, 7 * 1024], BF16))
        fpool = e(nc.sbuf_tensor("fpool", [128, 6144], F32))
        self.fs1 = fpool[:, 0:2048]
        self.fs2 = fpool[:, 2048:4096]
        self.wstg = fpool[:, 4096:6144]
        self._smf = e(nc.sbuf_tensor("smallf", [128, 448], F32))
        self._smb = e(nc.sbuf_tensor("smallb", [128, 3584], BF16))
        cf = [0]
        def vf(w, rows=128):
            a = self._smf[0:rows, cf[0]:cf[0] + w]; cf[0] += w; return a
        cb = [0]
        def vb(w, rows=128):
            a = self._smb[0:rows, cb[0]:cb[0] + w]; cb[0] += w; return a
        self.eye = vf(128)
        self.onesf = vf(128)
        self.negCb = vf(2)
        self.mu_r = vf(8)
        self.mu_k = vf(8)
        self.ebias = vf(128)
        self.cvec = vf(8)
        self.l1w = self.fs2[:, 0:192]
        self.scT = self.fs2[:, 192:320]
        self.gtmp = self.fs2[:, 320:832]
        self.smax = vf(8, rows=2)
        self._fc1 = vf(4)
        self._fc2 = vf(4)
        self._af1 = vf(4)
        self._fcr = vf(16)
        self._fci = 0
        self.tri = vb(128)
        self.ind2 = vb(2)
        self.ind2t = vb(256, rows=2)
        self.ind8 = vb(1024, rows=8)
        self.gwsb = vb(64)
        self.l1wb = vb(32)
        self.smaxb = vb(2, rows=2)
        self.w2rows = vb(2048, rows=8)
        self.psA = e(nc.psum_tensor("psA", [128, 2048], F32))
        self.psB = e(nc.psum_tensor("psB", [128, 2048], F32))

    def _declare_dram(self):
        nc = self.nc

        def din(name, shape, dtype=F32):
            self.d[name] = nc.dram_tensor(name, list(shape), dtype, kind="ExternalInput").ap()
        din("x_in", [S, 128, DC * T])
        for nm in ("wq_h", "wk_h", "wv_h", "wo_h", "wr_h", "wkk_h", "wvv_h"):
            din(nm, [8, 128, 1024])
        din("mu_r_h", [128, 8]); din("mu_k_h", [128, 8])
        din("bre_h", [128, SC * 1024]); din("bim_h", [128, SC * 1024])
        din("cre_h", [128, SC * 1024]); din("cim_h", [128, SC * 1024])
        din("dt_h", [128, DC * 1024])
        for nm in ("vre_h", "vim_h", "vnu_h", "vth_h", "vga_h"):
            din(nm, [128, 16])
        din("gw_h", [128, 64]); din("ebias_h", [128, 128])
        din("wu_h", [NE, FC, 128, 1024])
        din("wd_h", [NE, 128, FC * 1024])
        din("eye_h", [128, 128]); din("tri_h", [128, 128]); din("ind2_h", [128, 2])
        din("ind2t_h", [2, 256]); din("ind8_h", [8, 8 * 128]); din("ones_h", [128, 128])
        self.d["x_out"] = nc.dram_tensor("x_out", [S, 128, DC * T], F32, kind="ExternalOutput").ap()
        self.d["mtot_scr"] = nc.dram_tensor("mtot_scr", [128, DC * 1024], BF16, kind="Internal").ap()

    def cload(self, dram_ap, dst, pe_war=0, rows=128, width=1024, queue="gpsimd",
              dve_war=0):
        sPE = self.sPE
        k = getattr(self, "_ck", 0)
        self._ck = k + 1
        sWp = self._wsems[k % 4]
        prev = sWp.n
        sWp.inc(16)
        @self.on(queue)
        def _(E, prev=prev, pe_war=pe_war, dve_war=dve_war, dst=dst,
              dram_ap=dram_ap, sWp=sWp):
            if prev:
                E.wait_ge(sWp.h, prev)
            if pe_war:
                E.wait_ge(sPE.h, pe_war)
            if dve_war:
                E.wait_ge(self.sDVE.h, dve_war)
            E.dma_start(out=dst, in_=dram_ap).then_inc(sWp.h, 16)
        tgt = self.sPL.inc(1)
        self._plc[tgt] = (self.sW.n, self.sW2.n, self.sW3.n, self.sW4.n)
        return tgt

    def wpl(self, E, tgt):
        for sem, cnt in zip(self._wsems, self._plc[tgt]):
            if cnt:
                E.wait_ge(sem.h, cnt)

    def dfence(self, E):
        i = self._fci
        self._fci = (i + 1) % 8
        return E.tensor_copy(self._fcr[:, i * 2:i * 2 + 2],
                             self._fc2[:, 0:2]).then_inc(self.sDVE.h, 1)

    def on(self, eng):
        def deco(fn):
            self.q[eng].append(fn)
        return deco

    def maybe_flush(self, name):
        if getattr(self, "seq_merge", False):
            import os
            keys = os.environ.get("MK_FLUSH", "")
            if keys:
                for k in keys.split(","):
                    if k and name.startswith(k):
                        self.flush(name)
                        return
            return
        self.flush(name)

    def flush(self, name):
        with self.nc.Block(name) as blk:
            for eng in ENGS:
                fns = self.q[eng]
                if fns:
                    getattr(blk, eng)(lambda E, fns=tuple(fns): [f(E) for f in fns])
        self.q = {e: [] for e in ENGS}

    def tap(self, name, buf, shape, dtype):
        out = self.nc.dram_tensor("tap_" + name, list(shape), dtype, kind="ExternalOutput").ap()
        self.tap_list.append((out, buf))

    def flush_taps(self):
        if not self.tap_list:
            return
        taps, sX = tuple(self.tap_list), self.sX
        base = sX.n
        sX.inc(16 * len(taps))
        @self.on("sync")
        def _(E, base=base):
            for i, (out, buf) in enumerate(taps):
                if base + 16 * i > 0:
                    E.wait_ge(sX.h, base + 16 * i)
                E.dma_start(out=out, in_=buf).then_inc(sX.h, 16)
        self._tapn = getattr(self, "_tapn", 0) + 1
        self.flush(f"taps{self._tapn}")
        self.tap_list = []

    def finish(self):
        sX = self.sX
        @self.on("sync")
        def _(E, n=sX.n):
            E.wait_ge(sX.h, n)
        self.flush("finish")

    def load_consts(self):
        d, sX, sW, sDVE, sPL = self.d, self.sX, self.sW, self.sDVE, self.sPL
        loads = [(self.eye, "eye_h"), (self.mu_r, "mu_r_h"), (self.mu_k, "mu_k_h"),
                 (self.ebias, "ebias_h"), (self.onesf, "ones_h")]
        for nm, buf in (("vre_h", 0), ("vim_h", 1), ("vnu_h", 2), ("vth_h", 3), ("vga_h", 4)):
            loads.append((None, nm, buf))
        casts = [(self.tri, "tri_h"), (self.ind2, "ind2_h"), (self.ind2t, "ind2t_h"),
                 (self.ind8, "ind8_h"), (self.gwsb, "gw_h")]
        l1w = self.l1w
        @self.on("sync")
        def _(E, base=sX.n):
            for i, item in enumerate(loads):
                if base + 16 * i > 0:
                    E.wait_ge(sX.h, base + 16 * i)
                if item[0] is None:
                    _, nm, slot = item
                    E.dma_start(out=l1w[:, slot * 16:(slot + 1) * 16], in_=d[nm]).then_inc(sX.h, 16)
                else:
                    buf, nm = item
                    E.dma_start(out=buf[:], in_=d[nm]).then_inc(sX.h, 16)
        sX.inc(16 * len(loads))
        shapes = {"tri_h": (128, 128), "ind2_h": (128, 2), "ind2t_h": (2, 256),
                  "ind8_h": (8, 1024), "gw_h": (128, 64)}
        last_cast = 0
        for buf, nm in casts:
            r, w = shapes[nm]
            last_cast = self.cload(d[nm], buf[:], rows=r, width=w)
        @self.on("vector")
        def _(E, tx=sX.n):
            E.wait_ge(sX.h, tx)
            E.tensor_copy(self._fc2, self.eye[0:128, 0:4])
            E.tensor_copy(self._af1, self.eye[0:128, 0:4])
        for eng in ("vector", "scalar", "tensor"):
            @self.on(eng)
            def _(E, tx=sX.n, tc=last_cast):
                E.wait_ge(sX.h, tx)
                self.wpl(E, tc)
        self.flush("consts")

    def load_x(self, s):
        d, sX = self.d, self.sX
        prev_store = sX.n
        tgt = sX.inc(16)
        @self.on("sync")
        def _(E, s=s, prev_store=prev_store):
            E.wait_ge(sX.h, prev_store)
            E.dma_start(out=self.x_sb[:], in_=d["x_in"][s]).then_inc(sX.h, 16)
        for eng in ("vector", "scalar", "tensor"):
            @self.on(eng)
            def _(E, tgt=tgt):
                E.wait_ge(sX.h, tgt)
        self.maybe_flush(f"loadx{s}")

    def store_x(self, s, entry_dve=0):
        d, sX, sDVE = self.d, self.sX, self.sDVE
        prev = sX.n
        sX.inc(16)
        @self.on("sync")
        def _(E, s=s, ed=entry_dve, prev=prev):
            if ed:
                E.wait_ge(sDVE.h, ed)
            if prev:
                E.wait_ge(sX.h, prev)
            E.dma_start(out=d["x_out"][s], in_=self.x_sb[:]).then_inc(sX.h, 16)
        self.maybe_flush(f"storex{s}")

    def ln(self, tag="", entry_dve=0):
        x, xn = self.x_sb, self.xn
        fs1, fs2, onesf = self.fs1, self.fs2, self.onesf
        psA, psB = self.psA, self.psB
        sPE, sACT, sDVE = self.sPE, self.sACT, self.sDVE
        p_bc_prev = [0]
        d_norm_prev = [0]
        for h in range(2):
            t0 = h * TH
            accA = fs2[:, 0:TH]
            sqs = (fs2[:, TH:2 * TH], fs1[:, 0:TH])
            act0 = sACT.n
            pe0s = sPE.n
            @self.on("scalar")
            def _(E, t0=t0, pe0s=pe0s, pb=p_bc_prev[0], dn=d_norm_prev[0],
                  ed=entry_dve if h == 0 else 0, sqs=sqs):
                if ed:
                    E.wait_ge(sDVE.h, ed)
                if pb:
                    E.wait_ge(sPE.h, pb)
                if dn:
                    E.wait_ge(sDVE.h, dn)
                for ci in range(DC):
                    if ci >= 2:
                        E.wait_ge(sPE.h, pe0s + (ci - 1))
                    E.activation(sqs[ci % 2], x[:, ci * T + t0:ci * T + t0 + TH],
                                 AF.Square).then_inc(sACT.h, 1)
            sACT.inc(DC)
            @self.on("vector")
            def _(E, t0=t0, pb=p_bc_prev[0]):
                if pb:
                    E.wait_ge(sPE.h, pb)
                E.tensor_add(accA, x[:, t0:t0 + TH], x[:, T + t0:T + t0 + TH])
                for ci in range(2, DC):
                    ti = E.tensor_add(accA, accA, x[:, ci * T + t0:ci * T + t0 + TH])
                ti.then_inc(sDVE.h, 1)
            sDVE.inc(1)
            d_sums = sDVE.n
            @self.on("tensor")
            def _(E, act0=act0, d_sums=d_sums, sqs=sqs):
                for ci in range(DC):
                    E.wait_ge(sACT.h, act0 + ci + 1)
                    for nt in range(2):
                        mi = E.matmul(psA[0:1, 1024 + nt * 512:1024 + (nt + 1) * 512],
                                      onesf[:, 0:1],
                                      sqs[ci % 2][:, nt * 512:(nt + 1) * 512],
                                      start=(ci == 0), stop=(ci == DC - 1))
                    mi.then_inc(sPE.h, 1)
                E.wait_ge(sDVE.h, d_sums)
                for nt in range(2):
                    mi = E.matmul(psA[0:1, nt * 512:(nt + 1) * 512], onesf[:, 0:1],
                                  accA[:, nt * 512:(nt + 1) * 512], start=True, stop=True)
                mi.then_inc(sPE.h, 1)
            sPE.inc(DC)
            p_sums = sPE.inc(1)
            slot0 = fs1[0:1, TH:2 * TH]
            slot1 = fs1[0:1, 0:TH]
            psmean = psA[0:1, 0:TH]
            psms = psA[0:1, TH:2 * TH]
            @self.on("vector")
            def _(E, p_sums=p_sums):
                E.wait_ge(sPE.h, p_sums)
                E.tensor_scalar(slot1, psmean, 1.0 / D, None, ALU.mult)
                self.dfence(E)
                E.tensor_mul(slot0, slot1, slot1)
                E.tensor_scalar(psms, psms, 1.0 / D, None, ALU.mult)
                self.dfence(E)
                E.tensor_sub(psms, psms, slot0)
                self.dfence(E)
                E.tensor_scalar(psms, psms, EPS, None, ALU.add)
                self.dfence(E)
            d_var = sDVE.inc(4)
            a_ln = sACT.inc(1)
            @self.on("scalar")
            def _(E, d_var=d_var, a_ln=a_ln):
                E.wait_ge(sDVE.h, d_var)
                E.activation(psms, psms, AF.Ln).then_inc(sACT.h, 1)
                E.wait_ge(sACT.h, a_ln)
                E.activation(slot0, psms, AF.Exp, scale=-0.5).then_inc(sACT.h, 1)
            a_rstd = sACT.inc(1)
            @self.on("vector")
            def _(E, a_rstd=a_rstd):
                E.wait_ge(sACT.h, a_rstd)
                E.scalar_tensor_tensor(slot1, slot1, -1.0, slot0, ALU.mult, ALU.mult)
                self.dfence(E)
            d_nmr = sDVE.inc(1)
            @self.on("tensor")
            def _(E, d_nmr=d_nmr):
                E.wait_ge(sDVE.h, d_nmr)
                for nt in range(2):
                    E.matmul(psB[:, nt * 512:(nt + 1) * 512], onesf[0:1, :],
                             fs1[0:1, 1024 + nt * 512:1024 + nt * 512 + 512], start=True, stop=True)
                for nt in range(2):
                    mi = E.matmul(psB[:, 1024 + nt * 512:1024 + (nt + 1) * 512], onesf[0:1, :],
                                  fs1[0:1, nt * 512:nt * 512 + 512], start=True, stop=True)
                mi.then_inc(sPE.h, 1)
            p_bc = sPE.inc(1)
            dve_nb = sDVE.n
            @self.on("vector")
            def _(E, t0=t0, p_bc=p_bc):
                E.wait_ge(sPE.h, p_bc)
                for ci in range(DC):
                    E.tensor_mul(fs1[:, 0:TH], x[:, ci * T + t0:ci * T + t0 + TH], psB[:, 0:TH])
                    self.dfence(E)
                    E.tensor_add(xn[:, ci * T + t0:ci * T + t0 + TH], fs1[:, 0:TH],
                                 psB[:, TH:2 * TH]).then_inc(sDVE.h, 1)
            sDVE.inc(2 * DC)
            self._ln_chain = [dve_nb + 2 * (ci + 1) for ci in range(DC)]
            p_bc_prev[0] = p_bc
            d_norm_prev[0] = sDVE.n
        self.maybe_flush("ln" + tag)
        return sDVE.n

    def attn(self, tag="", entry_dve=0, entry_pe=0):
        x, xn, big1, big2, wring = self.x_sb, self.xn, self.big1, self.big2, self.wring
        fs1, fs2 = self.fs1, self.fs2
        psA, psB = self.psA, self.psB
        sPE, sACT, sDVE, sW, sPL = self.sPE, self.sACT, self.sDVE, self.sW, self.sPL
        d = self.d
        onesf, ind2, ind2t, tri = self.onesf, self.ind2, self.ind2t, self.tri
        smax, smaxb, negCb = self.smax, self.smaxb, self.negCb
        qp = big2[:, 0:2048]
        kp = big2[:, 2048:4096]
        vbufs = (big2[:, 4096:5136], big2[:, 5248:6288])
        eslot = [big2[:, 6400 + i * 1024:6400 + (i + 1) * 1024] for i in range(3)]
        qsq = big2[:, 9472:11520]
        ksq = big2[:, 11520:13568]
        wslot = [wring[:, i * 1024:(i + 1) * 1024] for i in range(7)]

        pair_q_done = [0] * PAIRS
        pair_k_done = [0] * PAIRS
        pair_v_done = [0] * PAIRS
        pair_vend = [0] * PAIRS
        pair_all_done = [0] * PAIRS
        prev_head_evac = [entry_dve]

        for p in range(PAIRS):
            wq_t = self.cload(d["wq_h"][p], wslot[(p % 2) + 0][:, :],
                              pe_war=pair_q_done[p - 2] if p >= 2 else entry_pe)
            wk_t = self.cload(d["wk_h"][p], wslot[(p % 2) + 2][:, :],
                              pe_war=pair_k_done[p - 2] if p >= 2 else entry_pe)
            wv_t = self.cload(d["wv_h"][p], wslot[(p % 2) + 4][:, :],
                              pe_war=pair_vend[p - 2] if p >= 2 else entry_pe)
            qev_prev = prev_head_evac[0]
            ln_chain = tuple(getattr(self, "_ln_chain", ())) if p == 0 else ()
            @self.on("tensor")
            def _(E, p=p, wq_t=wq_t, qev=qev_prev, ln_chain=ln_chain):
                self.wpl(E, wq_t)
                if qev and not ln_chain:
                    E.wait_ge(sDVE.h, qev)
                sl = wslot[(p % 2) + 0]
                for c in range(DC):
                    if ln_chain:
                        E.wait_ge(sDVE.h, ln_chain[c])
                    for nt in range(NT):
                        mi = E.matmul(psA[:, nt * 512:(nt + 1) * 512], sl[:, c * 128:(c + 1) * 128],
                                      xn[:, c * T + nt * 512:c * T + (nt + 1) * 512],
                                      start=(c == 0), stop=(c == DC - 1))
                mi.then_inc(sPE.h, 1)
            pair_q_done[p] = sPE.inc(1)
            @self.on("scalar")
            def _(E, p=p, t=pair_q_done[p]):
                E.wait_ge(sPE.h, t)
                E.activation(qp, psA[:, 0:2048], AF.Identity)
                E.activation(qsq, psA[:, 0:2048], AF.Square).then_inc(sACT.h, 1)
            a_qev = sACT.inc(1)
            @self.on("tensor")
            def _(E, p=p, wk_t=wk_t):
                self.wpl(E, wk_t)
                sl = wslot[(p % 2) + 2]
                for c in range(DC):
                    for nt in range(NT):
                        mi = E.matmul(psB[:, nt * 512:(nt + 1) * 512], sl[:, c * 128:(c + 1) * 128],
                                      xn[:, c * T + nt * 512:c * T + (nt + 1) * 512],
                                      start=(c == 0), stop=(c == DC - 1))
                mi.then_inc(sPE.h, 1)
            pair_k_done[p] = sPE.inc(1)
            @self.on("scalar")
            def _(E, p=p, t=pair_k_done[p]):
                E.wait_ge(sPE.h, t)
                E.activation(kp, psB[:, 0:2048], AF.Identity)
                E.activation(ksq, psB[:, 0:2048], AF.Square).then_inc(sACT.h, 1)
            a_kev = sACT.inc(1)
            @self.on("tensor")
            def _(E, a_qev=a_qev, a_kev=a_kev):
                E.wait_ge(sACT.h, a_qev)
                for nt in range(NT):
                    mi = E.matmul(psA[0:2, nt * 512:(nt + 1) * 512], ind2[:, :],
                                  qsq[:, nt * 512:(nt + 1) * 512], start=True, stop=True)
                E.wait_ge(sACT.h, a_kev)
                for nt in range(NT):
                    mi = E.matmul(psB[0:2, nt * 512:(nt + 1) * 512], ind2[:, :],
                                  ksq[:, nt * 512:(nt + 1) * 512], start=True, stop=True)
                mi.then_inc(sPE.h, 1)
            p_norm = sPE.inc(1)
            @self.on("vector")
            def _(E, p_norm=p_norm):
                E.wait_ge(sPE.h, p_norm)
                E.reduce_max(smax[:, 0:1], psA[0:2, 0:2048], AX.X)
                E.reduce_max(smax[:, 1:2], psB[0:2, 0:2048], AX.X)
                self.dfence(E)
                E.tensor_mul(smax[:, 2:3], smax[:, 0:1], smax[:, 1:2])
                self.dfence(E)
            d_norm = sDVE.inc(2)
            @self.on("scalar")
            def _(E, d_norm=d_norm):
                E.wait_ge(sDVE.h, d_norm)
                E.activation(smax[:, 3:4], smax[:, 2:3], AF.Sqrt).then_inc(sACT.h, 1)
                E.wait_ge(sACT.h, sACT.n + 1)
            a_sq1 = 0
            q = self.q["scalar"].pop()
            a_sq1 = sACT.inc(1)
            @self.on("scalar")
            def _(E, d_norm=d_norm, a_sq1=a_sq1):
                E.wait_ge(sDVE.h, d_norm)
                E.activation(smax[:, 3:4], smax[:, 2:3], AF.Sqrt).then_inc(sACT.h, 1)
                E.wait_ge(sACT.h, a_sq1)
                E.activation(smaxb[:, 0:1], smax[:, 3:4], AF.Copy, scale=-0.125).then_inc(sACT.h, 1)
            a_negc = sACT.inc(1)
            @self.on("tensor")
            def _(E, a_negc=a_negc, d_norm=d_norm):
                E.wait_ge(sACT.h, a_negc)
                E.matmul(psA[:, 0:1], ind2t[:, 0:128], smaxb[0:2, 0:1], start=True, stop=True)
                mi = E.matmul(psA[:, 512:513], ind2t[:, 128:256], smaxb[0:2, 0:1], start=True, stop=True)
                mi.then_inc(sPE.h, 1)
            p_negc = sPE.inc(1)
            @self.on("vector")
            def _(E, p_negc=p_negc):
                E.wait_ge(sPE.h, p_negc)
                E.tensor_copy(negCb[:, 0:1], psA[:, 0:1])
                E.tensor_copy(negCb[:, 1:2], psA[:, 512:513]).then_inc(sDVE.h, 1)
            d_negc = sDVE.inc(1)
            @self.on("vector")
            def _(E):
                E.memset(vbufs[0][:, :], 1.0)
                E.memset(vbufs[1][:, :], 1.0).then_inc(sDVE.h, 1)
            d_vmem = sDVE.inc(1)
            @self.on("tensor")
            def _(E, p=p, wv_t=wv_t, d_norm=d_norm):
                self.wpl(E, wv_t)
                E.wait_ge(sDVE.h, d_norm)
                sl = wslot[(p % 2) + 4]
                for c in range(DC):
                    for nt in range(NT):
                        mi = E.matmul(psB[:, nt * 512:(nt + 1) * 512], sl[:, c * 128:(c + 1) * 128],
                                      xn[:, c * T + nt * 512:c * T + (nt + 1) * 512],
                                      start=(c == 0), stop=(c == DC - 1))
                mi.then_inc(sPE.h, 1)
            p_vfm = sPE.inc(1)
            @self.on("scalar")
            def _(E, p=p, t=p_vfm):
                E.wait_ge(sPE.h, t)
                E.activation(fs2[:, 0:2048], psB[:, 0:2048], AF.Identity).then_inc(sACT.h, 1)
            a_vfm = sACT.inc(1)
            vtr = []
            @self.on("tensor")
            def _(E, a_vfm=a_vfm, base=sPE.n, dve_base=sDVE.n, d_vmem=d_vmem, d_negc=d_negc):
                E.wait_ge(sACT.h, a_vfm)
                E.wait_ge(sDVE.h, d_negc)
                for tc in range(16):
                    if tc >= 2:
                        E.wait_ge(sDVE.h, dve_base + (tc - 1) - 1 + 1)
                    mi = E.transpose(psA[:, (tc % 2) * 512:(tc % 2) * 512 + 128],
                                     fs2[:, tc * 128:(tc + 1) * 128], self.eye[:, :])
                    mi.then_inc(sPE.h, 1)
            for tc in range(16):
                vtr.append(sPE.inc(1))
            pair_vend[p] = vtr[-1]
            @self.on("vector")
            def _(E, vtr=tuple(vtr), d_vmem=d_vmem):
                E.wait_ge(sDVE.h, d_vmem)
                for tc in range(16):
                    E.wait_ge(sPE.h, vtr[tc])
                    pv = psA[:, (tc % 2) * 512:(tc % 2) * 512 + 128]
                    E.tensor_copy(vbufs[0][:, tc * 65:tc * 65 + 64], pv[:, 0:64])
                    E.tensor_copy(vbufs[1][:, tc * 65:tc * 65 + 64],
                                  pv[:, 64:128]).then_inc(sDVE.h, 1)
            vev = []
            for tc in range(16):
                vev.append(sDVE.inc(1))
            a_vdone = 0
            d_vdone = vev[-1]

            for h01 in range(2):
                r0 = 64 * h01
                vX = vbufs[h01]
                pieces = []
                for kc in range(16):
                    qs = 128 * kc
                    first = True
                    while qs < T:
                        qe = min(T, (qs // 1024 + 1) * 1024)
                        pieces.append((kc, qs, qe, first))
                        qs = qe
                        first = False
                npc = len(pieces)
                s_cnt = [0] * npc
                av_cnt = [0] * npc
                e_cnt = [0] * npc
                m_cnt = {}
                order = []
                for i in range(npc):
                    order.append(("s", i))
                    if i >= 1:
                        order.append(("av", i - 1))
                order.append(("av", npc - 1))
                cnt = sPE.n
                for kind, i in order:
                    cnt += 1
                    if kind == "s":
                        s_cnt[i] = cnt
                    else:
                        av_cnt[i] = cnt
                sPE.n = cnt
                a_exp0 = sACT.n
                for i in range(npc):
                    e_cnt[i] = a_exp0 + 2 * (i + 1)
                sACT.n = a_exp0 + 2 * npc
                d_m0 = sDVE.n
                nd = 0
                for i in range(npc):
                    if pieces[i][3]:
                        nd += 1
                        m_cnt[i] = d_m0 + nd
                sDVE.n = d_m0 + nd
                @self.on("tensor")
                def _(E, pieces=tuple(pieces), order=tuple(order), s_cnt=tuple(s_cnt),
                      e_cnt=tuple(e_cnt), m_cnt=dict(m_cnt), r0=r0, vX=vX, p=p,
                      d_vdone=d_vdone, d_negc=d_negc, hev=prev_head_evac[0]):
                    E.wait_ge(sDVE.h, d_vdone)
                    E.wait_ge(sDVE.h, d_negc)
                    if hev:
                        E.wait_ge(sDVE.h, hev)
                    for kind, i in order:
                        kc, qs, qe, diag = pieces[i]
                        w = qe - qs
                        if kind == "s":
                            if i >= 2:
                                E.wait_ge(sACT.h, e_cnt[i - 2])
                            ring = i % 2
                            for o in range(0, w, 512):
                                ww = min(512, w - o)
                                mi = E.matmul(psA[:, ring * 1024 + o:ring * 1024 + o + ww],
                                              kp[r0:r0 + 64, kc * 128:(kc + 1) * 128],
                                              qp[r0:r0 + 64, qs + o:qs + o + ww],
                                              start=True, stop=True)
                        else:
                            if i in m_cnt:
                                E.wait_ge(sDVE.h, m_cnt[i])
                            else:
                                E.wait_ge(sACT.h, e_cnt[i])
                            for o in range(0, w, 512):
                                ww = min(512, w - o)
                                mi = E.matmul(psB[0:65, qs + o:qs + o + ww],
                                              vX[:, kc * 65:(kc + 1) * 65],
                                              eslot[i % 3][:, o:o + ww], start=(kc == 0),
                                              stop=(kc == 15))
                        mi.then_inc(sPE.h, 1)
                @self.on("scalar")
                def _(E, pieces=tuple(pieces), s_cnt=tuple(s_cnt), av_cnt=tuple(av_cnt),
                      h01=h01):
                    for i in range(len(pieces)):
                        kc, qs, qe, diag = pieces[i]
                        w = qe - qs
                        ring = i % 2
                        E.wait_ge(sPE.h, s_cnt[i])
                        if i >= 3:
                            E.wait_ge(sPE.h, av_cnt[i - 3])
                        E.activation(eslot[i % 3][:, 0:w], psA[:, ring * 1024:ring * 1024 + w],
                                     AF.Exp, bias=negCb[:, h01:h01 + 1],
                                     scale=0.125).then_inc(sACT.h, 1)
                        E.activation(self._af1, self._af1, AF.Identity).then_inc(sACT.h, 1)
                @self.on("vector")
                def _(E, pieces=tuple(pieces), e_cnt=tuple(e_cnt)):
                    for i in range(len(pieces)):
                        if pieces[i][3]:
                            E.wait_ge(sACT.h, e_cnt[i])
                            E.tensor_mul(eslot[i % 3][:, 0:128], eslot[i % 3][:, 0:128],
                                         tri[:, :]).then_inc(sDVE.h, 1)
                av_last = av_cnt[npc - 1]
                @self.on("vector")
                def _(E, av_last=av_last):
                    E.wait_ge(sPE.h, av_last)
                    E.reciprocal(fs1[0:1, 0:2048], psB[64:65, 0:2048])
                    self.dfence(E)
                d_linv = sDVE.inc(1)
                @self.on("tensor")
                def _(E, d_linv=d_linv, e_last=e_cnt[npc - 1]):
                    E.wait_ge(sDVE.h, d_linv)
                    E.wait_ge(sACT.h, e_last)
                    for nt in range(NT):
                        mi = E.matmul(psA[0:64, nt * 512:(nt + 1) * 512], onesf[0:1, 0:64],
                                      fs1[0:1, nt * 512:(nt + 1) * 512], start=True, stop=True)
                    mi.then_inc(sPE.h, 1)
                p_lb = sPE.inc(1)
                @self.on("vector")
                def _(E, p_lb=p_lb, r0=r0, p=p):
                    E.wait_ge(sPE.h, p_lb)
                    E.tensor_copy(fs2[0:64, 0:2048], psA[0:64, 0:2048])
                    self.dfence(E)
                    E.tensor_mul(big1[r0:r0 + 64, p * 2048:(p + 1) * 2048],
                                 psB[0:64, 0:2048], fs2[0:64, 0:2048]).then_inc(sDVE.h, 1)
                sDVE.inc(1)
                prev_head_evac[0] = sDVE.inc(1)
            pair_all_done[p] = prev_head_evac[0]

        xadd_done = [0] * 8
        wo_pe = [0] * 8
        for j in range(8):
            wo_t = self.cload(d["wo_h"][j], wslot[4 + (j % 2)][:, :],
                              pe_war=wo_pe[j - 2] if j >= 2 else pair_vend[6 + (j % 2)])
            ps = psA if (j % 2 == 0) else psB
            war2 = xadd_done[j - 2] if j >= 2 else pair_all_done[7]
            @self.on("tensor")
            def _(E, j=j, wo_t=wo_t, ps=ps, war2=war2):
                self.wpl(E, wo_t)
                if war2:
                    E.wait_ge(sDVE.h, war2)
                sl = wslot[4 + (j % 2)]
                for c in range(DC):
                    for nt in range(NT):
                        mi = E.matmul(ps[:, nt * 512:(nt + 1) * 512], sl[:, c * 128:(c + 1) * 128],
                                      big1[:, c * 2048 + nt * 512:c * 2048 + (nt + 1) * 512],
                                      start=(c == 0), stop=(c == DC - 1))
                mi.then_inc(sPE.h, 1)
            p_j = sPE.inc(1)
            wo_pe[j] = p_j
            @self.on("vector")
            def _(E, j=j, p_j=p_j, ps=ps):
                E.wait_ge(sPE.h, p_j)
                for nt in range(NT):
                    ti = E.tensor_add(x[:, j * 2048 + nt * 512:j * 2048 + (nt + 1) * 512],
                                      x[:, j * 2048 + nt * 512:j * 2048 + (nt + 1) * 512],
                                      ps[:, nt * 512:(nt + 1) * 512])
                ti.then_inc(sDVE.h, 1)
            xadd_done[j] = sDVE.inc(1)
        self.maybe_flush("attn" + tag)
        return wo_pe[7], xadd_done[7]

    def _shift_mix(self, mu, d_after_pe=0):
        xn, big1 = self.xn, self.big1
        sDVE, sPE = self.sDVE, self.sPE
        @self.on("vector")
        def _(E, mu=mu, d_after_pe=d_after_pe):
            if d_after_pe:
                E.wait_ge(sPE.h, d_after_pe)
            for c in range(DC):
                E.tensor_sub(big1[:, c * T + 1:c * T + T], xn[:, c * T + 1:c * T + T],
                             xn[:, c * T:c * T + T - 1])
                self.dfence(E)
                E.scalar_tensor_tensor(big1[:, c * T + 1:c * T + T], big1[:, c * T + 1:c * T + T],
                                       mu[:, c:c + 1], xn[:, c * T:c * T + T - 1],
                                       ALU.mult, ALU.add)
                E.tensor_scalar(big1[:, c * T:c * T + 1], xn[:, c * T:c * T + 1],
                                mu[:, c:c + 1], None, ALU.mult)
            self.dfence(E)
        return sDVE.inc(DC + 1)

    def ffn0(self, tag="", entry_pe=0):
        x, xn, big1, big2, wring = self.x_sb, self.xn, self.big1, self.big2, self.wring
        fs1 = self.fs1
        psA, psB = self.psA, self.psB
        sPE, sACT, sDVE, sW = self.sPE, self.sACT, self.sDVE, self.sW
        sPL = self.sPL
        d = self.d
        wslot = [wring[:, i * 1024:(i + 1) * 1024] for i in range(7)]
        st_buf = wring[:, 6 * 1024:7 * 1024]

        d_xk = self._shift_mix(self.mu_k, d_after_pe=entry_pe)
        evac_dve = [0] * 8
        evac_act = [0] * 8
        kk_pe = [0] * 8
        for j in range(8):
            wk_t = self.cload(d["wkk_h"][j], wslot[j % 2][:, :],
                              pe_war=kk_pe[j - 2] if j >= 2 else entry_pe)
            ps = psA if j % 2 == 0 else psB
            @self.on("tensor")
            def _(E, j=j, wk_t=wk_t, ps=ps, d_xk=d_xk, war=evac_act[j - 2] if j >= 2 else 0):
                self.wpl(E, wk_t)
                if j == 0:
                    E.wait_ge(sDVE.h, d_xk)
                if war:
                    E.wait_ge(sACT.h, war)
                sl = wslot[j % 2]
                for c in range(DC):
                    for nt in range(NT):
                        mi = E.matmul(ps[:, nt * 512:(nt + 1) * 512], sl[:, c * 128:(c + 1) * 128],
                                      big1[:, c * T + nt * 512:c * T + (nt + 1) * 512],
                                      start=(c == 0), stop=(c == DC - 1))
                mi.then_inc(sPE.h, 1)
            p_j = sPE.inc(1)
            kk_pe[j] = p_j
            a_half = [sACT.inc(1), sACT.inc(1)]
            d_half = [sDVE.inc(1), sDVE.inc(1)]
            @self.on("scalar")
            def _(E, j=j, p_j=p_j, ps=ps, war=evac_dve[j - 1] if j >= 1 else 0,
                  d_half=tuple(d_half)):
                E.wait_ge(sPE.h, p_j)
                if war:
                    E.wait_ge(sDVE.h, war)
                E.activation(st_buf, ps[:, 0:1024], AF.Relu).then_inc(sACT.h, 1)
                E.wait_ge(sDVE.h, d_half[0])
                E.activation(st_buf, ps[:, 1024:2048], AF.Relu).then_inc(sACT.h, 1)
            @self.on("vector")
            def _(E, j=j, a_half=tuple(a_half)):
                E.wait_ge(sACT.h, a_half[0])
                E.tensor_mul(big2[:, j * 2048:j * 2048 + 1024], st_buf, st_buf).then_inc(sDVE.h, 1)
                E.wait_ge(sACT.h, a_half[1])
                E.tensor_mul(big2[:, j * 2048 + 1024:(j + 1) * 2048], st_buf, st_buf).then_inc(sDVE.h, 1)
            evac_act[j] = a_half[1]
            evac_dve[j] = d_half[1]
        d_xr = self._shift_mix(self.mu_r, d_after_pe=sPE.n)
        r_act = [0] * 8
        x_dve = [0] * 8
        rv_pe = [0] * 8
        for j in range(8):
            wr_t = self.cload(d["wr_h"][j], wslot[2 + (j % 2)][:, :],
                              pe_war=rv_pe[j - 2] if j >= 2 else entry_pe)
            wv_t = self.cload(d["wvv_h"][j], wslot[4 + (j % 2)][:, :],
                              pe_war=rv_pe[j - 2] if j >= 2 else entry_pe)
            @self.on("tensor")
            def _(E, j=j, wr_t=wr_t, wv_t=wv_t, d_xr=d_xr,
                  war_a=r_act[j - 1] if j >= 1 else 0,
                  war_d=x_dve[j - 1] if j >= 1 else 0):
                self.wpl(E, wr_t)
                if j == 0:
                    E.wait_ge(sDVE.h, d_xr)
                if war_a:
                    E.wait_ge(sACT.h, war_a)
                sl = wslot[2 + (j % 2)]
                for c in range(DC):
                    for nt in range(NT):
                        mi = E.matmul(psA[:, nt * 512:(nt + 1) * 512], sl[:, c * 128:(c + 1) * 128],
                                      big1[:, c * T + nt * 512:c * T + (nt + 1) * 512],
                                      start=(c == 0), stop=(c == DC - 1))
                mi.then_inc(sPE.h, 1)
                self.wpl(E, wv_t)
                if war_d:
                    E.wait_ge(sDVE.h, war_d)
                sl = wslot[4 + (j % 2)]
                for c in range(DC):
                    for nt in range(NT):
                        mi = E.matmul(psB[:, nt * 512:(nt + 1) * 512], sl[:, c * 128:(c + 1) * 128],
                                      big2[:, c * T + nt * 512:c * T + (nt + 1) * 512],
                                      start=(c == 0), stop=(c == DC - 1))
                mi.then_inc(sPE.h, 1)
            p_r = sPE.inc(1)
            p_v = sPE.inc(1)
            rv_pe[j] = p_v
            ra_half = [sACT.inc(1), sACT.inc(1)]
            xd_half = [0, 0]
            sDVE.inc(2)
            xd_half[0] = sDVE.inc(1)
            sDVE.inc(2)
            xd_half[1] = sDVE.inc(1)
            @self.on("scalar")
            def _(E, j=j, p_r=p_r, war=x_dve[j - 1] if j >= 1 else 0,
                  xd_half=tuple(xd_half)):
                E.wait_ge(sPE.h, p_r)
                if war:
                    E.wait_ge(sDVE.h, war)
                E.activation(st_buf, psA[:, 0:1024], AF.Sigmoid).then_inc(sACT.h, 1)
                E.wait_ge(sDVE.h, xd_half[0])
                E.activation(st_buf, psA[:, 1024:2048], AF.Sigmoid).then_inc(sACT.h, 1)
            @self.on("vector")
            def _(E, j=j, p_v=p_v, ra_half=tuple(ra_half)):
                E.wait_ge(sPE.h, p_v)
                for h2 in range(2):
                    E.wait_ge(sACT.h, ra_half[h2])
                    for nt in (2 * h2, 2 * h2 + 1):
                        E.tensor_mul(fs1[:, 0:512], psB[:, nt * 512:(nt + 1) * 512],
                                     st_buf[:, (nt % 2) * 512:(nt % 2) * 512 + 512])
                        self.dfence(E)
                        ti = E.tensor_add(x[:, j * 2048 + nt * 512:j * 2048 + (nt + 1) * 512],
                                          x[:, j * 2048 + nt * 512:j * 2048 + (nt + 1) * 512],
                                          fs1[:, 0:512])
                    ti.then_inc(sDVE.h, 1)
            r_act[j] = ra_half[1]
            x_dve[j] = xd_half[1]
        self.maybe_flush("ffn0" + tag)
        return rv_pe[7], x_dve[7]

    def fold(self):
        big1, big2, wring, fs1 = self.big1, self.big2, self.wring, self.fs1
        psA, psB = self.psA, self.psB
        sPE, sACT, sDVE, sW, sX = self.sPE, self.sACT, self.sDVE, self.sW, self.sX
        sPL = self.sPL
        d, l1w, l1wb, cvec = self.d, self.l1w, self.l1wb, self.cvec
        wslot = [wring[:, i * 1024:(i + 1) * 1024] for i in range(7)]
        sl_ = lambda i: l1w[:, i * 16:(i + 1) * 16]

        aN = [sACT.inc(1) for _ in range(4)]
        a2 = aN[3]
        @self.on("scalar")
        def _(E, aN=tuple(aN)):
            E.activation(sl_(5), sl_(2), AF.Exp).then_inc(sACT.h, 1)
            E.activation(sl_(6), sl_(3), AF.Exp).then_inc(sACT.h, 1)
            E.wait_ge(sACT.h, aN[0])
            E.activation(sl_(2), sl_(4), AF.Exp).then_inc(sACT.h, 1)
            E.wait_ge(sACT.h, aN[2])
            E.activation(sl_(5), sl_(5), AF.Exp, scale=-1.0).then_inc(sACT.h, 1)
        d1 = sDVE.inc(1)
        @self.on("vector")
        def _(E, a2=a2, d1=d1):
            E.wait_ge(sACT.h, a2)
            E.tensor_scalar(sl_(7), sl_(6), -1.0, 1.5707963267948966, ALU.mult, ALU.add)
            E.tensor_scalar(sl_(9), sl_(2), 2.0, None, ALU.mult)
            E.tensor_scalar(sl_(10), sl_(2), -2.0, None, ALU.mult)
            self.dfence(E)
        a3 = sACT.inc(1)
        @self.on("scalar")
        def _(E, d1=d1, a3=a3):
            E.wait_ge(sDVE.h, d1)
            E.activation(sl_(7), sl_(7), AF.Sin)
            E.activation(sl_(8), sl_(6), AF.Sin).then_inc(sACT.h, 1)
        d2 = sDVE.inc(1); d3 = sDVE.inc(1); d4 = sDVE.inc(1); d_vecs = sDVE.inc(1)
        @self.on("vector")
        def _(E, a3=a3, d2=d2, d3=d3, d4=d4, d_vecs=d_vecs):
            E.wait_ge(sACT.h, a3)
            E.tensor_mul(sl_(3), sl_(5), sl_(7))
            E.tensor_mul(sl_(4), sl_(5), sl_(8)).then_inc(sDVE.h, 1)
            E.wait_ge(sDVE.h, d2)
            E.tensor_mul(sl_(6), sl_(3), sl_(0))
            E.tensor_mul(sl_(7), sl_(4), sl_(1))
            E.tensor_mul(sl_(8), sl_(3), sl_(1))
            E.tensor_mul(sl_(11), sl_(4), sl_(0)).then_inc(sDVE.h, 1)
            E.wait_ge(sDVE.h, d3)
            E.tensor_sub(sl_(6), sl_(6), sl_(7))
            E.tensor_add(sl_(8), sl_(8), sl_(11)).then_inc(sDVE.h, 1)
            E.wait_ge(sDVE.h, d4)
            E.tensor_scalar(l1wb[:, 0:16], sl_(6), 2.0, None, ALU.mult)
            E.tensor_scalar(l1wb[:, 16:32], sl_(8), -2.0, None, ALU.mult)
            self.dfence(E)
        self.flush("foldv")

        pe0 = sPE.n
        ctg = []
        for sc in range(SC):
            war = pe0 + sc - 1 if pe0 + sc - 1 > 0 else 0
            t1 = self.cload(d["cre_h"][:, sc * 1024:(sc + 1) * 1024], wslot[sc % 2][:, :], pe_war=war)
            t2 = self.cload(d["cim_h"][:, sc * 1024:(sc + 1) * 1024], wslot[2 + sc % 2][:, :], pe_war=war)
            ctg.append(t2)
        @self.on("tensor")
        def _(E, ctg=tuple(ctg), d_vecs=d_vecs, pe0=pe0):
            E.wait_ge(sDVE.h, d_vecs)
            for sc in range(SC):
                self.wpl(E, ctg[sc])
                for jc in range(8):
                    ps = psA if jc < 4 else psB
                    col = (jc % 4) * 512
                    E.matmul(ps[:, col:col + 1], wslot[sc % 2][:, jc * 128:(jc + 1) * 128],
                             l1wb[:, sc:sc + 1], start=(sc == 0), stop=False)
                    mi = E.matmul(ps[:, col:col + 1], wslot[2 + sc % 2][:, jc * 128:(jc + 1) * 128],
                                  l1wb[:, 16 + sc:16 + sc + 1], start=False, stop=(sc == SC - 1))
                mi.then_inc(sPE.h, 1)
        for sc in range(SC):
            sPE.inc(1)
        p_cvec = sPE.n
        @self.on("vector")
        def _(E, p_cvec=p_cvec):
            E.wait_ge(sPE.h, p_cvec)
            for jc in range(8):
                ps = psA if jc < 4 else psB
                col = (jc % 4) * 512
                ti = E.tensor_copy(cvec[:, jc:jc + 1], ps[:, col:col + 1])
            ti.then_inc(sDVE.h, 1)
        d_cvec = sDVE.inc(1)
        self.flush("foldc")

        for c8 in range(8):
            dt_t = self.cload(d["dt_h"][:, c8 * 1024:(c8 + 1) * 1024],
                              big2[:, 8192 + c8 * 1024:8192 + (c8 + 1) * 1024])
        self._dt_t = dt_t
        mm_done = {}
        for g in range(2):
            for ri, (bh, gsl) in enumerate((("bre_h", 9), ("bim_h", 10))):
                pidx = g * 2 + ri
                dve0 = sDVE.n
                ch = "cre_h" if ri == 0 else "cim_h"
                pe0 = sPE.n
                btg = []
                ctg = []
                for sc in range(SC):
                    bwar = dve0 + sc - 1 if dve0 + sc - 1 > 0 else 0
                    btg.append(self.cload(d[bh][:, sc * 1024:(sc + 1) * 1024],
                                          wslot[sc % 2][:, :], dve_war=bwar))
                    war = pe0 + sc - 1 if pe0 + sc - 1 > 0 else 0
                    ctg.append(self.cload(d[ch][:, sc * 1024:(sc + 1) * 1024],
                                          wslot[4 + sc % 2][:, :], pe_war=war))
                @self.on("vector")
                def _(E, btg=tuple(btg), gsl=gsl, pidx=pidx, mmp=dict(mm_done)):
                    for sc in range(SC):
                        self.wpl(E, btg[sc])
                        if pidx > 0:
                            E.wait_ge(sPE.h, mmp[(pidx - 1, sc)])
                        E.tensor_scalar(big1[:, sc * 1024:(sc + 1) * 1024],
                                        wslot[sc % 2][:, :],
                                        l1w[:, gsl * 16 + sc:gsl * 16 + sc + 1], None,
                                        ALU.mult)
                        self.dfence(E)
                sDVE.inc(SC)
                @self.on("tensor")
                def _(E, g=g, ctg=tuple(ctg), dve0=dve0, pidx=pidx, pe0=pe0, ri=ri):
                    for sc in range(SC):
                        self.wpl(E, ctg[sc])
                        E.wait_ge(sDVE.h, dve0 + sc + 1)
                        for djl in range(4):
                            dj = g * 4 + djl
                            for jh in range(2):
                                ps = psA if djl < 2 else psB
                                col = ((djl % 2) * 2 + jh) * 512
                                mi = E.matmul(ps[:, col:col + 512],
                                              big1[:, sc * 1024 + dj * 128:sc * 1024 + (dj + 1) * 128],
                                              wslot[4 + sc % 2][:, jh * 512:(jh + 1) * 512],
                                              start=(ri == 0 and sc == 0),
                                              stop=(ri == 1 and sc == SC - 1))
                        mi.then_inc(sPE.h, 1)
                for sc in range(SC):
                    mm_done[(pidx, sc)] = sPE.inc(1)
                if getattr(self, "_fold_debug", False) and pidx == 0:
                    @self.on("vector")
                    def _(E, t=mm_done[(0, SC - 1)]):
                        E.wait_ge(sPE.h, t)
                        E.tensor_copy(fs1[:, 0:512], psA[:, 0:512])
                        E.tensor_copy(fs1[:, 512:1024], psA[:, 512:1024]).then_inc(sDVE.h, 1)
                    sDVE.inc(1)
                    dbg = sDVE.n
                    @self.on("tensor")
                    def _(E, dbg=dbg):
                        E.wait_ge(sDVE.h, dbg)
            @self.on("vector")
            def _(E, g=g, t=mm_done[(g * 2 + 1, SC - 1)], dtt=self._dt_t):
                E.wait_ge(sPE.h, t)
                self.wpl(E, dtt)
                for djl in range(4):
                    dj = g * 4 + djl
                    for jh in range(2):
                        ps = psA if djl < 2 else psB
                        col = ((djl % 2) * 2 + jh) * 512
                        E.scalar_tensor_tensor(
                            big2[:, dj * 1024 + jh * 512:dj * 1024 + jh * 512 + 512],
                            ps[:, col:col + 512], 1.0,
                            big2[:, 8192 + dj * 1024 + jh * 512:8192 + dj * 1024 + jh * 512 + 512],
                            ALU.mult, ALU.add)
                self.dfence(E)
            sDVE.inc(1)
            if g == 0:
                mm_done[(1, -1)] = None
                ev = sDVE.n
                @self.on("tensor")
                def _(E, ev=ev):
                    E.wait_ge(sDVE.h, ev)
        ev_all = sDVE.n
        prev = sX.n
        sX.inc(16)
        @self.on("sync")
        def _(E, ev_all=ev_all, prev=prev):
            E.wait_ge(sDVE.h, ev_all)
            if prev:
                E.wait_ge(sX.h, prev)
            E.dma_start(out=d["mtot_scr"], in_=big2[:, 0:8192]).then_inc(sX.h, 16)
        self.flush("foldm")

    def l1apply(self, tag="", entry_dve=0, entry_pe=0):
        x, xn, big2 = self.x_sb, self.xn, self.big2
        psA, psB = self.psA, self.psB
        sPE, sDVE, sX = self.sPE, self.sDVE, self.sX
        d, cvec = self.d, self.cvec
        mt = sX.inc(16)
        @self.on("sync")
        def _(E, mt=mt, ep=entry_pe):
            if ep:
                E.wait_ge(sPE.h, ep)
            E.wait_ge(sX.h, mt - 16)
            E.dma_start(out=big2[:, 0:8192], in_=d["mtot_scr"]).then_inc(sX.h, 16)
        x_dve = [0] * 8
        for jc in range(8):
            ps = psA if jc % 2 == 0 else psB
            @self.on("tensor")
            def _(E, jc=jc, ps=ps, mt=mt, war=x_dve[jc - 2] if jc >= 2 else 0,
                  ed=entry_dve if jc == 0 else 0):
                if jc == 0:
                    E.wait_ge(sX.h, mt)
                if ed:
                    E.wait_ge(sDVE.h, ed)
                if war:
                    E.wait_ge(sDVE.h, war)
                for c in range(DC):
                    for nt in range(NT):
                        mi = E.matmul(ps[:, nt * 512:(nt + 1) * 512],
                                      big2[:, c * 1024 + jc * 128:c * 1024 + jc * 128 + 128],
                                      xn[:, c * T + nt * 512:c * T + (nt + 1) * 512],
                                      start=(c == 0), stop=(c == DC - 1))
                mi.then_inc(sPE.h, 1)
            p_j = sPE.inc(1)
            @self.on("vector")
            def _(E, jc=jc, p_j=p_j, ps=ps):
                E.wait_ge(sPE.h, p_j)
                for nt in range(NT):
                    E.scalar_tensor_tensor(
                        x[:, jc * 2048 + nt * 512:jc * 2048 + (nt + 1) * 512],
                        ps[:, nt * 512:(nt + 1) * 512], cvec[:, jc:jc + 1],
                        x[:, jc * 2048 + nt * 512:jc * 2048 + (nt + 1) * 512],
                        ALU.add, ALU.add)
                self.dfence(E)
            x_dve[jc] = sDVE.inc(1)
        self.maybe_flush("l1a" + tag)
        return p_j, x_dve[7]

    def gating(self, tag="", entry_dve=0):
        xn, gtmp, scT = self.xn, self.gtmp, self.scT
        psA, psB = self.psA, self.psB
        sPE, sACT, sDVE = self.sPE, self.sACT, self.sDVE
        gw, ebias, eye, w2rows = self.gwsb, self.ebias, self.eye, self.w2rows
        @self.on("tensor")
        def _(E, ed=entry_dve):
            if ed:
                E.wait_ge(sDVE.h, ed)
            for tt in range(16):
                for c in range(DC):
                    mi = E.matmul(psA[:, tt * 8:(tt + 1) * 8],
                                  xn[:, c * T + tt * 128:c * T + (tt + 1) * 128],
                                  gw[:, c * 8:(c + 1) * 8],
                                  start=(c == 0), stop=(c == DC - 1))
            mi.then_inc(sPE.h, 1)
        p_sc = sPE.inc(1)
        @self.on("scalar")
        def _(E, p_sc=p_sc):
            E.wait_ge(sPE.h, p_sc)
            E.activation(scT[:, 0:128], psA[:, 0:128], AF.Sigmoid).then_inc(sACT.h, 1)
        a_sc = sACT.inc(1)
        biased = gtmp[:, 0:128]
        m1 = gtmp[:, 128:144]
        mask1 = gtmp[:, 144:272]
        cnt = gtmp[:, 272:288]
        b2 = gtmp[:, 288:416]
        m2 = gtmp[:, 416:432]
        ge2 = gtmp[:, 432:448]
        dm = gtmp[:, 448:464]
        t8 = gtmp[:, 464:480]
        thr = gtmp[:, 480:496]
        w2t = b2
        d_base = sDVE.n
        d_w2 = sDVE.inc(11)
        @self.on("vector")
        def _(E, a_sc=a_sc, d_base=d_base):
            _b = bass
            def r3(ap):
                return ap.rearrange("p (t e) -> p t e", e=8)
            def bcast(big3, small):
                a, b2_ = _b.broadcast_tensor_aps(big3, small.rearrange("p (t e) -> p t e", e=1))
                return a, b2_
            E.wait_ge(sACT.h, a_sc)
            E.tensor_add(biased, scT[:, 0:128], ebias[:, 0:128])
            self.dfence(E)
            E.tensor_reduce(m1, r3(biased), AX.X, ALU.max)
            self.dfence(E)
            a, b = bcast(r3(biased), m1)
            E.tensor_tensor(r3(mask1), a, b, ALU.is_ge)
            self.dfence(E)
            E.tensor_reduce(cnt, r3(mask1), AX.X, ALU.add)
            E.scalar_tensor_tensor(b2, mask1, -1e30, biased, ALU.mult, ALU.add)
            self.dfence(E)
            E.tensor_reduce(m2, r3(b2), AX.X, ALU.max)
            E.tensor_scalar(ge2, cnt, 1.5, None, ALU.is_ge)
            self.dfence(E)
            E.tensor_sub(dm, m1, m2)
            self.dfence(E)
            E.tensor_mul(t8, dm, ge2)
            self.dfence(E)
            E.tensor_add(thr, t8, m2)
            self.dfence(E)
            a, b = bcast(r3(biased), thr)
            E.tensor_tensor(r3(mask1), a, b, ALU.is_ge)
            self.dfence(E)
            E.tensor_mul(mask1, scT[:, 0:128], mask1)
            self.dfence(E)
            E.tensor_mul(w2t, mask1, mask1).then_inc(sDVE.h, 1)
        if getattr(self, "_gate_half", False):
            self.flush("gate" + tag)
            return
        tr_done = []
        @self.on("tensor")
        def _(E, d_w2=d_w2):
            E.wait_ge(sDVE.h, d_w2)
            for tt in range(16):
                if tt >= 4:
                    E.wait_ge(sDVE.h, d_w2 + tt - 3)
                mi = E.transpose(psB[0:8, (tt % 4) * 512:(tt % 4) * 512 + 128],
                                 w2t[:, tt * 8:(tt + 1) * 8], eye[:, :])
                mi.then_inc(sPE.h, 1)
        tr0 = sPE.n
        for tt in range(16):
            tr_done.append(sPE.inc(1))
        @self.on("vector")
        def _(E, tr=tuple(tr_done)):
            for tt in range(16):
                E.wait_ge(sPE.h, tr[tt])
                E.tensor_copy(w2rows[0:8, tt * 128:(tt + 1) * 128],
                              psB[0:8, (tt % 4) * 512:(tt % 4) * 512 + 128]).then_inc(sDVE.h, 1)
        gate_dve = 0
        for tt in range(16):
            gate_dve = sDVE.inc(1)
        self.maybe_flush("gate" + tag)
        return gate_dve

    def moe(self, tag="", entry_pe=0, entry_dve=0):
        x, xn, big1, big2, wring = self.x_sb, self.xn, self.big1, self.big2, self.wring
        psA, psB = self.psA, self.psB
        sPE, sACT, sDVE = self.sPE, self.sACT, self.sDVE
        sPL = self.sPL
        d = self.d
        wslot = [wring[:, i * 1024:(i + 1) * 1024] for i in range(7)]
        w2b = wslot[4]
        s_sl = wslot[5]
        t_sl = wslot[6]
        wu_k = [0]
        wu_pe = {}
        prev_down_pe = [entry_pe]
        prev_xadd = [entry_dve]
        prev_upevac = [0]
        first_pe = entry_pe
        wu_tgts = {}

        def emit_wu(e, h, f):
            k = wu_k[0]; wu_k[0] += 1
            wu_tgts[(e, h, f)] = (self.cload(d["wu_h"][e, f], wslot[k % 4][:, :],
                                             pe_war=wu_pe.get(k - 4, first_pe)), k)
        for e in range(NE):
            wd_t = 0
            for fc in range(FC):
                wd_t = self.cload(d["wd_h"][e][:, fc * 1024:(fc + 1) * 1024],
                                  big2[:, fc * 1024:(fc + 1) * 1024],
                                  pe_war=prev_down_pe[0])
            for h in range(2):
                for f in range(4):
                    if (e, h, f) not in wu_tgts:
                        emit_wu(e, h, f)
                @self.on("tensor")
                def _(E, e=e, h=h, dwar=prev_xadd[0]):
                    if dwar:
                        E.wait_ge(sDVE.h, dwar)
                    for hnt in range(2):
                        mi = E.matmul(psB[:, hnt * 512:(hnt + 1) * 512],
                                      self.ind8[0:8, e * 128:(e + 1) * 128],
                                      self.w2rows[0:8, h * 1024 + hnt * 512:h * 1024 + (hnt + 1) * 512],
                                      start=True, stop=True)
                    mi.then_inc(sPE.h, 1)
                p_bc = sPE.inc(1)
                @self.on("scalar")
                def _(E, p_bc=p_bc, uwar=prev_upevac[0], fp=first_pe):
                    E.wait_ge(sPE.h, p_bc)
                    if uwar:
                        E.wait_ge(sDVE.h, uwar)
                    elif fp:
                        E.wait_ge(sPE.h, fp)
                    E.activation(w2b, psB[:, 0:1024], AF.Identity).then_inc(sACT.h, 1)
                a_w2b = sACT.inc(1)
                up_act = [0] * FC
                up_dve = [0] * FC
                for f in range(FC):
                    if (e, h, f) not in wu_tgts:
                        emit_wu(e, h, f)
                    wu_t, k = wu_tgts[(e, h, f)]
                    @self.on("tensor")
                    def _(E, e=e, h=h, f=f, k=k, wu_t=wu_t,
                          awar=up_act[f - 2] if f >= 2 else 0):
                        self.wpl(E, wu_t)
                        if awar:
                            E.wait_ge(sACT.h, awar)
                        sl = wslot[k % 4]
                        for c in range(DC):
                            for ntu in range(2):
                                mi = E.matmul(
                                    psA[:, (f % 2) * 1024 + ntu * 512:(f % 2) * 1024 + (ntu + 1) * 512],
                                    sl[:, c * 128:(c + 1) * 128],
                                    xn[:, c * T + h * TH + ntu * 512:c * T + h * TH + (ntu + 1) * 512],
                                    start=(c == 0), stop=(c == DC - 1))
                        mi.then_inc(sPE.h, 1)
                    p_f = sPE.inc(1)
                    wu_pe[k] = p_f
                    @self.on("scalar")
                    def _(E, f=f, p_f=p_f, a_w2b=a_w2b,
                          dwar=up_dve[f - 1] if f >= 1 else 0,
                          fp=first_pe if (e == 0 and h == 0 and f == 0) else 0):
                        E.wait_ge(sPE.h, p_f)
                        E.wait_ge(sACT.h, a_w2b)
                        if dwar:
                            E.wait_ge(sDVE.h, dwar)
                        elif fp:
                            E.wait_ge(sPE.h, fp)
                        E.activation(s_sl, psA[:, (f % 2) * 1024:(f % 2) * 1024 + 1024],
                                     AF.Relu).then_inc(sACT.h, 1)
                    up_act[f] = sACT.inc(1)
                    @self.on("vector")
                    def _(E, f=f, a_f=up_act[f], pwar=prev_down_pe[0]):
                        E.wait_ge(sACT.h, a_f)
                        if pwar and f < 2:
                            E.wait_ge(sPE.h, pwar)
                        E.tensor_mul(t_sl, s_sl, w2b)
                        self.dfence(E)
                        E.tensor_mul(big1[:, f * 1024:(f + 1) * 1024], t_sl,
                                     s_sl).then_inc(sDVE.h, 1)
                    sDVE.inc(1)
                    up_dve[f] = sDVE.inc(1)
                prev_upevac[0] = up_dve[FC - 1]
                if h == 0:
                    for f in range(4):
                        emit_wu(e, 1, f)
                elif e + 1 < NE:
                    for f in range(4):
                        emit_wu(e + 1, 0, f)
                for j in range(8):
                    @self.on("tensor")
                    def _(E, e=e, h=h, j=j, hdone=up_dve[FC - 1], wd_t=wd_t,
                          dwar=prev_xadd[0] if j >= 2 else 0,
                          jwar=0):
                        E.wait_ge(sDVE.h, hdone)
                        self.wpl(E, wd_t)
                        if dwar:
                            E.wait_ge(sDVE.h, dwar)
                        for f in range(FC):
                            for ntd in range(2):
                                mi = E.matmul(
                                    psB[:, (j % 2) * 1024 + ntd * 512:(j % 2) * 1024 + (ntd + 1) * 512],
                                    big2[:, f * 1024 + j * 128:f * 1024 + j * 128 + 128],
                                    big1[:, f * 1024 + ntd * 512:f * 1024 + (ntd + 1) * 512],
                                    start=(f == 0), stop=(f == FC - 1))
                        mi.then_inc(sPE.h, 1)
                    p_j = sPE.inc(1)
                    prev_down_pe[0] = p_j
                    @self.on("vector")
                    def _(E, h=h, j=j, p_j=p_j):
                        E.wait_ge(sPE.h, p_j)
                        for ntd in range(2):
                            ti = E.tensor_add(
                                x[:, j * 2048 + h * TH + ntd * 512:j * 2048 + h * TH + (ntd + 1) * 512],
                                x[:, j * 2048 + h * TH + ntd * 512:j * 2048 + h * TH + (ntd + 1) * 512],
                                psB[:, (j % 2) * 1024 + ntd * 512:(j % 2) * 1024 + (ntd + 1) * 512])
                        ti.then_inc(sDVE.h, 1)
                    prev_xadd[0] = sDVE.inc(1)
        self.maybe_flush("moe" + tag)
        return prev_xadd[0]



def build_full(nc, stack, taps=()):
    b = Builder(nc, stack)
    b.load_consts()
    b.fold()
    b.seq_merge = MERGE_SEQ
    prev_dve = b.sDVE.n
    prev_pe = b.sPE.n
    for s in range(S):
        b.load_x(s)
        ln1_dve = b.ln(f"1s{s}", entry_dve=prev_dve)
        wo_pe, attn_dve = b.attn(f"s{s}", entry_dve=ln1_dve, entry_pe=prev_pe)
        rv_pe, ffn_dve = b.ffn0(f"s{s}", entry_pe=wo_pe)
        ln2_dve = b.ln(f"2s{s}", entry_dve=ffn_dve)
        l1_pe, l1_dve = b.l1apply(f"s{s}", entry_dve=ln2_dve, entry_pe=rv_pe)
        gate_dve = b.gating(f"s{s}", entry_dve=l1_dve)
        moe_dve = b.moe(f"s{s}", entry_pe=l1_pe, entry_dve=gate_dve)
        b.store_x(s, entry_dve=moe_dve)
        prev_dve = moe_dve
        prev_pe = b.sPE.n
    b.seq_merge = False
    b.flush("seqs")
    for name, buf, shape, dtype in taps:
        b.tap(name, buf(b), shape, dtype)
    b.flush_taps()
    b.finish()
    return b


def numpy_ref(inp, x_slice):
    def ln(v):
        m = v.mean(-1, keepdims=True)
        s = v.var(-1, keepdims=True)
        return (v - m) / np.sqrt(s + 1e-5)
    out = []
    wq = np.asarray(inp["L0_attn_wq"]); wk = np.asarray(inp["L0_attn_wk"])
    wv = np.asarray(inp["L0_attn_wv"]); wo = np.asarray(inp["L0_attn_wo"])
    Wr = np.asarray(inp["L0_ffn_W_r"]); Wk = np.asarray(inp["L0_ffn_W_k"])
    Wv = np.asarray(inp["L0_ffn_W_v"])
    mur = np.asarray(inp["L0_ffn_mu_r"]); muk = np.asarray(inp["L0_ffn_mu_k"])
    nu = np.asarray(inp["L1_attn_nu_log"]); th = np.asarray(inp["L1_attn_theta_log"])
    ga = np.asarray(inp["L1_attn_gamma_log"])
    xre = np.asarray(inp["L1_attn_x_re"]); xim = np.asarray(inp["L1_attn_x_im"])
    Bre = np.asarray(inp["L1_attn_B_re"]); Bim = np.asarray(inp["L1_attn_B_im"])
    Cre = np.asarray(inp["L1_attn_C_re"]); Cim = np.asarray(inp["L1_attn_C_im"])
    Dm = np.asarray(inp["L1_attn_D"])
    gwm = np.asarray(inp["L1_ffn_gate_w"]); wu = np.asarray(inp["L1_ffn_w_up"])
    wd = np.asarray(inp["L1_ffn_w_down"]); eb = np.asarray(inp["L1_ffn_expert_bias"])
    lam = np.exp(-np.exp(nu)); lre = lam * np.cos(np.exp(th)); lim = lam * np.sin(np.exp(th))
    gam = np.exp(ga)
    cre = lre * xre - lim * xim
    cim = lre * xim + lim * xre
    for xs in x_slice:
        x = xs.astype(np.float32)
        xn = ln(x)
        att = np.zeros_like(x)
        for hh in range(16):
            q = xn @ wq[hh].T; k = xn @ wk[hh].T; v = xn @ wv[hh].T
            sco = (q @ k.T) / 8.0
            mask = np.tril(np.ones((T, T), dtype=bool))
            sco = np.where(mask, sco, -np.inf)
            m = sco.max(-1, keepdims=True)
            eo = np.exp(sco - m)
            w = eo / eo.sum(-1, keepdims=True)
            att[:, hh * 64:(hh + 1) * 64] = w @ v
        x = x + att @ wo.T
        xprev = np.vstack([np.zeros((1, D), np.float32), xn[:-1]])
        xr = mur * xn + (1 - mur) * xprev
        xk = muk * xn + (1 - muk) * xprev
        r = 1 / (1 + np.exp(-(xr @ Wr.T)))
        kk = np.maximum(xk @ Wk.T, 0) ** 2
        x = x + r * (kk @ Wv.T)
        xn2 = ln(x)
        Bu_re = xn2 @ Bre.T; Bu_im = xn2 @ Bim.T
        zr = lre * xre - lim * xim + gam * Bu_re
        zi = lre * xim + lim * xre + gam * Bu_im
        y = 2 * (zr @ Cre.T - zi @ Cim.T) + xn2 @ Dm.T
        x = x + y
        sc = 1 / (1 + np.exp(-(xn2 @ gwm.T)))
        bi = sc + eb
        sel = np.argsort(-bi, axis=-1, kind="stable")[:, :2]
        outm = np.zeros_like(x)
        for ee in range(8):
            wt = np.where((sel == ee).any(-1), np.take_along_axis(sc, sel, -1)[
                np.arange(sc.shape[0]), (sel == ee).argmax(-1)] * (sel == ee).any(-1), 0.0)
            xe = xn2 * wt[:, None]
            hdd = np.maximum(xe @ wu[ee].T, 0) ** 2
            outm += hdd @ wd[ee].T
        x = x + outm
        out.append(x)
    return np.stack(out)



from concourse.bass_utils import run_bass_kernel_spmd


def kernel(**inputs):
    common, xs = host_prep(inputs, n_cores=8)
    nc = bass.Bass("TRN2", target_bir_lowering=False)
    with ExitStack() as stack:
        build_full(nc, stack)
    in_maps = []
    for ci in range(8):
        m = dict(common)
        m["x_in"] = xs[ci]
        in_maps.append(m)
    res = run_bass_kernel_spmd(nc, in_maps, core_ids=list(range(8)))
    out = np.concatenate([unprep_x(res.results[ci]["x_out"]) for ci in range(8)], axis=0)
    return np.ascontiguousarray(out.astype(np.float32))


# version 19
# speedup vs baseline: 1.1447x; 1.1447x over previous
import numpy as np
from contextlib import ExitStack

import concourse.bass as bass
import concourse.mybir as mybir

F32 = mybir.dt.float32
BF16 = mybir.dt.bfloat16
AF = mybir.ActivationFunctionType
ALU = mybir.AluOpType
AX = mybir.AxisListType

T = 2048
D = 1024
DC = 8
S = 2
NT = 4
PAIRS = 8
SC = 16
FC = 16
NE = 8
TH = 1024
EPS = 1e-5

ENGS = ("sync", "tensor", "vector", "scalar", "gpsimd")
MERGE_SEQ = True



def _chunk128(a):
    n = a.shape[0] // 128
    return np.ascontiguousarray(a.reshape(n, 128, -1).transpose(1, 0, 2).reshape(128, -1))


def _colvec(a):
    n = a.shape[0] // 128
    return np.ascontiguousarray(a.reshape(n, 128).T)


def host_prep(inp, n_cores=8):
    f = lambda a: np.ascontiguousarray(a.astype(np.float32))
    c = {}
    for nm in ("wq", "wk", "wv"):
        w = np.asarray(inp[f"L0_attn_{nm}"])
        wT = w.transpose(2, 0, 1).reshape(D, D)
        c[nm + "_h"] = f(np.stack([_chunk128(wT[:, p * 128:(p + 1) * 128]) for p in range(PAIRS)]))
    woT = np.asarray(inp["L0_attn_wo"]).T
    c["wo_h"] = f(np.stack([_chunk128(woT[:, j * 128:(j + 1) * 128]) for j in range(8)]))
    for nm, key in (("wr", "L0_ffn_W_r"), ("wkk", "L0_ffn_W_k"), ("wvv", "L0_ffn_W_v")):
        wT = np.asarray(inp[key]).T
        c[nm + "_h"] = f(np.stack([_chunk128(wT[:, j * 128:(j + 1) * 128]) for j in range(8)]))
    c["mu_r_h"] = f(_colvec(np.asarray(inp["L0_ffn_mu_r"])))
    c["mu_k_h"] = f(_colvec(np.asarray(inp["L0_ffn_mu_k"])))
    c["bre_h"] = f(_chunk128(np.asarray(inp["L1_attn_B_re"])))
    c["bim_h"] = f(_chunk128(np.asarray(inp["L1_attn_B_im"])))
    c["cre_h"] = f(_chunk128(np.asarray(inp["L1_attn_C_re"]).T))
    c["cim_h"] = f(_chunk128(np.asarray(inp["L1_attn_C_im"]).T))
    c["dt_h"] = f(_chunk128(np.asarray(inp["L1_attn_D"]).T))
    for nm, key in (("vre", "L1_attn_x_re"), ("vim", "L1_attn_x_im"), ("vnu", "L1_attn_nu_log"),
                    ("vth", "L1_attn_theta_log"), ("vga", "L1_attn_gamma_log")):
        c[nm + "_h"] = f(_colvec(np.asarray(inp[key])))
    c["gw_h"] = f(_chunk128(np.asarray(inp["L1_ffn_gate_w"]).T))
    c["ebias_h"] = f(np.tile(np.asarray(inp["L1_ffn_expert_bias"]), (128, 16)))
    wuT = np.asarray(inp["L1_ffn_w_up"]).transpose(0, 2, 1)
    c["wu_h"] = f(np.stack([
        np.stack([_chunk128(wuT[e][:, fs * 128:(fs + 1) * 128]) for fs in range(FC)])
        for e in range(NE)]))
    wdT = np.asarray(inp["L1_ffn_w_down"]).transpose(0, 2, 1)
    c["wd_h"] = f(np.stack([_chunk128(wdT[e]) for e in range(NE)]))
    c["eye_h"] = np.eye(128, dtype=np.float32)
    c["tri_h"] = np.triu(np.ones((128, 128), dtype=np.float32))
    ind2 = np.zeros((128, 2), dtype=np.float32); ind2[0:64, 0] = 1.0; ind2[64:128, 1] = 1.0
    c["ind2_h"] = ind2
    ind2t = np.zeros((2, 256), dtype=np.float32); ind2t[0, 0:128] = 1.0; ind2t[1, 128:256] = 1.0
    c["ind2t_h"] = ind2t
    ind8 = np.zeros((8, 8 * 128), dtype=np.float32)
    for e in range(8):
        ind8[e, e * 128:(e + 1) * 128] = 1.0
    c["ind8_h"] = ind8
    c["ones_h"] = np.ones((128, 128), dtype=np.float32)
    x = np.asarray(inp["L0_attn_x"])
    xs = []
    for ci in range(n_cores):
        sl = x[2 * ci:2 * ci + 2]
        a = sl.transpose(0, 2, 1).reshape(S, DC, 128, T).transpose(0, 2, 1, 3).reshape(S, 128, DC * T)
        xs.append(f(a))
    return c, xs


def unprep_x(y):
    y = np.asarray(y)
    n = y.shape[0]
    return y.reshape(n, 128, DC, T).transpose(0, 2, 1, 3).reshape(n, D, T).transpose(0, 2, 1)



class Sem:
    def __init__(self, h):
        self.h = h
        self.n = 0

    def inc(self, k=1):
        self.n += k
        return self.n


class Builder:
    def __init__(self, nc, stack):
        self.nc = nc
        self.stack = stack
        self.tap_list = []
        self._ctgt = {}
        self._ck = 0
        self.q = {e: [] for e in ENGS}
        e = stack.enter_context
        self.sW = Sem(e(nc.semaphore("sW")))
        self.sW2 = Sem(e(nc.semaphore("sW2")))
        self.sW3 = Sem(e(nc.semaphore("sW3")))
        self.sW4 = Sem(e(nc.semaphore("sW4")))
        self._wsems = tuple([self.sW, self.sW2, self.sW3, self.sW4] +
                            [Sem(e(nc.semaphore(f"sW{i}"))) for i in range(5, 9)])
        self._plc = {0: (0, 0)}
        self._plcall = {0: (0,) * 8}
        self.sX = Sem(e(nc.semaphore("sX")))
        self.sPE = Sem(e(nc.semaphore("sPE")))
        self.sACT = Sem(e(nc.semaphore("sACT")))
        self.sDVE = Sem(e(nc.semaphore("sDVE")))
        self.sPL = Sem(e(nc.semaphore("sPL")))
        self._alloc()
        self.d = {}
        self._declare_dram()

    def _alloc(self):
        e = self.stack.enter_context
        nc = self.nc
        self.x_sb = e(nc.sbuf_tensor("x_sb", [128, DC * T], F32))
        self.xn = e(nc.sbuf_tensor("xn", [128, DC * T], BF16))
        self.big1 = e(nc.sbuf_tensor("big1", [128, DC * T], BF16))
        self.big2 = e(nc.sbuf_tensor("big2", [128, DC * T], BF16))
        self.wring = e(nc.sbuf_tensor("wring", [128, 7 * 1024], BF16))
        fpool = e(nc.sbuf_tensor("fpool", [128, 6144], F32))
        self.fs1 = fpool[:, 0:2048]
        self.fs2 = fpool[:, 2048:4096]
        self.wstg = fpool[:, 4096:6144]
        self._smf = e(nc.sbuf_tensor("smallf", [128, 448], F32))
        self._smb = e(nc.sbuf_tensor("smallb", [128, 3584], BF16))
        cf = [0]
        def vf(w, rows=128):
            a = self._smf[0:rows, cf[0]:cf[0] + w]; cf[0] += w; return a
        cb = [0]
        def vb(w, rows=128):
            a = self._smb[0:rows, cb[0]:cb[0] + w]; cb[0] += w; return a
        self.eye = vf(128)
        self.onesf = vf(128)
        self.negCb = vf(2)
        self.mu_r = vf(8)
        self.mu_k = vf(8)
        self.ebias = vf(128)
        self.cvec = vf(8)
        self.l1w = self.fs2[:, 0:192]
        self.scT = self.fs2[:, 192:320]
        self.gtmp = self.fs2[:, 320:832]
        self.smax = vf(8, rows=2)
        self._fc1 = vf(4)
        self._fc2 = vf(4)
        self._af1 = vf(4)
        self._fcr = vf(16)
        self._fci = 0
        self.tri = vb(128)
        self.ind2 = vb(2)
        self.ind2t = vb(256, rows=2)
        self.ind8 = vb(1024, rows=8)
        self.gwsb = vb(64)
        self.l1wb = vb(32)
        self.smaxb = vb(2, rows=2)
        self.w2rows = vb(2048, rows=8)
        self.psA = e(nc.psum_tensor("psA", [128, 2048], F32))
        self.psB = e(nc.psum_tensor("psB", [128, 2048], F32))

    def _declare_dram(self):
        nc = self.nc

        def din(name, shape, dtype=F32):
            self.d[name] = nc.dram_tensor(name, list(shape), dtype, kind="ExternalInput").ap()
        din("x_in", [S, 128, DC * T])
        for nm in ("wq_h", "wk_h", "wv_h", "wo_h", "wr_h", "wkk_h", "wvv_h"):
            din(nm, [8, 128, 1024])
        din("mu_r_h", [128, 8]); din("mu_k_h", [128, 8])
        din("bre_h", [128, SC * 1024]); din("bim_h", [128, SC * 1024])
        din("cre_h", [128, SC * 1024]); din("cim_h", [128, SC * 1024])
        din("dt_h", [128, DC * 1024])
        for nm in ("vre_h", "vim_h", "vnu_h", "vth_h", "vga_h"):
            din(nm, [128, 16])
        din("gw_h", [128, 64]); din("ebias_h", [128, 128])
        din("wu_h", [NE, FC, 128, 1024])
        din("wd_h", [NE, 128, FC * 1024])
        din("eye_h", [128, 128]); din("tri_h", [128, 128]); din("ind2_h", [128, 2])
        din("ind2t_h", [2, 256]); din("ind8_h", [8, 8 * 128]); din("ones_h", [128, 128])
        self.d["x_out"] = nc.dram_tensor("x_out", [S, 128, DC * T], F32, kind="ExternalOutput").ap()
        self.d["mtot_scr"] = nc.dram_tensor("mtot_scr", [128, DC * 1024], BF16, kind="Internal").ap()

    def cload(self, dram_ap, dst, pe_war=0, rows=128, width=1024, queue="gpsimd",
              dve_war=0):
        sPE = self.sPE
        k = getattr(self, "_ck", 0)
        self._ck = k + 1
        sWp = self._wsems[k % 8]
        prev = sWp.n
        sWp.inc(16)
        @self.on(queue)
        def _(E, prev=prev, pe_war=pe_war, dve_war=dve_war, dst=dst,
              dram_ap=dram_ap, sWp=sWp):
            if prev:
                E.wait_ge(sWp.h, prev)
            if pe_war:
                E.wait_ge(sPE.h, pe_war)
            if dve_war:
                E.wait_ge(self.sDVE.h, dve_war)
            E.dma_start(out=dst, in_=dram_ap).then_inc(sWp.h, 16)
        tgt = self.sPL.inc(1)
        self._plc[tgt] = (k % 8, sWp.n)
        self._plcall[tgt] = tuple(s.n for s in self._wsems)
        return tgt

    def wpl(self, E, tgt):
        p, cnt = self._plc[tgt]
        E.wait_ge(self._wsems[p].h, cnt)

    def wpl_all(self, E, tgt):
        for sem, cnt in zip(self._wsems, self._plcall[tgt]):
            if cnt:
                E.wait_ge(sem.h, cnt)

    def dfence(self, E):
        i = self._fci
        self._fci = (i + 1) % 8
        return E.tensor_copy(self._fcr[:, i * 2:i * 2 + 2],
                             self._fc2[:, 0:2]).then_inc(self.sDVE.h, 1)

    def on(self, eng):
        def deco(fn):
            self.q[eng].append(fn)
        return deco

    def maybe_flush(self, name):
        if getattr(self, "seq_merge", False):
            import os
            keys = os.environ.get("MK_FLUSH", "")
            if keys:
                for k in keys.split(","):
                    if k and name.startswith(k):
                        self.flush(name)
                        return
            return
        self.flush(name)

    def flush(self, name):
        with self.nc.Block(name) as blk:
            for eng in ENGS:
                fns = self.q[eng]
                if fns:
                    getattr(blk, eng)(lambda E, fns=tuple(fns): [f(E) for f in fns])
        self.q = {e: [] for e in ENGS}

    def tap(self, name, buf, shape, dtype):
        out = self.nc.dram_tensor("tap_" + name, list(shape), dtype, kind="ExternalOutput").ap()
        self.tap_list.append((out, buf))

    def flush_taps(self):
        if not self.tap_list:
            return
        taps, sX = tuple(self.tap_list), self.sX
        base = sX.n
        sX.inc(16 * len(taps))
        @self.on("sync")
        def _(E, base=base):
            for i, (out, buf) in enumerate(taps):
                if base + 16 * i > 0:
                    E.wait_ge(sX.h, base + 16 * i)
                E.dma_start(out=out, in_=buf).then_inc(sX.h, 16)
        self._tapn = getattr(self, "_tapn", 0) + 1
        self.flush(f"taps{self._tapn}")
        self.tap_list = []

    def finish(self):
        sX = self.sX
        @self.on("sync")
        def _(E, n=sX.n):
            E.wait_ge(sX.h, n)
        self.flush("finish")

    def load_consts(self):
        d, sX, sW, sDVE, sPL = self.d, self.sX, self.sW, self.sDVE, self.sPL
        loads = [(self.eye, "eye_h"), (self.mu_r, "mu_r_h"), (self.mu_k, "mu_k_h"),
                 (self.ebias, "ebias_h"), (self.onesf, "ones_h")]
        for nm, buf in (("vre_h", 0), ("vim_h", 1), ("vnu_h", 2), ("vth_h", 3), ("vga_h", 4)):
            loads.append((None, nm, buf))
        casts = [(self.tri, "tri_h"), (self.ind2, "ind2_h"), (self.ind2t, "ind2t_h"),
                 (self.ind8, "ind8_h"), (self.gwsb, "gw_h")]
        l1w = self.l1w
        @self.on("sync")
        def _(E, base=sX.n):
            for i, item in enumerate(loads):
                if base + 16 * i > 0:
                    E.wait_ge(sX.h, base + 16 * i)
                if item[0] is None:
                    _, nm, slot = item
                    E.dma_start(out=l1w[:, slot * 16:(slot + 1) * 16], in_=d[nm]).then_inc(sX.h, 16)
                else:
                    buf, nm = item
                    E.dma_start(out=buf[:], in_=d[nm]).then_inc(sX.h, 16)
        sX.inc(16 * len(loads))
        shapes = {"tri_h": (128, 128), "ind2_h": (128, 2), "ind2t_h": (2, 256),
                  "ind8_h": (8, 1024), "gw_h": (128, 64)}
        last_cast = 0
        for buf, nm in casts:
            r, w = shapes[nm]
            last_cast = self.cload(d[nm], buf[:], rows=r, width=w)
        @self.on("vector")
        def _(E, tx=sX.n):
            E.wait_ge(sX.h, tx)
            E.tensor_copy(self._fc2, self.eye[0:128, 0:4])
            E.tensor_copy(self._af1, self.eye[0:128, 0:4])
        for eng in ("vector", "scalar", "tensor"):
            @self.on(eng)
            def _(E, tx=sX.n, tc=last_cast):
                E.wait_ge(sX.h, tx)
                self.wpl_all(E, tc)
        self.flush("consts")

    def load_x(self, s):
        d, sX = self.d, self.sX
        prev_store = sX.n
        tgt = sX.inc(16)
        @self.on("sync")
        def _(E, s=s, prev_store=prev_store):
            E.wait_ge(sX.h, prev_store)
            E.dma_start(out=self.x_sb[:], in_=d["x_in"][s]).then_inc(sX.h, 16)
        for eng in ("vector", "scalar", "tensor"):
            @self.on(eng)
            def _(E, tgt=tgt):
                E.wait_ge(sX.h, tgt)
        self.maybe_flush(f"loadx{s}")

    def store_x(self, s, entry_dve=0):
        d, sX, sDVE = self.d, self.sX, self.sDVE
        prev = sX.n
        sX.inc(16)
        @self.on("sync")
        def _(E, s=s, ed=entry_dve, prev=prev):
            if ed:
                E.wait_ge(sDVE.h, ed)
            if prev:
                E.wait_ge(sX.h, prev)
            E.dma_start(out=d["x_out"][s], in_=self.x_sb[:]).then_inc(sX.h, 16)
        self.maybe_flush(f"storex{s}")

    def ln(self, tag="", entry_dve=0):
        x, xn = self.x_sb, self.xn
        fs1, fs2, onesf = self.fs1, self.fs2, self.onesf
        psA, psB = self.psA, self.psB
        sPE, sACT, sDVE = self.sPE, self.sACT, self.sDVE
        p_bc_prev = [0]
        d_norm_prev = [0]
        for h in range(2):
            t0 = h * TH
            accA = fs2[:, 0:TH]
            sqs = (fs2[:, TH:2 * TH], fs1[:, 0:TH])
            act0 = sACT.n
            pe0s = sPE.n
            @self.on("scalar")
            def _(E, t0=t0, pe0s=pe0s, pb=p_bc_prev[0], dn=d_norm_prev[0],
                  ed=entry_dve if h == 0 else 0, sqs=sqs):
                if ed:
                    E.wait_ge(sDVE.h, ed)
                if pb:
                    E.wait_ge(sPE.h, pb)
                if dn:
                    E.wait_ge(sDVE.h, dn)
                for ci in range(DC):
                    if ci >= 2:
                        E.wait_ge(sPE.h, pe0s + (ci - 1))
                    E.activation(sqs[ci % 2], x[:, ci * T + t0:ci * T + t0 + TH],
                                 AF.Square).then_inc(sACT.h, 1)
            sACT.inc(DC)
            @self.on("vector")
            def _(E, t0=t0, pb=p_bc_prev[0]):
                if pb:
                    E.wait_ge(sPE.h, pb)
                E.tensor_add(accA, x[:, t0:t0 + TH], x[:, T + t0:T + t0 + TH])
                for ci in range(2, DC):
                    ti = E.tensor_add(accA, accA, x[:, ci * T + t0:ci * T + t0 + TH])
                ti.then_inc(sDVE.h, 1)
            sDVE.inc(1)
            d_sums = sDVE.n
            @self.on("tensor")
            def _(E, act0=act0, d_sums=d_sums, sqs=sqs):
                for ci in range(DC):
                    E.wait_ge(sACT.h, act0 + ci + 1)
                    for nt in range(2):
                        mi = E.matmul(psA[0:1, 1024 + nt * 512:1024 + (nt + 1) * 512],
                                      onesf[:, 0:1],
                                      sqs[ci % 2][:, nt * 512:(nt + 1) * 512],
                                      start=(ci == 0), stop=(ci == DC - 1))
                    mi.then_inc(sPE.h, 1)
                E.wait_ge(sDVE.h, d_sums)
                for nt in range(2):
                    mi = E.matmul(psA[0:1, nt * 512:(nt + 1) * 512], onesf[:, 0:1],
                                  accA[:, nt * 512:(nt + 1) * 512], start=True, stop=True)
                mi.then_inc(sPE.h, 1)
            sPE.inc(DC)
            p_sums = sPE.inc(1)
            slot0 = fs1[0:1, TH:2 * TH]
            slot1 = fs1[0:1, 0:TH]
            psmean = psA[0:1, 0:TH]
            psms = psA[0:1, TH:2 * TH]
            @self.on("vector")
            def _(E, p_sums=p_sums):
                E.wait_ge(sPE.h, p_sums)
                E.tensor_scalar(slot1, psmean, 1.0 / D, None, ALU.mult)
                self.dfence(E)
                E.tensor_mul(slot0, slot1, slot1)
                E.tensor_scalar(psms, psms, 1.0 / D, None, ALU.mult)
                self.dfence(E)
                E.tensor_sub(psms, psms, slot0)
                self.dfence(E)
                E.tensor_scalar(psms, psms, EPS, None, ALU.add)
                self.dfence(E)
            d_var = sDVE.inc(4)
            a_ln = sACT.inc(1)
            @self.on("scalar")
            def _(E, d_var=d_var, a_ln=a_ln):
                E.wait_ge(sDVE.h, d_var)
                E.activation(psms, psms, AF.Ln).then_inc(sACT.h, 1)
                E.wait_ge(sACT.h, a_ln)
                E.activation(slot0, psms, AF.Exp, scale=-0.5).then_inc(sACT.h, 1)
            a_rstd = sACT.inc(1)
            @self.on("vector")
            def _(E, a_rstd=a_rstd):
                E.wait_ge(sACT.h, a_rstd)
                E.scalar_tensor_tensor(slot1, slot1, -1.0, slot0, ALU.mult, ALU.mult)
                self.dfence(E)
            d_nmr = sDVE.inc(1)
            @self.on("tensor")
            def _(E, d_nmr=d_nmr):
                E.wait_ge(sDVE.h, d_nmr)
                for nt in range(2):
                    E.matmul(psB[:, nt * 512:(nt + 1) * 512], onesf[0:1, :],
                             fs1[0:1, 1024 + nt * 512:1024 + nt * 512 + 512], start=True, stop=True)
                for nt in range(2):
                    mi = E.matmul(psB[:, 1024 + nt * 512:1024 + (nt + 1) * 512], onesf[0:1, :],
                                  fs1[0:1, nt * 512:nt * 512 + 512], start=True, stop=True)
                mi.then_inc(sPE.h, 1)
            p_bc = sPE.inc(1)
            dve_nb = sDVE.n
            @self.on("vector")
            def _(E, t0=t0, p_bc=p_bc):
                E.wait_ge(sPE.h, p_bc)
                for ci in range(DC):
                    E.tensor_mul(fs1[:, 0:TH], x[:, ci * T + t0:ci * T + t0 + TH], psB[:, 0:TH])
                    self.dfence(E)
                    E.tensor_add(xn[:, ci * T + t0:ci * T + t0 + TH], fs1[:, 0:TH],
                                 psB[:, TH:2 * TH]).then_inc(sDVE.h, 1)
            sDVE.inc(2 * DC)
            self._ln_chain = [dve_nb + 2 * (ci + 1) for ci in range(DC)]
            p_bc_prev[0] = p_bc
            d_norm_prev[0] = sDVE.n
        self.maybe_flush("ln" + tag)
        return sDVE.n

    def attn(self, tag="", entry_dve=0, entry_pe=0):
        x, xn, big1, big2, wring = self.x_sb, self.xn, self.big1, self.big2, self.wring
        fs1, fs2 = self.fs1, self.fs2
        psA, psB = self.psA, self.psB
        sPE, sACT, sDVE, sW, sPL = self.sPE, self.sACT, self.sDVE, self.sW, self.sPL
        d = self.d
        onesf, ind2, ind2t, tri = self.onesf, self.ind2, self.ind2t, self.tri
        smax, smaxb, negCb = self.smax, self.smaxb, self.negCb
        qp = big2[:, 0:2048]
        kp = big2[:, 2048:4096]
        vbufs = (big2[:, 4096:5136], big2[:, 5248:6288])
        eslot = [big2[:, 6400 + i * 1024:6400 + (i + 1) * 1024] for i in range(3)]
        qsq = big2[:, 9472:11520]
        ksq = big2[:, 11520:13568]
        wslot = [wring[:, i * 1024:(i + 1) * 1024] for i in range(7)]

        pair_q_done = [0] * PAIRS
        pair_k_done = [0] * PAIRS
        pair_v_done = [0] * PAIRS
        pair_vend = [0] * PAIRS
        pair_all_done = [0] * PAIRS
        prev_head_evac = [entry_dve]

        for p in range(PAIRS):
            wq_t = self.cload(d["wq_h"][p], wslot[(p % 2) + 0][:, :],
                              pe_war=pair_q_done[p - 2] if p >= 2 else entry_pe)
            wk_t = self.cload(d["wk_h"][p], wslot[(p % 2) + 2][:, :],
                              pe_war=pair_k_done[p - 2] if p >= 2 else entry_pe)
            wv_t = self.cload(d["wv_h"][p], wslot[(p % 2) + 4][:, :],
                              pe_war=pair_vend[p - 2] if p >= 2 else entry_pe)
            qev_prev = prev_head_evac[0]
            ln_chain = tuple(getattr(self, "_ln_chain", ())) if p == 0 else ()
            @self.on("tensor")
            def _(E, p=p, wq_t=wq_t, qev=qev_prev, ln_chain=ln_chain):
                self.wpl(E, wq_t)
                if qev and not ln_chain:
                    E.wait_ge(sDVE.h, qev)
                sl = wslot[(p % 2) + 0]
                for c in range(DC):
                    if ln_chain:
                        E.wait_ge(sDVE.h, ln_chain[c])
                    for nt in range(NT):
                        mi = E.matmul(psA[:, nt * 512:(nt + 1) * 512], sl[:, c * 128:(c + 1) * 128],
                                      xn[:, c * T + nt * 512:c * T + (nt + 1) * 512],
                                      start=(c == 0), stop=(c == DC - 1))
                mi.then_inc(sPE.h, 1)
            pair_q_done[p] = sPE.inc(1)
            @self.on("scalar")
            def _(E, p=p, t=pair_q_done[p]):
                E.wait_ge(sPE.h, t)
                E.activation(qp, psA[:, 0:2048], AF.Identity)
                E.activation(qsq, psA[:, 0:2048], AF.Square).then_inc(sACT.h, 1)
            a_qev = sACT.inc(1)
            @self.on("tensor")
            def _(E, p=p, wk_t=wk_t):
                self.wpl(E, wk_t)
                sl = wslot[(p % 2) + 2]
                for c in range(DC):
                    for nt in range(NT):
                        mi = E.matmul(psB[:, nt * 512:(nt + 1) * 512], sl[:, c * 128:(c + 1) * 128],
                                      xn[:, c * T + nt * 512:c * T + (nt + 1) * 512],
                                      start=(c == 0), stop=(c == DC - 1))
                mi.then_inc(sPE.h, 1)
            pair_k_done[p] = sPE.inc(1)
            @self.on("scalar")
            def _(E, p=p, t=pair_k_done[p]):
                E.wait_ge(sPE.h, t)
                E.activation(kp, psB[:, 0:2048], AF.Identity)
                E.activation(ksq, psB[:, 0:2048], AF.Square).then_inc(sACT.h, 1)
            a_kev = sACT.inc(1)
            @self.on("tensor")
            def _(E, a_qev=a_qev, a_kev=a_kev):
                E.wait_ge(sACT.h, a_qev)
                for nt in range(NT):
                    mi = E.matmul(psA[0:2, nt * 512:(nt + 1) * 512], ind2[:, :],
                                  qsq[:, nt * 512:(nt + 1) * 512], start=True, stop=True)
                E.wait_ge(sACT.h, a_kev)
                for nt in range(NT):
                    mi = E.matmul(psB[0:2, nt * 512:(nt + 1) * 512], ind2[:, :],
                                  ksq[:, nt * 512:(nt + 1) * 512], start=True, stop=True)
                mi.then_inc(sPE.h, 1)
            p_norm = sPE.inc(1)
            @self.on("vector")
            def _(E, p_norm=p_norm):
                E.wait_ge(sPE.h, p_norm)
                E.reduce_max(smax[:, 0:1], psA[0:2, 0:2048], AX.X)
                E.reduce_max(smax[:, 1:2], psB[0:2, 0:2048], AX.X)
                self.dfence(E)
                E.tensor_mul(smax[:, 2:3], smax[:, 0:1], smax[:, 1:2])
                self.dfence(E)
            d_norm = sDVE.inc(2)
            @self.on("scalar")
            def _(E, d_norm=d_norm):
                E.wait_ge(sDVE.h, d_norm)
                E.activation(smax[:, 3:4], smax[:, 2:3], AF.Sqrt).then_inc(sACT.h, 1)
                E.wait_ge(sACT.h, sACT.n + 1)
            a_sq1 = 0
            q = self.q["scalar"].pop()
            a_sq1 = sACT.inc(1)
            @self.on("scalar")
            def _(E, d_norm=d_norm, a_sq1=a_sq1):
                E.wait_ge(sDVE.h, d_norm)
                E.activation(smax[:, 3:4], smax[:, 2:3], AF.Sqrt).then_inc(sACT.h, 1)
                E.wait_ge(sACT.h, a_sq1)
                E.activation(smaxb[:, 0:1], smax[:, 3:4], AF.Copy, scale=-0.125).then_inc(sACT.h, 1)
            a_negc = sACT.inc(1)
            @self.on("tensor")
            def _(E, a_negc=a_negc, d_norm=d_norm):
                E.wait_ge(sACT.h, a_negc)
                E.matmul(psA[:, 0:1], ind2t[:, 0:128], smaxb[0:2, 0:1], start=True, stop=True)
                mi = E.matmul(psA[:, 512:513], ind2t[:, 128:256], smaxb[0:2, 0:1], start=True, stop=True)
                mi.then_inc(sPE.h, 1)
            p_negc = sPE.inc(1)
            @self.on("vector")
            def _(E, p_negc=p_negc):
                E.wait_ge(sPE.h, p_negc)
                E.tensor_copy(negCb[:, 0:1], psA[:, 0:1])
                E.tensor_copy(negCb[:, 1:2], psA[:, 512:513]).then_inc(sDVE.h, 1)
            d_negc = sDVE.inc(1)
            @self.on("vector")
            def _(E):
                E.memset(vbufs[0][:, :], 1.0)
                E.memset(vbufs[1][:, :], 1.0).then_inc(sDVE.h, 1)
            d_vmem = sDVE.inc(1)
            @self.on("tensor")
            def _(E, p=p, wv_t=wv_t, d_norm=d_norm):
                self.wpl(E, wv_t)
                E.wait_ge(sDVE.h, d_norm)
                sl = wslot[(p % 2) + 4]
                for c in range(DC):
                    for nt in range(NT):
                        mi = E.matmul(psB[:, nt * 512:(nt + 1) * 512], sl[:, c * 128:(c + 1) * 128],
                                      xn[:, c * T + nt * 512:c * T + (nt + 1) * 512],
                                      start=(c == 0), stop=(c == DC - 1))
                mi.then_inc(sPE.h, 1)
            p_vfm = sPE.inc(1)
            @self.on("scalar")
            def _(E, p=p, t=p_vfm):
                E.wait_ge(sPE.h, t)
                E.activation(fs2[:, 0:2048], psB[:, 0:2048], AF.Identity).then_inc(sACT.h, 1)
            a_vfm = sACT.inc(1)
            vtr = []
            @self.on("tensor")
            def _(E, a_vfm=a_vfm, base=sPE.n, dve_base=sDVE.n, d_vmem=d_vmem, d_negc=d_negc):
                E.wait_ge(sACT.h, a_vfm)
                E.wait_ge(sDVE.h, d_negc)
                for tc in range(16):
                    if tc >= 2:
                        E.wait_ge(sDVE.h, dve_base + (tc - 1) - 1 + 1)
                    mi = E.transpose(psA[:, (tc % 2) * 512:(tc % 2) * 512 + 128],
                                     fs2[:, tc * 128:(tc + 1) * 128], self.eye[:, :])
                    mi.then_inc(sPE.h, 1)
            for tc in range(16):
                vtr.append(sPE.inc(1))
            pair_vend[p] = vtr[-1]
            @self.on("vector")
            def _(E, vtr=tuple(vtr), d_vmem=d_vmem):
                E.wait_ge(sDVE.h, d_vmem)
                for tc in range(16):
                    E.wait_ge(sPE.h, vtr[tc])
                    pv = psA[:, (tc % 2) * 512:(tc % 2) * 512 + 128]
                    E.tensor_copy(vbufs[0][:, tc * 65:tc * 65 + 64], pv[:, 0:64])
                    E.tensor_copy(vbufs[1][:, tc * 65:tc * 65 + 64],
                                  pv[:, 64:128]).then_inc(sDVE.h, 1)
            vev = []
            for tc in range(16):
                vev.append(sDVE.inc(1))
            a_vdone = 0
            d_vdone = vev[-1]

            for h01 in range(2):
                r0 = 64 * h01
                vX = vbufs[h01]
                pieces = []
                for kc in range(16):
                    qs = 128 * kc
                    first = True
                    while qs < T:
                        qe = min(T, (qs // 1024 + 1) * 1024)
                        pieces.append((kc, qs, qe, first))
                        qs = qe
                        first = False
                npc = len(pieces)
                s_cnt = [0] * npc
                av_cnt = [0] * npc
                e_cnt = [0] * npc
                m_cnt = {}
                order = []
                for i in range(npc):
                    order.append(("s", i))
                    if i >= 1:
                        order.append(("av", i - 1))
                order.append(("av", npc - 1))
                cnt = sPE.n
                for kind, i in order:
                    cnt += 1
                    if kind == "s":
                        s_cnt[i] = cnt
                    else:
                        av_cnt[i] = cnt
                sPE.n = cnt
                a_exp0 = sACT.n
                for i in range(npc):
                    e_cnt[i] = a_exp0 + 2 * (i + 1)
                sACT.n = a_exp0 + 2 * npc
                d_m0 = sDVE.n
                nd = 0
                for i in range(npc):
                    if pieces[i][3]:
                        nd += 1
                        m_cnt[i] = d_m0 + nd
                sDVE.n = d_m0 + nd
                @self.on("tensor")
                def _(E, pieces=tuple(pieces), order=tuple(order), s_cnt=tuple(s_cnt),
                      e_cnt=tuple(e_cnt), m_cnt=dict(m_cnt), r0=r0, vX=vX, p=p,
                      d_vdone=d_vdone, d_negc=d_negc, hev=prev_head_evac[0]):
                    E.wait_ge(sDVE.h, d_vdone)
                    E.wait_ge(sDVE.h, d_negc)
                    if hev:
                        E.wait_ge(sDVE.h, hev)
                    for kind, i in order:
                        kc, qs, qe, diag = pieces[i]
                        w = qe - qs
                        if kind == "s":
                            if i >= 2:
                                E.wait_ge(sACT.h, e_cnt[i - 2])
                            ring = i % 2
                            for o in range(0, w, 512):
                                ww = min(512, w - o)
                                mi = E.matmul(psA[:, ring * 1024 + o:ring * 1024 + o + ww],
                                              kp[r0:r0 + 64, kc * 128:(kc + 1) * 128],
                                              qp[r0:r0 + 64, qs + o:qs + o + ww],
                                              start=True, stop=True)
                        else:
                            if i in m_cnt:
                                E.wait_ge(sDVE.h, m_cnt[i])
                            else:
                                E.wait_ge(sACT.h, e_cnt[i])
                            for o in range(0, w, 512):
                                ww = min(512, w - o)
                                mi = E.matmul(psB[0:65, qs + o:qs + o + ww],
                                              vX[:, kc * 65:(kc + 1) * 65],
                                              eslot[i % 3][:, o:o + ww], start=(kc == 0),
                                              stop=(kc == 15))
                        mi.then_inc(sPE.h, 1)
                @self.on("scalar")
                def _(E, pieces=tuple(pieces), s_cnt=tuple(s_cnt), av_cnt=tuple(av_cnt),
                      h01=h01):
                    for i in range(len(pieces)):
                        kc, qs, qe, diag = pieces[i]
                        w = qe - qs
                        ring = i % 2
                        E.wait_ge(sPE.h, s_cnt[i])
                        if i >= 3:
                            E.wait_ge(sPE.h, av_cnt[i - 3])
                        E.activation(eslot[i % 3][:, 0:w], psA[:, ring * 1024:ring * 1024 + w],
                                     AF.Exp, bias=negCb[:, h01:h01 + 1],
                                     scale=0.125).then_inc(sACT.h, 1)
                        E.activation(self._af1, self._af1, AF.Identity).then_inc(sACT.h, 1)
                @self.on("vector")
                def _(E, pieces=tuple(pieces), e_cnt=tuple(e_cnt)):
                    for i in range(len(pieces)):
                        if pieces[i][3]:
                            E.wait_ge(sACT.h, e_cnt[i])
                            E.tensor_mul(eslot[i % 3][:, 0:128], eslot[i % 3][:, 0:128],
                                         tri[:, :]).then_inc(sDVE.h, 1)
                av_last = av_cnt[npc - 1]
                @self.on("vector")
                def _(E, av_last=av_last):
                    E.wait_ge(sPE.h, av_last)
                    E.reciprocal(fs1[0:1, 0:2048], psB[64:65, 0:2048])
                    self.dfence(E)
                d_linv = sDVE.inc(1)
                @self.on("tensor")
                def _(E, d_linv=d_linv, e_last=e_cnt[npc - 1]):
                    E.wait_ge(sDVE.h, d_linv)
                    E.wait_ge(sACT.h, e_last)
                    for nt in range(NT):
                        mi = E.matmul(psA[0:64, nt * 512:(nt + 1) * 512], onesf[0:1, 0:64],
                                      fs1[0:1, nt * 512:(nt + 1) * 512], start=True, stop=True)
                    mi.then_inc(sPE.h, 1)
                p_lb = sPE.inc(1)
                @self.on("vector")
                def _(E, p_lb=p_lb, r0=r0, p=p):
                    E.wait_ge(sPE.h, p_lb)
                    E.tensor_copy(fs2[0:64, 0:2048], psA[0:64, 0:2048])
                    self.dfence(E)
                    E.tensor_mul(big1[r0:r0 + 64, p * 2048:(p + 1) * 2048],
                                 psB[0:64, 0:2048], fs2[0:64, 0:2048]).then_inc(sDVE.h, 1)
                sDVE.inc(1)
                prev_head_evac[0] = sDVE.inc(1)
            pair_all_done[p] = prev_head_evac[0]

        xadd_done = [0] * 8
        wo_pe = [0] * 8
        for j in range(8):
            wo_t = self.cload(d["wo_h"][j], wslot[4 + (j % 2)][:, :],
                              pe_war=wo_pe[j - 2] if j >= 2 else pair_vend[6 + (j % 2)])
            ps = psA if (j % 2 == 0) else psB
            war2 = xadd_done[j - 2] if j >= 2 else pair_all_done[7]
            @self.on("tensor")
            def _(E, j=j, wo_t=wo_t, ps=ps, war2=war2):
                self.wpl(E, wo_t)
                if war2:
                    E.wait_ge(sDVE.h, war2)
                sl = wslot[4 + (j % 2)]
                for c in range(DC):
                    for nt in range(NT):
                        mi = E.matmul(ps[:, nt * 512:(nt + 1) * 512], sl[:, c * 128:(c + 1) * 128],
                                      big1[:, c * 2048 + nt * 512:c * 2048 + (nt + 1) * 512],
                                      start=(c == 0), stop=(c == DC - 1))
                mi.then_inc(sPE.h, 1)
            p_j = sPE.inc(1)
            wo_pe[j] = p_j
            @self.on("vector")
            def _(E, j=j, p_j=p_j, ps=ps):
                E.wait_ge(sPE.h, p_j)
                for nt in range(NT):
                    ti = E.tensor_add(x[:, j * 2048 + nt * 512:j * 2048 + (nt + 1) * 512],
                                      x[:, j * 2048 + nt * 512:j * 2048 + (nt + 1) * 512],
                                      ps[:, nt * 512:(nt + 1) * 512])
                ti.then_inc(sDVE.h, 1)
            xadd_done[j] = sDVE.inc(1)
        self.maybe_flush("attn" + tag)
        return wo_pe[7], xadd_done[7]

    def _shift_mix(self, mu, d_after_pe=0):
        xn, big1 = self.xn, self.big1
        sDVE, sPE = self.sDVE, self.sPE
        @self.on("vector")
        def _(E, mu=mu, d_after_pe=d_after_pe):
            if d_after_pe:
                E.wait_ge(sPE.h, d_after_pe)
            for c in range(DC):
                E.tensor_sub(big1[:, c * T + 1:c * T + T], xn[:, c * T + 1:c * T + T],
                             xn[:, c * T:c * T + T - 1])
                self.dfence(E)
                E.scalar_tensor_tensor(big1[:, c * T + 1:c * T + T], big1[:, c * T + 1:c * T + T],
                                       mu[:, c:c + 1], xn[:, c * T:c * T + T - 1],
                                       ALU.mult, ALU.add)
                E.tensor_scalar(big1[:, c * T:c * T + 1], xn[:, c * T:c * T + 1],
                                mu[:, c:c + 1], None, ALU.mult)
            self.dfence(E)
        return sDVE.inc(DC + 1)

    def ffn0(self, tag="", entry_pe=0):
        x, xn, big1, big2, wring = self.x_sb, self.xn, self.big1, self.big2, self.wring
        fs1 = self.fs1
        psA, psB = self.psA, self.psB
        sPE, sACT, sDVE, sW = self.sPE, self.sACT, self.sDVE, self.sW
        sPL = self.sPL
        d = self.d
        wslot = [wring[:, i * 1024:(i + 1) * 1024] for i in range(7)]
        st_buf = wring[:, 6 * 1024:7 * 1024]

        d_xk = self._shift_mix(self.mu_k, d_after_pe=entry_pe)
        evac_dve = [0] * 8
        evac_act = [0] * 8
        kk_pe = [0] * 8
        for j in range(8):
            wk_t = self.cload(d["wkk_h"][j], wslot[j % 2][:, :],
                              pe_war=kk_pe[j - 2] if j >= 2 else entry_pe)
            ps = psA if j % 2 == 0 else psB
            @self.on("tensor")
            def _(E, j=j, wk_t=wk_t, ps=ps, d_xk=d_xk, war=evac_act[j - 2] if j >= 2 else 0):
                self.wpl(E, wk_t)
                if j == 0:
                    E.wait_ge(sDVE.h, d_xk)
                if war:
                    E.wait_ge(sACT.h, war)
                sl = wslot[j % 2]
                for c in range(DC):
                    for nt in range(NT):
                        mi = E.matmul(ps[:, nt * 512:(nt + 1) * 512], sl[:, c * 128:(c + 1) * 128],
                                      big1[:, c * T + nt * 512:c * T + (nt + 1) * 512],
                                      start=(c == 0), stop=(c == DC - 1))
                mi.then_inc(sPE.h, 1)
            p_j = sPE.inc(1)
            kk_pe[j] = p_j
            a_half = [sACT.inc(1), sACT.inc(1)]
            d_half = [sDVE.inc(1), sDVE.inc(1)]
            @self.on("scalar")
            def _(E, j=j, p_j=p_j, ps=ps, war=evac_dve[j - 1] if j >= 1 else 0,
                  d_half=tuple(d_half)):
                E.wait_ge(sPE.h, p_j)
                if war:
                    E.wait_ge(sDVE.h, war)
                E.activation(st_buf, ps[:, 0:1024], AF.Relu).then_inc(sACT.h, 1)
                E.wait_ge(sDVE.h, d_half[0])
                E.activation(st_buf, ps[:, 1024:2048], AF.Relu).then_inc(sACT.h, 1)
            @self.on("vector")
            def _(E, j=j, a_half=tuple(a_half)):
                E.wait_ge(sACT.h, a_half[0])
                E.tensor_mul(big2[:, j * 2048:j * 2048 + 1024], st_buf, st_buf).then_inc(sDVE.h, 1)
                E.wait_ge(sACT.h, a_half[1])
                E.tensor_mul(big2[:, j * 2048 + 1024:(j + 1) * 2048], st_buf, st_buf).then_inc(sDVE.h, 1)
            evac_act[j] = a_half[1]
            evac_dve[j] = d_half[1]
        d_xr = self._shift_mix(self.mu_r, d_after_pe=sPE.n)
        r_act = [0] * 8
        x_dve = [0] * 8
        rv_pe = [0] * 8
        for j in range(8):
            wr_t = self.cload(d["wr_h"][j], wslot[2 + (j % 2)][:, :],
                              pe_war=rv_pe[j - 2] if j >= 2 else entry_pe)
            wv_t = self.cload(d["wvv_h"][j], wslot[4 + (j % 2)][:, :],
                              pe_war=rv_pe[j - 2] if j >= 2 else entry_pe)
            @self.on("tensor")
            def _(E, j=j, wr_t=wr_t, wv_t=wv_t, d_xr=d_xr,
                  war_a=r_act[j - 1] if j >= 1 else 0,
                  war_d=x_dve[j - 1] if j >= 1 else 0):
                self.wpl(E, wr_t)
                if j == 0:
                    E.wait_ge(sDVE.h, d_xr)
                if war_a:
                    E.wait_ge(sACT.h, war_a)
                sl = wslot[2 + (j % 2)]
                for c in range(DC):
                    for nt in range(NT):
                        mi = E.matmul(psA[:, nt * 512:(nt + 1) * 512], sl[:, c * 128:(c + 1) * 128],
                                      big1[:, c * T + nt * 512:c * T + (nt + 1) * 512],
                                      start=(c == 0), stop=(c == DC - 1))
                mi.then_inc(sPE.h, 1)
                self.wpl(E, wv_t)
                if war_d:
                    E.wait_ge(sDVE.h, war_d)
                sl = wslot[4 + (j % 2)]
                for c in range(DC):
                    for nt in range(NT):
                        mi = E.matmul(psB[:, nt * 512:(nt + 1) * 512], sl[:, c * 128:(c + 1) * 128],
                                      big2[:, c * T + nt * 512:c * T + (nt + 1) * 512],
                                      start=(c == 0), stop=(c == DC - 1))
                mi.then_inc(sPE.h, 1)
            p_r = sPE.inc(1)
            p_v = sPE.inc(1)
            rv_pe[j] = p_v
            ra_half = [sACT.inc(1), sACT.inc(1)]
            xd_half = [0, 0]
            sDVE.inc(2)
            xd_half[0] = sDVE.inc(1)
            sDVE.inc(2)
            xd_half[1] = sDVE.inc(1)
            @self.on("scalar")
            def _(E, j=j, p_r=p_r, war=x_dve[j - 1] if j >= 1 else 0,
                  xd_half=tuple(xd_half)):
                E.wait_ge(sPE.h, p_r)
                if war:
                    E.wait_ge(sDVE.h, war)
                E.activation(st_buf, psA[:, 0:1024], AF.Sigmoid).then_inc(sACT.h, 1)
                E.wait_ge(sDVE.h, xd_half[0])
                E.activation(st_buf, psA[:, 1024:2048], AF.Sigmoid).then_inc(sACT.h, 1)
            @self.on("vector")
            def _(E, j=j, p_v=p_v, ra_half=tuple(ra_half)):
                E.wait_ge(sPE.h, p_v)
                for h2 in range(2):
                    E.wait_ge(sACT.h, ra_half[h2])
                    for nt in (2 * h2, 2 * h2 + 1):
                        E.tensor_mul(fs1[:, 0:512], psB[:, nt * 512:(nt + 1) * 512],
                                     st_buf[:, (nt % 2) * 512:(nt % 2) * 512 + 512])
                        self.dfence(E)
                        ti = E.tensor_add(x[:, j * 2048 + nt * 512:j * 2048 + (nt + 1) * 512],
                                          x[:, j * 2048 + nt * 512:j * 2048 + (nt + 1) * 512],
                                          fs1[:, 0:512])
                    ti.then_inc(sDVE.h, 1)
            r_act[j] = ra_half[1]
            x_dve[j] = xd_half[1]
        self.maybe_flush("ffn0" + tag)
        return rv_pe[7], x_dve[7]

    def fold(self):
        big1, big2, wring, fs1 = self.big1, self.big2, self.wring, self.fs1
        psA, psB = self.psA, self.psB
        sPE, sACT, sDVE, sW, sX = self.sPE, self.sACT, self.sDVE, self.sW, self.sX
        sPL = self.sPL
        d, l1w, l1wb, cvec = self.d, self.l1w, self.l1wb, self.cvec
        wslot = [wring[:, i * 1024:(i + 1) * 1024] for i in range(7)]
        sl_ = lambda i: l1w[:, i * 16:(i + 1) * 16]

        aN = [sACT.inc(1) for _ in range(4)]
        a2 = aN[3]
        @self.on("scalar")
        def _(E, aN=tuple(aN)):
            E.activation(sl_(5), sl_(2), AF.Exp).then_inc(sACT.h, 1)
            E.activation(sl_(6), sl_(3), AF.Exp).then_inc(sACT.h, 1)
            E.wait_ge(sACT.h, aN[0])
            E.activation(sl_(2), sl_(4), AF.Exp).then_inc(sACT.h, 1)
            E.wait_ge(sACT.h, aN[2])
            E.activation(sl_(5), sl_(5), AF.Exp, scale=-1.0).then_inc(sACT.h, 1)
        d1 = sDVE.inc(1)
        @self.on("vector")
        def _(E, a2=a2, d1=d1):
            E.wait_ge(sACT.h, a2)
            E.tensor_scalar(sl_(7), sl_(6), -1.0, 1.5707963267948966, ALU.mult, ALU.add)
            E.tensor_scalar(sl_(9), sl_(2), 2.0, None, ALU.mult)
            E.tensor_scalar(sl_(10), sl_(2), -2.0, None, ALU.mult)
            self.dfence(E)
        a3 = sACT.inc(1)
        @self.on("scalar")
        def _(E, d1=d1, a3=a3):
            E.wait_ge(sDVE.h, d1)
            E.activation(sl_(7), sl_(7), AF.Sin)
            E.activation(sl_(8), sl_(6), AF.Sin).then_inc(sACT.h, 1)
        d2 = sDVE.inc(1); d3 = sDVE.inc(1); d4 = sDVE.inc(1); d_vecs = sDVE.inc(1)
        @self.on("vector")
        def _(E, a3=a3, d2=d2, d3=d3, d4=d4, d_vecs=d_vecs):
            E.wait_ge(sACT.h, a3)
            E.tensor_mul(sl_(3), sl_(5), sl_(7))
            E.tensor_mul(sl_(4), sl_(5), sl_(8)).then_inc(sDVE.h, 1)
            E.wait_ge(sDVE.h, d2)
            E.tensor_mul(sl_(6), sl_(3), sl_(0))
            E.tensor_mul(sl_(7), sl_(4), sl_(1))
            E.tensor_mul(sl_(8), sl_(3), sl_(1))
            E.tensor_mul(sl_(11), sl_(4), sl_(0)).then_inc(sDVE.h, 1)
            E.wait_ge(sDVE.h, d3)
            E.tensor_sub(sl_(6), sl_(6), sl_(7))
            E.tensor_add(sl_(8), sl_(8), sl_(11)).then_inc(sDVE.h, 1)
            E.wait_ge(sDVE.h, d4)
            E.tensor_scalar(l1wb[:, 0:16], sl_(6), 2.0, None, ALU.mult)
            E.tensor_scalar(l1wb[:, 16:32], sl_(8), -2.0, None, ALU.mult)
            self.dfence(E)
        self.flush("foldv")

        pe0 = sPE.n
        ctg = []
        for sc in range(SC):
            war = pe0 + sc - 1 if pe0 + sc - 1 > 0 else 0
            t1 = self.cload(d["cre_h"][:, sc * 1024:(sc + 1) * 1024], wslot[sc % 2][:, :], pe_war=war)
            t2 = self.cload(d["cim_h"][:, sc * 1024:(sc + 1) * 1024], wslot[2 + sc % 2][:, :], pe_war=war)
            ctg.append(t2)
        @self.on("tensor")
        def _(E, ctg=tuple(ctg), d_vecs=d_vecs, pe0=pe0):
            E.wait_ge(sDVE.h, d_vecs)
            for sc in range(SC):
                self.wpl(E, ctg[sc])
                for jc in range(8):
                    ps = psA if jc < 4 else psB
                    col = (jc % 4) * 512
                    E.matmul(ps[:, col:col + 1], wslot[sc % 2][:, jc * 128:(jc + 1) * 128],
                             l1wb[:, sc:sc + 1], start=(sc == 0), stop=False)
                    mi = E.matmul(ps[:, col:col + 1], wslot[2 + sc % 2][:, jc * 128:(jc + 1) * 128],
                                  l1wb[:, 16 + sc:16 + sc + 1], start=False, stop=(sc == SC - 1))
                mi.then_inc(sPE.h, 1)
        for sc in range(SC):
            sPE.inc(1)
        p_cvec = sPE.n
        @self.on("vector")
        def _(E, p_cvec=p_cvec):
            E.wait_ge(sPE.h, p_cvec)
            for jc in range(8):
                ps = psA if jc < 4 else psB
                col = (jc % 4) * 512
                ti = E.tensor_copy(cvec[:, jc:jc + 1], ps[:, col:col + 1])
            ti.then_inc(sDVE.h, 1)
        d_cvec = sDVE.inc(1)
        self.flush("foldc")

        for c8 in range(8):
            dt_t = self.cload(d["dt_h"][:, c8 * 1024:(c8 + 1) * 1024],
                              big2[:, 8192 + c8 * 1024:8192 + (c8 + 1) * 1024])
        self._dt_t = dt_t
        mm_done = {}
        for g in range(2):
            for ri, (bh, gsl) in enumerate((("bre_h", 9), ("bim_h", 10))):
                pidx = g * 2 + ri
                dve0 = sDVE.n
                ch = "cre_h" if ri == 0 else "cim_h"
                pe0 = sPE.n
                btg = []
                ctg = []
                for sc in range(SC):
                    bwar = dve0 + sc - 1 if dve0 + sc - 1 > 0 else 0
                    btg.append(self.cload(d[bh][:, sc * 1024:(sc + 1) * 1024],
                                          wslot[sc % 2][:, :], dve_war=bwar))
                    war = pe0 + sc - 1 if pe0 + sc - 1 > 0 else 0
                    ctg.append(self.cload(d[ch][:, sc * 1024:(sc + 1) * 1024],
                                          wslot[4 + sc % 2][:, :], pe_war=war))
                @self.on("vector")
                def _(E, btg=tuple(btg), gsl=gsl, pidx=pidx, mmp=dict(mm_done)):
                    for sc in range(SC):
                        self.wpl(E, btg[sc])
                        if pidx > 0:
                            E.wait_ge(sPE.h, mmp[(pidx - 1, sc)])
                        E.tensor_scalar(big1[:, sc * 1024:(sc + 1) * 1024],
                                        wslot[sc % 2][:, :],
                                        l1w[:, gsl * 16 + sc:gsl * 16 + sc + 1], None,
                                        ALU.mult)
                        self.dfence(E)
                sDVE.inc(SC)
                @self.on("tensor")
                def _(E, g=g, ctg=tuple(ctg), dve0=dve0, pidx=pidx, pe0=pe0, ri=ri):
                    for sc in range(SC):
                        self.wpl(E, ctg[sc])
                        E.wait_ge(sDVE.h, dve0 + sc + 1)
                        for djl in range(4):
                            dj = g * 4 + djl
                            for jh in range(2):
                                ps = psA if djl < 2 else psB
                                col = ((djl % 2) * 2 + jh) * 512
                                mi = E.matmul(ps[:, col:col + 512],
                                              big1[:, sc * 1024 + dj * 128:sc * 1024 + (dj + 1) * 128],
                                              wslot[4 + sc % 2][:, jh * 512:(jh + 1) * 512],
                                              start=(ri == 0 and sc == 0),
                                              stop=(ri == 1 and sc == SC - 1))
                        mi.then_inc(sPE.h, 1)
                for sc in range(SC):
                    mm_done[(pidx, sc)] = sPE.inc(1)
                if getattr(self, "_fold_debug", False) and pidx == 0:
                    @self.on("vector")
                    def _(E, t=mm_done[(0, SC - 1)]):
                        E.wait_ge(sPE.h, t)
                        E.tensor_copy(fs1[:, 0:512], psA[:, 0:512])
                        E.tensor_copy(fs1[:, 512:1024], psA[:, 512:1024]).then_inc(sDVE.h, 1)
                    sDVE.inc(1)
                    dbg = sDVE.n
                    @self.on("tensor")
                    def _(E, dbg=dbg):
                        E.wait_ge(sDVE.h, dbg)
            @self.on("vector")
            def _(E, g=g, t=mm_done[(g * 2 + 1, SC - 1)], dtt=self._dt_t):
                E.wait_ge(sPE.h, t)
                self.wpl_all(E, dtt)
                for djl in range(4):
                    dj = g * 4 + djl
                    for jh in range(2):
                        ps = psA if djl < 2 else psB
                        col = ((djl % 2) * 2 + jh) * 512
                        E.scalar_tensor_tensor(
                            big2[:, dj * 1024 + jh * 512:dj * 1024 + jh * 512 + 512],
                            ps[:, col:col + 512], 1.0,
                            big2[:, 8192 + dj * 1024 + jh * 512:8192 + dj * 1024 + jh * 512 + 512],
                            ALU.mult, ALU.add)
                self.dfence(E)
            sDVE.inc(1)
            if g == 0:
                mm_done[(1, -1)] = None
                ev = sDVE.n
                @self.on("tensor")
                def _(E, ev=ev):
                    E.wait_ge(sDVE.h, ev)
        ev_all = sDVE.n
        prev = sX.n
        sX.inc(16)
        @self.on("sync")
        def _(E, ev_all=ev_all, prev=prev):
            E.wait_ge(sDVE.h, ev_all)
            if prev:
                E.wait_ge(sX.h, prev)
            E.dma_start(out=d["mtot_scr"], in_=big2[:, 0:8192]).then_inc(sX.h, 16)
        self.flush("foldm")

    def l1apply(self, tag="", entry_dve=0, entry_pe=0):
        x, xn, big2 = self.x_sb, self.xn, self.big2
        psA, psB = self.psA, self.psB
        sPE, sDVE, sX = self.sPE, self.sDVE, self.sX
        d, cvec = self.d, self.cvec
        mt = sX.inc(16)
        @self.on("sync")
        def _(E, mt=mt, ep=entry_pe):
            if ep:
                E.wait_ge(sPE.h, ep)
            E.wait_ge(sX.h, mt - 16)
            E.dma_start(out=big2[:, 0:8192], in_=d["mtot_scr"]).then_inc(sX.h, 16)
        x_dve = [0] * 8
        for jc in range(8):
            ps = psA if jc % 2 == 0 else psB
            @self.on("tensor")
            def _(E, jc=jc, ps=ps, mt=mt, war=x_dve[jc - 2] if jc >= 2 else 0,
                  ed=entry_dve if jc == 0 else 0):
                if jc == 0:
                    E.wait_ge(sX.h, mt)
                if ed:
                    E.wait_ge(sDVE.h, ed)
                if war:
                    E.wait_ge(sDVE.h, war)
                for c in range(DC):
                    for nt in range(NT):
                        mi = E.matmul(ps[:, nt * 512:(nt + 1) * 512],
                                      big2[:, c * 1024 + jc * 128:c * 1024 + jc * 128 + 128],
                                      xn[:, c * T + nt * 512:c * T + (nt + 1) * 512],
                                      start=(c == 0), stop=(c == DC - 1))
                mi.then_inc(sPE.h, 1)
            p_j = sPE.inc(1)
            @self.on("vector")
            def _(E, jc=jc, p_j=p_j, ps=ps):
                E.wait_ge(sPE.h, p_j)
                for nt in range(NT):
                    E.scalar_tensor_tensor(
                        x[:, jc * 2048 + nt * 512:jc * 2048 + (nt + 1) * 512],
                        ps[:, nt * 512:(nt + 1) * 512], cvec[:, jc:jc + 1],
                        x[:, jc * 2048 + nt * 512:jc * 2048 + (nt + 1) * 512],
                        ALU.add, ALU.add)
                self.dfence(E)
            x_dve[jc] = sDVE.inc(1)
        self.maybe_flush("l1a" + tag)
        return p_j, x_dve[7]

    def gating(self, tag="", entry_dve=0):
        xn, gtmp, scT = self.xn, self.gtmp, self.scT
        psA, psB = self.psA, self.psB
        sPE, sACT, sDVE = self.sPE, self.sACT, self.sDVE
        gw, ebias, eye, w2rows = self.gwsb, self.ebias, self.eye, self.w2rows
        @self.on("tensor")
        def _(E, ed=entry_dve):
            if ed:
                E.wait_ge(sDVE.h, ed)
            for tt in range(16):
                for c in range(DC):
                    mi = E.matmul(psA[:, tt * 8:(tt + 1) * 8],
                                  xn[:, c * T + tt * 128:c * T + (tt + 1) * 128],
                                  gw[:, c * 8:(c + 1) * 8],
                                  start=(c == 0), stop=(c == DC - 1))
            mi.then_inc(sPE.h, 1)
        p_sc = sPE.inc(1)
        @self.on("scalar")
        def _(E, p_sc=p_sc):
            E.wait_ge(sPE.h, p_sc)
            E.activation(scT[:, 0:128], psA[:, 0:128], AF.Sigmoid).then_inc(sACT.h, 1)
        a_sc = sACT.inc(1)
        biased = gtmp[:, 0:128]
        m1 = gtmp[:, 128:144]
        mask1 = gtmp[:, 144:272]
        cnt = gtmp[:, 272:288]
        b2 = gtmp[:, 288:416]
        m2 = gtmp[:, 416:432]
        ge2 = gtmp[:, 432:448]
        dm = gtmp[:, 448:464]
        t8 = gtmp[:, 464:480]
        thr = gtmp[:, 480:496]
        w2t = b2
        d_base = sDVE.n
        d_w2 = sDVE.inc(11)
        @self.on("vector")
        def _(E, a_sc=a_sc, d_base=d_base):
            _b = bass
            def r3(ap):
                return ap.rearrange("p (t e) -> p t e", e=8)
            def bcast(big3, small):
                a, b2_ = _b.broadcast_tensor_aps(big3, small.rearrange("p (t e) -> p t e", e=1))
                return a, b2_
            E.wait_ge(sACT.h, a_sc)
            E.tensor_add(biased, scT[:, 0:128], ebias[:, 0:128])
            self.dfence(E)
            E.tensor_reduce(m1, r3(biased), AX.X, ALU.max)
            self.dfence(E)
            a, b = bcast(r3(biased), m1)
            E.tensor_tensor(r3(mask1), a, b, ALU.is_ge)
            self.dfence(E)
            E.tensor_reduce(cnt, r3(mask1), AX.X, ALU.add)
            E.scalar_tensor_tensor(b2, mask1, -1e30, biased, ALU.mult, ALU.add)
            self.dfence(E)
            E.tensor_reduce(m2, r3(b2), AX.X, ALU.max)
            E.tensor_scalar(ge2, cnt, 1.5, None, ALU.is_ge)
            self.dfence(E)
            E.tensor_sub(dm, m1, m2)
            self.dfence(E)
            E.tensor_mul(t8, dm, ge2)
            self.dfence(E)
            E.tensor_add(thr, t8, m2)
            self.dfence(E)
            a, b = bcast(r3(biased), thr)
            E.tensor_tensor(r3(mask1), a, b, ALU.is_ge)
            self.dfence(E)
            E.tensor_mul(mask1, scT[:, 0:128], mask1)
            self.dfence(E)
            E.tensor_mul(w2t, mask1, mask1).then_inc(sDVE.h, 1)
        if getattr(self, "_gate_half", False):
            self.flush("gate" + tag)
            return
        tr_done = []
        @self.on("tensor")
        def _(E, d_w2=d_w2):
            E.wait_ge(sDVE.h, d_w2)
            for tt in range(16):
                if tt >= 4:
                    E.wait_ge(sDVE.h, d_w2 + tt - 3)
                mi = E.transpose(psB[0:8, (tt % 4) * 512:(tt % 4) * 512 + 128],
                                 w2t[:, tt * 8:(tt + 1) * 8], eye[:, :])
                mi.then_inc(sPE.h, 1)
        tr0 = sPE.n
        for tt in range(16):
            tr_done.append(sPE.inc(1))
        @self.on("vector")
        def _(E, tr=tuple(tr_done)):
            for tt in range(16):
                E.wait_ge(sPE.h, tr[tt])
                E.tensor_copy(w2rows[0:8, tt * 128:(tt + 1) * 128],
                              psB[0:8, (tt % 4) * 512:(tt % 4) * 512 + 128]).then_inc(sDVE.h, 1)
        gate_dve = 0
        for tt in range(16):
            gate_dve = sDVE.inc(1)
        self.maybe_flush("gate" + tag)
        return gate_dve

    def moe(self, tag="", entry_pe=0, entry_dve=0):
        x, xn, big1, big2, wring = self.x_sb, self.xn, self.big1, self.big2, self.wring
        psA, psB = self.psA, self.psB
        sPE, sACT, sDVE = self.sPE, self.sACT, self.sDVE
        sPL = self.sPL
        d = self.d
        wslot = [wring[:, i * 1024:(i + 1) * 1024] for i in range(7)]
        w2b = wslot[4]
        s_sl = wslot[5]
        t_sl = wslot[6]
        wu_k = [0]
        wu_pe = {}
        prev_down_pe = [entry_pe]
        prev_xadd = [entry_dve]
        prev_upevac = [0]
        first_pe = entry_pe
        wu_tgts = {}

        def emit_wu(e, h, f):
            k = wu_k[0]; wu_k[0] += 1
            wu_tgts[(e, h, f)] = (self.cload(d["wu_h"][e, f], wslot[k % 4][:, :],
                                             pe_war=wu_pe.get(k - 4, first_pe)), k)
        for e in range(NE):
            wd_t = 0
            for fc in range(FC):
                wd_t = self.cload(d["wd_h"][e][:, fc * 1024:(fc + 1) * 1024],
                                  big2[:, fc * 1024:(fc + 1) * 1024],
                                  pe_war=prev_down_pe[0])
            for h in range(2):
                for f in range(4):
                    if (e, h, f) not in wu_tgts:
                        emit_wu(e, h, f)
                @self.on("tensor")
                def _(E, e=e, h=h, dwar=prev_xadd[0]):
                    if dwar:
                        E.wait_ge(sDVE.h, dwar)
                    for hnt in range(2):
                        mi = E.matmul(psB[:, hnt * 512:(hnt + 1) * 512],
                                      self.ind8[0:8, e * 128:(e + 1) * 128],
                                      self.w2rows[0:8, h * 1024 + hnt * 512:h * 1024 + (hnt + 1) * 512],
                                      start=True, stop=True)
                    mi.then_inc(sPE.h, 1)
                p_bc = sPE.inc(1)
                @self.on("scalar")
                def _(E, p_bc=p_bc, uwar=prev_upevac[0], fp=first_pe):
                    E.wait_ge(sPE.h, p_bc)
                    if uwar:
                        E.wait_ge(sDVE.h, uwar)
                    elif fp:
                        E.wait_ge(sPE.h, fp)
                    E.activation(w2b, psB[:, 0:1024], AF.Identity).then_inc(sACT.h, 1)
                a_w2b = sACT.inc(1)
                up_act = [0] * FC
                up_dve = [0] * FC
                for f in range(FC):
                    if (e, h, f) not in wu_tgts:
                        emit_wu(e, h, f)
                    wu_t, k = wu_tgts[(e, h, f)]
                    @self.on("tensor")
                    def _(E, e=e, h=h, f=f, k=k, wu_t=wu_t,
                          awar=up_act[f - 2] if f >= 2 else 0):
                        self.wpl(E, wu_t)
                        if awar:
                            E.wait_ge(sACT.h, awar)
                        sl = wslot[k % 4]
                        for c in range(DC):
                            for ntu in range(2):
                                mi = E.matmul(
                                    psA[:, (f % 2) * 1024 + ntu * 512:(f % 2) * 1024 + (ntu + 1) * 512],
                                    sl[:, c * 128:(c + 1) * 128],
                                    xn[:, c * T + h * TH + ntu * 512:c * T + h * TH + (ntu + 1) * 512],
                                    start=(c == 0), stop=(c == DC - 1))
                        mi.then_inc(sPE.h, 1)
                    p_f = sPE.inc(1)
                    wu_pe[k] = p_f
                    @self.on("scalar")
                    def _(E, f=f, p_f=p_f, a_w2b=a_w2b,
                          dwar=up_dve[f - 1] if f >= 1 else 0,
                          fp=first_pe if (e == 0 and h == 0 and f == 0) else 0):
                        E.wait_ge(sPE.h, p_f)
                        E.wait_ge(sACT.h, a_w2b)
                        if dwar:
                            E.wait_ge(sDVE.h, dwar)
                        elif fp:
                            E.wait_ge(sPE.h, fp)
                        E.activation(s_sl, psA[:, (f % 2) * 1024:(f % 2) * 1024 + 1024],
                                     AF.Relu).then_inc(sACT.h, 1)
                    up_act[f] = sACT.inc(1)
                    @self.on("vector")
                    def _(E, f=f, a_f=up_act[f], pwar=prev_down_pe[0]):
                        E.wait_ge(sACT.h, a_f)
                        if pwar and f < 2:
                            E.wait_ge(sPE.h, pwar)
                        E.tensor_mul(t_sl, s_sl, w2b)
                        self.dfence(E)
                        E.tensor_mul(big1[:, f * 1024:(f + 1) * 1024], t_sl,
                                     s_sl).then_inc(sDVE.h, 1)
                    sDVE.inc(1)
                    up_dve[f] = sDVE.inc(1)
                prev_upevac[0] = up_dve[FC - 1]
                if h == 0:
                    for f in range(4):
                        emit_wu(e, 1, f)
                elif e + 1 < NE:
                    for f in range(4):
                        emit_wu(e + 1, 0, f)
                for j in range(8):
                    @self.on("tensor")
                    def _(E, e=e, h=h, j=j, hdone=up_dve[FC - 1], wd_t=wd_t,
                          dwar=prev_xadd[0] if j >= 2 else 0,
                          jwar=0):
                        E.wait_ge(sDVE.h, hdone)
                        self.wpl_all(E, wd_t)
                        if dwar:
                            E.wait_ge(sDVE.h, dwar)
                        for f in range(FC):
                            for ntd in range(2):
                                mi = E.matmul(
                                    psB[:, (j % 2) * 1024 + ntd * 512:(j % 2) * 1024 + (ntd + 1) * 512],
                                    big2[:, f * 1024 + j * 128:f * 1024 + j * 128 + 128],
                                    big1[:, f * 1024 + ntd * 512:f * 1024 + (ntd + 1) * 512],
                                    start=(f == 0), stop=(f == FC - 1))
                        mi.then_inc(sPE.h, 1)
                    p_j = sPE.inc(1)
                    prev_down_pe[0] = p_j
                    @self.on("vector")
                    def _(E, h=h, j=j, p_j=p_j):
                        E.wait_ge(sPE.h, p_j)
                        for ntd in range(2):
                            ti = E.tensor_add(
                                x[:, j * 2048 + h * TH + ntd * 512:j * 2048 + h * TH + (ntd + 1) * 512],
                                x[:, j * 2048 + h * TH + ntd * 512:j * 2048 + h * TH + (ntd + 1) * 512],
                                psB[:, (j % 2) * 1024 + ntd * 512:(j % 2) * 1024 + (ntd + 1) * 512])
                        ti.then_inc(sDVE.h, 1)
                    prev_xadd[0] = sDVE.inc(1)
        self.maybe_flush("moe" + tag)
        return prev_xadd[0]



def build_full(nc, stack, taps=()):
    b = Builder(nc, stack)
    b.load_consts()
    b.fold()
    b.seq_merge = MERGE_SEQ
    prev_dve = b.sDVE.n
    prev_pe = b.sPE.n
    for s in range(S):
        b.load_x(s)
        ln1_dve = b.ln(f"1s{s}", entry_dve=prev_dve)
        wo_pe, attn_dve = b.attn(f"s{s}", entry_dve=ln1_dve, entry_pe=prev_pe)
        rv_pe, ffn_dve = b.ffn0(f"s{s}", entry_pe=wo_pe)
        ln2_dve = b.ln(f"2s{s}", entry_dve=ffn_dve)
        l1_pe, l1_dve = b.l1apply(f"s{s}", entry_dve=ln2_dve, entry_pe=rv_pe)
        gate_dve = b.gating(f"s{s}", entry_dve=l1_dve)
        moe_dve = b.moe(f"s{s}", entry_pe=l1_pe, entry_dve=gate_dve)
        b.store_x(s, entry_dve=moe_dve)
        prev_dve = moe_dve
        prev_pe = b.sPE.n
    b.seq_merge = False
    b.flush("seqs")
    for name, buf, shape, dtype in taps:
        b.tap(name, buf(b), shape, dtype)
    b.flush_taps()
    b.finish()
    return b


def numpy_ref(inp, x_slice):
    def ln(v):
        m = v.mean(-1, keepdims=True)
        s = v.var(-1, keepdims=True)
        return (v - m) / np.sqrt(s + 1e-5)
    out = []
    wq = np.asarray(inp["L0_attn_wq"]); wk = np.asarray(inp["L0_attn_wk"])
    wv = np.asarray(inp["L0_attn_wv"]); wo = np.asarray(inp["L0_attn_wo"])
    Wr = np.asarray(inp["L0_ffn_W_r"]); Wk = np.asarray(inp["L0_ffn_W_k"])
    Wv = np.asarray(inp["L0_ffn_W_v"])
    mur = np.asarray(inp["L0_ffn_mu_r"]); muk = np.asarray(inp["L0_ffn_mu_k"])
    nu = np.asarray(inp["L1_attn_nu_log"]); th = np.asarray(inp["L1_attn_theta_log"])
    ga = np.asarray(inp["L1_attn_gamma_log"])
    xre = np.asarray(inp["L1_attn_x_re"]); xim = np.asarray(inp["L1_attn_x_im"])
    Bre = np.asarray(inp["L1_attn_B_re"]); Bim = np.asarray(inp["L1_attn_B_im"])
    Cre = np.asarray(inp["L1_attn_C_re"]); Cim = np.asarray(inp["L1_attn_C_im"])
    Dm = np.asarray(inp["L1_attn_D"])
    gwm = np.asarray(inp["L1_ffn_gate_w"]); wu = np.asarray(inp["L1_ffn_w_up"])
    wd = np.asarray(inp["L1_ffn_w_down"]); eb = np.asarray(inp["L1_ffn_expert_bias"])
    lam = np.exp(-np.exp(nu)); lre = lam * np.cos(np.exp(th)); lim = lam * np.sin(np.exp(th))
    gam = np.exp(ga)
    cre = lre * xre - lim * xim
    cim = lre * xim + lim * xre
    for xs in x_slice:
        x = xs.astype(np.float32)
        xn = ln(x)
        att = np.zeros_like(x)
        for hh in range(16):
            q = xn @ wq[hh].T; k = xn @ wk[hh].T; v = xn @ wv[hh].T
            sco = (q @ k.T) / 8.0
            mask = np.tril(np.ones((T, T), dtype=bool))
            sco = np.where(mask, sco, -np.inf)
            m = sco.max(-1, keepdims=True)
            eo = np.exp(sco - m)
            w = eo / eo.sum(-1, keepdims=True)
            att[:, hh * 64:(hh + 1) * 64] = w @ v
        x = x + att @ wo.T
        xprev = np.vstack([np.zeros((1, D), np.float32), xn[:-1]])
        xr = mur * xn + (1 - mur) * xprev
        xk = muk * xn + (1 - muk) * xprev
        r = 1 / (1 + np.exp(-(xr @ Wr.T)))
        kk = np.maximum(xk @ Wk.T, 0) ** 2
        x = x + r * (kk @ Wv.T)
        xn2 = ln(x)
        Bu_re = xn2 @ Bre.T; Bu_im = xn2 @ Bim.T
        zr = lre * xre - lim * xim + gam * Bu_re
        zi = lre * xim + lim * xre + gam * Bu_im
        y = 2 * (zr @ Cre.T - zi @ Cim.T) + xn2 @ Dm.T
        x = x + y
        sc = 1 / (1 + np.exp(-(xn2 @ gwm.T)))
        bi = sc + eb
        sel = np.argsort(-bi, axis=-1, kind="stable")[:, :2]
        outm = np.zeros_like(x)
        for ee in range(8):
            wt = np.where((sel == ee).any(-1), np.take_along_axis(sc, sel, -1)[
                np.arange(sc.shape[0]), (sel == ee).argmax(-1)] * (sel == ee).any(-1), 0.0)
            xe = xn2 * wt[:, None]
            hdd = np.maximum(xe @ wu[ee].T, 0) ** 2
            outm += hdd @ wd[ee].T
        x = x + outm
        out.append(x)
    return np.stack(out)



from concourse.bass_utils import run_bass_kernel_spmd


def kernel(**inputs):
    common, xs = host_prep(inputs, n_cores=8)
    nc = bass.Bass("TRN2", target_bir_lowering=False)
    with ExitStack() as stack:
        build_full(nc, stack)
    in_maps = []
    for ci in range(8):
        m = dict(common)
        m["x_in"] = xs[ci]
        in_maps.append(m)
    res = run_bass_kernel_spmd(nc, in_maps, core_ids=list(range(8)))
    out = np.concatenate([unprep_x(res.results[ci]["x_out"]) for ci in range(8)], axis=0)
    return np.ascontiguousarray(out.astype(np.float32))


# version 22
# speedup vs baseline: 1.2301x; 1.0746x over previous
import numpy as np
from contextlib import ExitStack

import concourse.bass as bass
import concourse.mybir as mybir

F32 = mybir.dt.float32
BF16 = mybir.dt.bfloat16
AF = mybir.ActivationFunctionType
ALU = mybir.AluOpType
AX = mybir.AxisListType

T = 2048
D = 1024
DC = 8
S = 2
NT = 4
PAIRS = 8
SC = 16
FC = 16
NE = 8
TH = 1024
EPS = 1e-5

ENGS = ("sync", "tensor", "vector", "scalar", "gpsimd")
MERGE_SEQ = True



def _chunk128(a):
    n = a.shape[0] // 128
    return np.ascontiguousarray(a.reshape(n, 128, -1).transpose(1, 0, 2).reshape(128, -1))


def _colvec(a):
    n = a.shape[0] // 128
    return np.ascontiguousarray(a.reshape(n, 128).T)


def host_prep(inp, n_cores=8):
    f = lambda a: np.ascontiguousarray(a.astype(np.float32))
    c = {}
    for nm in ("wq", "wk", "wv"):
        w = np.asarray(inp[f"L0_attn_{nm}"])
        wT = w.transpose(2, 0, 1).reshape(D, D)
        c[nm + "_h"] = f(np.stack([_chunk128(wT[:, p * 128:(p + 1) * 128]) for p in range(PAIRS)]))
    woT = np.asarray(inp["L0_attn_wo"]).T
    c["wo_h"] = f(np.stack([_chunk128(woT[:, j * 128:(j + 1) * 128]) for j in range(8)]))
    for nm, key in (("wr", "L0_ffn_W_r"), ("wkk", "L0_ffn_W_k"), ("wvv", "L0_ffn_W_v")):
        wT = np.asarray(inp[key]).T
        c[nm + "_h"] = f(np.stack([_chunk128(wT[:, j * 128:(j + 1) * 128]) for j in range(8)]))
    c["mu_r_h"] = f(_colvec(np.asarray(inp["L0_ffn_mu_r"])))
    c["mu_k_h"] = f(_colvec(np.asarray(inp["L0_ffn_mu_k"])))
    c["bre_h"] = f(_chunk128(np.asarray(inp["L1_attn_B_re"])))
    c["bim_h"] = f(_chunk128(np.asarray(inp["L1_attn_B_im"])))
    c["cre_h"] = f(_chunk128(np.asarray(inp["L1_attn_C_re"]).T))
    c["cim_h"] = f(_chunk128(np.asarray(inp["L1_attn_C_im"]).T))
    c["dt_h"] = f(_chunk128(np.asarray(inp["L1_attn_D"]).T))
    for nm, key in (("vre", "L1_attn_x_re"), ("vim", "L1_attn_x_im"), ("vnu", "L1_attn_nu_log"),
                    ("vth", "L1_attn_theta_log"), ("vga", "L1_attn_gamma_log")):
        c[nm + "_h"] = f(_colvec(np.asarray(inp[key])))
    c["gw_h"] = f(_chunk128(np.asarray(inp["L1_ffn_gate_w"]).T))
    c["ebias_h"] = f(np.tile(np.asarray(inp["L1_ffn_expert_bias"]), (128, 16)))
    wuT = np.asarray(inp["L1_ffn_w_up"]).transpose(0, 2, 1)
    c["wu_h"] = f(np.stack([
        np.stack([_chunk128(wuT[e][:, fs * 128:(fs + 1) * 128]) for fs in range(FC)])
        for e in range(NE)]))
    wdT = np.asarray(inp["L1_ffn_w_down"]).transpose(0, 2, 1)
    c["wd_h"] = f(np.stack([_chunk128(wdT[e]) for e in range(NE)]))
    c["eye_h"] = np.eye(128, dtype=np.float32)
    c["tri_h"] = np.triu(np.ones((128, 128), dtype=np.float32))
    ind2 = np.zeros((128, 2), dtype=np.float32); ind2[0:64, 0] = 1.0; ind2[64:128, 1] = 1.0
    c["ind2_h"] = ind2
    ind2t = np.zeros((2, 256), dtype=np.float32); ind2t[0, 0:128] = 1.0; ind2t[1, 128:256] = 1.0
    c["ind2t_h"] = ind2t
    ind8 = np.zeros((8, 8 * 128), dtype=np.float32)
    for e in range(8):
        ind8[e, e * 128:(e + 1) * 128] = 1.0
    c["ind8_h"] = ind8
    c["ones_h"] = np.ones((128, 128), dtype=np.float32)
    x = np.asarray(inp["L0_attn_x"])
    xs = []
    for ci in range(n_cores):
        sl = x[2 * ci:2 * ci + 2]
        a = sl.transpose(0, 2, 1).reshape(S, DC, 128, T).transpose(0, 2, 1, 3).reshape(S, 128, DC * T)
        xs.append(f(a))
    return c, xs


def unprep_x(y):
    y = np.asarray(y)
    n = y.shape[0]
    return y.reshape(n, 128, DC, T).transpose(0, 2, 1, 3).reshape(n, D, T).transpose(0, 2, 1)



class Sem:
    def __init__(self, h):
        self.h = h
        self.n = 0

    def inc(self, k=1):
        self.n += k
        return self.n


class Builder:
    def __init__(self, nc, stack):
        self.nc = nc
        self.stack = stack
        self.tap_list = []
        self._ctgt = {}
        self._ck = 0
        self.q = {e: [] for e in ENGS}
        e = stack.enter_context
        self.sW = Sem(e(nc.semaphore("sW")))
        self.sW2 = Sem(e(nc.semaphore("sW2")))
        self.sW3 = Sem(e(nc.semaphore("sW3")))
        self.sW4 = Sem(e(nc.semaphore("sW4")))
        self._wsems = (self.sW, self.sW2, self.sW3, self.sW4)
        self._plc = {0: (0, 0)}
        self._plcall = {0: (0,) * 4}
        self.sX = Sem(e(nc.semaphore("sX")))
        self.sPE = Sem(e(nc.semaphore("sPE")))
        self.sACT = Sem(e(nc.semaphore("sACT")))
        self.sDVE = Sem(e(nc.semaphore("sDVE")))
        self.sPL = Sem(e(nc.semaphore("sPL")))
        self._alloc()
        self.d = {}
        self._declare_dram()

    def _alloc(self):
        e = self.stack.enter_context
        nc = self.nc
        self.x_sb = e(nc.sbuf_tensor("x_sb", [128, DC * T], F32))
        self.xn = e(nc.sbuf_tensor("xn", [128, DC * T], BF16))
        self.big1 = e(nc.sbuf_tensor("big1", [128, DC * T], BF16))
        self.big2 = e(nc.sbuf_tensor("big2", [128, DC * T], BF16))
        self.wring = e(nc.sbuf_tensor("wring", [128, 7 * 1024], BF16))
        fpool = e(nc.sbuf_tensor("fpool", [128, 6144], F32))
        self.fs1 = fpool[:, 0:2048]
        self.fs2 = fpool[:, 2048:4096]
        self.wstg = fpool[:, 4096:6144]
        self._smf = e(nc.sbuf_tensor("smallf", [128, 448], F32))
        self._smb = e(nc.sbuf_tensor("smallb", [128, 3584], BF16))
        cf = [0]
        def vf(w, rows=128):
            a = self._smf[0:rows, cf[0]:cf[0] + w]; cf[0] += w; return a
        cb = [0]
        def vb(w, rows=128):
            a = self._smb[0:rows, cb[0]:cb[0] + w]; cb[0] += w; return a
        self.eye = vf(128)
        self.onesf = vf(128)
        self.negCb = vf(2)
        self.mu_r = vf(8)
        self.mu_k = vf(8)
        self.ebias = vf(128)
        self.cvec = vf(8)
        self.l1w = self.fs2[:, 0:192]
        self.scT = self.fs2[:, 192:320]
        self.gtmp = self.fs2[:, 320:832]
        self.smax = vf(8, rows=2)
        self._fc1 = vf(4)
        self._fc2 = vf(4)
        self._af1 = vf(4)
        self._fcr = vf(16)
        self._fci = 0
        self.tri = vb(128)
        self.ind2 = vb(2)
        self.ind2t = vb(256, rows=2)
        self.ind8 = vb(1024, rows=8)
        self.gwsb = vb(64)
        self.l1wb = vb(32)
        self.smaxb = vb(2, rows=2)
        self.w2rows = vb(2048, rows=8)
        self.psA = e(nc.psum_tensor("psA", [128, 2048], F32))
        self.psB = e(nc.psum_tensor("psB", [128, 2048], F32))

    def _declare_dram(self):
        nc = self.nc

        def din(name, shape, dtype=F32):
            self.d[name] = nc.dram_tensor(name, list(shape), dtype, kind="ExternalInput").ap()
        din("x_in", [S, 128, DC * T])
        for nm in ("wq_h", "wk_h", "wv_h", "wo_h", "wr_h", "wkk_h", "wvv_h"):
            din(nm, [8, 128, 1024])
        din("mu_r_h", [128, 8]); din("mu_k_h", [128, 8])
        din("bre_h", [128, SC * 1024]); din("bim_h", [128, SC * 1024])
        din("cre_h", [128, SC * 1024]); din("cim_h", [128, SC * 1024])
        din("dt_h", [128, DC * 1024])
        for nm in ("vre_h", "vim_h", "vnu_h", "vth_h", "vga_h"):
            din(nm, [128, 16])
        din("gw_h", [128, 64]); din("ebias_h", [128, 128])
        din("wu_h", [NE, FC, 128, 1024])
        din("wd_h", [NE, 128, FC * 1024])
        din("eye_h", [128, 128]); din("tri_h", [128, 128]); din("ind2_h", [128, 2])
        din("ind2t_h", [2, 256]); din("ind8_h", [8, 8 * 128]); din("ones_h", [128, 128])
        self.d["x_out"] = nc.dram_tensor("x_out", [S, 128, DC * T], F32, kind="ExternalOutput").ap()
        self.d["mtot_scr"] = nc.dram_tensor("mtot_scr", [128, DC * 1024], BF16, kind="Internal").ap()

    def cload(self, dram_ap, dst, pe_war=0, rows=128, width=1024, queue="gpsimd",
              dve_war=0):
        sPE = self.sPE
        k = getattr(self, "_ck", 0)
        self._ck = k + 1
        sWp = self._wsems[k % 4]
        prev = sWp.n
        sWp.inc(16)
        @self.on(queue)
        def _(E, prev=prev, pe_war=pe_war, dve_war=dve_war, dst=dst,
              dram_ap=dram_ap, sWp=sWp):
            if prev:
                E.wait_ge(sWp.h, prev)
            if pe_war:
                E.wait_ge(sPE.h, pe_war)
            if dve_war:
                E.wait_ge(self.sDVE.h, dve_war)
            E.dma_start(out=dst, in_=dram_ap).then_inc(sWp.h, 16)
        tgt = self.sPL.inc(1)
        self._plc[tgt] = (k % 4, sWp.n)
        self._plcall[tgt] = tuple(s.n for s in self._wsems)
        return tgt

    def wpl(self, E, tgt):
        p, cnt = self._plc[tgt]
        E.wait_ge(self._wsems[p].h, cnt)

    def wpl_all(self, E, tgt):
        for sem, cnt in zip(self._wsems, self._plcall[tgt]):
            if cnt:
                E.wait_ge(sem.h, cnt)

    def dfence(self, E):
        i = self._fci
        self._fci = (i + 1) % 8
        return E.tensor_copy(self._fcr[:, i * 2:i * 2 + 2],
                             self._fc2[:, 0:2]).then_inc(self.sDVE.h, 1)

    def on(self, eng):
        def deco(fn):
            self.q[eng].append(fn)
        return deco

    def maybe_flush(self, name):
        if getattr(self, "seq_merge", False):
            import os
            keys = os.environ.get("MK_FLUSH", "")
            if keys:
                for k in keys.split(","):
                    if k and name.startswith(k):
                        self.flush(name)
                        return
            return
        self.flush(name)

    def flush(self, name):
        with self.nc.Block(name) as blk:
            for eng in ENGS:
                fns = self.q[eng]
                if fns:
                    getattr(blk, eng)(lambda E, fns=tuple(fns): [f(E) for f in fns])
        self.q = {e: [] for e in ENGS}

    def tap(self, name, buf, shape, dtype):
        out = self.nc.dram_tensor("tap_" + name, list(shape), dtype, kind="ExternalOutput").ap()
        self.tap_list.append((out, buf))

    def flush_taps(self):
        if not self.tap_list:
            return
        taps, sX = tuple(self.tap_list), self.sX
        base = sX.n
        sX.inc(16 * len(taps))
        @self.on("sync")
        def _(E, base=base):
            for i, (out, buf) in enumerate(taps):
                if base + 16 * i > 0:
                    E.wait_ge(sX.h, base + 16 * i)
                E.dma_start(out=out, in_=buf).then_inc(sX.h, 16)
        self._tapn = getattr(self, "_tapn", 0) + 1
        self.flush(f"taps{self._tapn}")
        self.tap_list = []

    def finish(self):
        sX = self.sX
        @self.on("sync")
        def _(E, n=sX.n):
            E.wait_ge(sX.h, n)
        self.flush("finish")

    def load_consts(self):
        d, sX, sW, sDVE, sPL = self.d, self.sX, self.sW, self.sDVE, self.sPL
        loads = [(self.eye, "eye_h"), (self.mu_r, "mu_r_h"), (self.mu_k, "mu_k_h"),
                 (self.ebias, "ebias_h"), (self.onesf, "ones_h")]
        for nm, buf in (("vre_h", 0), ("vim_h", 1), ("vnu_h", 2), ("vth_h", 3), ("vga_h", 4)):
            loads.append((None, nm, buf))
        casts = [(self.tri, "tri_h"), (self.ind2, "ind2_h"), (self.ind2t, "ind2t_h"),
                 (self.ind8, "ind8_h"), (self.gwsb, "gw_h")]
        l1w = self.l1w
        @self.on("sync")
        def _(E, base=sX.n):
            for i, item in enumerate(loads):
                if base + 16 * i > 0:
                    E.wait_ge(sX.h, base + 16 * i)
                if item[0] is None:
                    _, nm, slot = item
                    E.dma_start(out=l1w[:, slot * 16:(slot + 1) * 16], in_=d[nm]).then_inc(sX.h, 16)
                else:
                    buf, nm = item
                    E.dma_start(out=buf[:], in_=d[nm]).then_inc(sX.h, 16)
        sX.inc(16 * len(loads))
        shapes = {"tri_h": (128, 128), "ind2_h": (128, 2), "ind2t_h": (2, 256),
                  "ind8_h": (8, 1024), "gw_h": (128, 64)}
        last_cast = 0
        for buf, nm in casts:
            r, w = shapes[nm]
            last_cast = self.cload(d[nm], buf[:], rows=r, width=w)
        @self.on("vector")
        def _(E, tx=sX.n):
            E.wait_ge(sX.h, tx)
            E.tensor_copy(self._fc2, self.eye[0:128, 0:4])
            E.tensor_copy(self._af1, self.eye[0:128, 0:4])
        for eng in ("vector", "scalar", "tensor"):
            @self.on(eng)
            def _(E, tx=sX.n, tc=last_cast):
                E.wait_ge(sX.h, tx)
                self.wpl_all(E, tc)
        self.flush("consts")

    def load_x(self, s):
        d, sX = self.d, self.sX
        prev_store = sX.n
        tgt = sX.inc(16)
        @self.on("sync")
        def _(E, s=s, prev_store=prev_store):
            E.wait_ge(sX.h, prev_store)
            E.dma_start(out=self.x_sb[:], in_=d["x_in"][s]).then_inc(sX.h, 16)
        for eng in ("vector", "scalar", "tensor"):
            @self.on(eng)
            def _(E, tgt=tgt):
                E.wait_ge(sX.h, tgt)
        self.maybe_flush(f"loadx{s}")

    def store_x(self, s, entry_dve=0):
        d, sX, sDVE = self.d, self.sX, self.sDVE
        prev = sX.n
        sX.inc(16)
        @self.on("sync")
        def _(E, s=s, ed=entry_dve, prev=prev):
            if ed:
                E.wait_ge(sDVE.h, ed)
            if prev:
                E.wait_ge(sX.h, prev)
            E.dma_start(out=d["x_out"][s], in_=self.x_sb[:]).then_inc(sX.h, 16)
        self.maybe_flush(f"storex{s}")

    def ln(self, tag="", entry_dve=0):
        x, xn = self.x_sb, self.xn
        fs1, fs2, onesf = self.fs1, self.fs2, self.onesf
        psA, psB = self.psA, self.psB
        sPE, sACT, sDVE = self.sPE, self.sACT, self.sDVE
        p_bc_prev = [0]
        d_norm_prev = [0]
        for h in range(2):
            t0 = h * TH
            accA = fs2[:, 0:TH]
            sqs = (fs2[:, TH:2 * TH], fs1[:, 0:TH])
            act0 = sACT.n
            pe0s = sPE.n
            @self.on("scalar")
            def _(E, t0=t0, pe0s=pe0s, pb=p_bc_prev[0], dn=d_norm_prev[0],
                  ed=entry_dve if h == 0 else 0, sqs=sqs):
                if ed:
                    E.wait_ge(sDVE.h, ed)
                if pb:
                    E.wait_ge(sPE.h, pb)
                if dn:
                    E.wait_ge(sDVE.h, dn)
                for ci in range(DC):
                    if ci >= 2:
                        E.wait_ge(sPE.h, pe0s + (ci - 1))
                    E.activation(sqs[ci % 2], x[:, ci * T + t0:ci * T + t0 + TH],
                                 AF.Square).then_inc(sACT.h, 1)
            sACT.inc(DC)
            @self.on("vector")
            def _(E, t0=t0, pb=p_bc_prev[0]):
                if pb:
                    E.wait_ge(sPE.h, pb)
                E.tensor_add(accA, x[:, t0:t0 + TH], x[:, T + t0:T + t0 + TH])
                for ci in range(2, DC):
                    ti = E.tensor_add(accA, accA, x[:, ci * T + t0:ci * T + t0 + TH])
                ti.then_inc(sDVE.h, 1)
            sDVE.inc(1)
            d_sums = sDVE.n
            @self.on("tensor")
            def _(E, act0=act0, d_sums=d_sums, sqs=sqs):
                for ci in range(DC):
                    E.wait_ge(sACT.h, act0 + ci + 1)
                    for nt in range(2):
                        mi = E.matmul(psA[0:1, 1024 + nt * 512:1024 + (nt + 1) * 512],
                                      onesf[:, 0:1],
                                      sqs[ci % 2][:, nt * 512:(nt + 1) * 512],
                                      start=(ci == 0), stop=(ci == DC - 1))
                    mi.then_inc(sPE.h, 1)
                E.wait_ge(sDVE.h, d_sums)
                for nt in range(2):
                    mi = E.matmul(psA[0:1, nt * 512:(nt + 1) * 512], onesf[:, 0:1],
                                  accA[:, nt * 512:(nt + 1) * 512], start=True, stop=True)
                mi.then_inc(sPE.h, 1)
            sPE.inc(DC)
            p_sums = sPE.inc(1)
            slot0 = fs1[0:1, TH:2 * TH]
            slot1 = fs1[0:1, 0:TH]
            psmean = psA[0:1, 0:TH]
            psms = psA[0:1, TH:2 * TH]
            @self.on("vector")
            def _(E, p_sums=p_sums):
                E.wait_ge(sPE.h, p_sums)
                E.tensor_scalar(slot1, psmean, 1.0 / D, None, ALU.mult)
                self.dfence(E)
                E.tensor_mul(slot0, slot1, slot1)
                E.tensor_scalar(psms, psms, 1.0 / D, None, ALU.mult)
                self.dfence(E)
                E.tensor_sub(psms, psms, slot0)
                self.dfence(E)
                E.tensor_scalar(psms, psms, EPS, None, ALU.add)
                self.dfence(E)
            d_var = sDVE.inc(4)
            a_ln = sACT.inc(1)
            @self.on("scalar")
            def _(E, d_var=d_var, a_ln=a_ln):
                E.wait_ge(sDVE.h, d_var)
                E.activation(psms, psms, AF.Ln).then_inc(sACT.h, 1)
                E.wait_ge(sACT.h, a_ln)
                E.activation(slot0, psms, AF.Exp, scale=-0.5).then_inc(sACT.h, 1)
            a_rstd = sACT.inc(1)
            @self.on("vector")
            def _(E, a_rstd=a_rstd):
                E.wait_ge(sACT.h, a_rstd)
                E.scalar_tensor_tensor(slot1, slot1, -1.0, slot0, ALU.mult, ALU.mult)
                self.dfence(E)
            d_nmr = sDVE.inc(1)
            @self.on("tensor")
            def _(E, d_nmr=d_nmr):
                E.wait_ge(sDVE.h, d_nmr)
                for nt in range(2):
                    E.matmul(psB[:, nt * 512:(nt + 1) * 512], onesf[0:1, :],
                             fs1[0:1, 1024 + nt * 512:1024 + nt * 512 + 512], start=True, stop=True)
                for nt in range(2):
                    mi = E.matmul(psB[:, 1024 + nt * 512:1024 + (nt + 1) * 512], onesf[0:1, :],
                                  fs1[0:1, nt * 512:nt * 512 + 512], start=True, stop=True)
                mi.then_inc(sPE.h, 1)
            p_bc = sPE.inc(1)
            dve_nb = sDVE.n
            @self.on("vector")
            def _(E, t0=t0, p_bc=p_bc):
                E.wait_ge(sPE.h, p_bc)
                for ci in range(DC):
                    E.tensor_mul(fs1[:, 0:TH], x[:, ci * T + t0:ci * T + t0 + TH], psB[:, 0:TH])
                    self.dfence(E)
                    E.tensor_add(xn[:, ci * T + t0:ci * T + t0 + TH], fs1[:, 0:TH],
                                 psB[:, TH:2 * TH]).then_inc(sDVE.h, 1)
            sDVE.inc(2 * DC)
            self._ln_chain = [dve_nb + 2 * (ci + 1) for ci in range(DC)]
            p_bc_prev[0] = p_bc
            d_norm_prev[0] = sDVE.n
        self.maybe_flush("ln" + tag)
        return sDVE.n

    def attn(self, tag="", entry_dve=0, entry_pe=0):
        x, xn, big1, big2, wring = self.x_sb, self.xn, self.big1, self.big2, self.wring
        fs1, fs2 = self.fs1, self.fs2
        psA, psB = self.psA, self.psB
        sPE, sACT, sDVE, sW, sPL = self.sPE, self.sACT, self.sDVE, self.sW, self.sPL
        d = self.d
        onesf, ind2, ind2t, tri = self.onesf, self.ind2, self.ind2t, self.tri
        smax, smaxb, negCb = self.smax, self.smaxb, self.negCb
        qp = big2[:, 0:2048]
        kp = big2[:, 2048:4096]
        vbufs = (big2[:, 4096:5136], big2[:, 5248:6288])
        eslot = [big2[:, 6400 + i * 1024:6400 + (i + 1) * 1024] for i in range(3)]
        qsq = big2[:, 9472:11520]
        ksq = big2[:, 11520:13568]
        wslot = [wring[:, i * 1024:(i + 1) * 1024] for i in range(7)]

        pair_q_done = [0] * PAIRS
        pair_k_done = [0] * PAIRS
        pair_v_done = [0] * PAIRS
        pair_vend = [0] * PAIRS
        pair_all_done = [0] * PAIRS
        prev_head_evac = [entry_dve]

        for p in range(PAIRS):
            wq_t = self.cload(d["wq_h"][p], wslot[(p % 2) + 0][:, :],
                              pe_war=pair_q_done[p - 2] if p >= 2 else entry_pe)
            wk_t = self.cload(d["wk_h"][p], wslot[(p % 2) + 2][:, :],
                              pe_war=pair_k_done[p - 2] if p >= 2 else entry_pe)
            wv_t = self.cload(d["wv_h"][p], wslot[(p % 2) + 4][:, :],
                              pe_war=pair_vend[p - 2] if p >= 2 else entry_pe)
            qev_prev = prev_head_evac[0]
            ln_chain = tuple(getattr(self, "_ln_chain", ())) if p == 0 else ()
            @self.on("tensor")
            def _(E, p=p, wq_t=wq_t, qev=qev_prev, ln_chain=ln_chain):
                self.wpl(E, wq_t)
                if qev and not ln_chain:
                    E.wait_ge(sDVE.h, qev)
                sl = wslot[(p % 2) + 0]
                for c in range(DC):
                    if ln_chain:
                        E.wait_ge(sDVE.h, ln_chain[c])
                    for nt in range(NT):
                        mi = E.matmul(psA[:, nt * 512:(nt + 1) * 512], sl[:, c * 128:(c + 1) * 128],
                                      xn[:, c * T + nt * 512:c * T + (nt + 1) * 512],
                                      start=(c == 0), stop=(c == DC - 1))
                mi.then_inc(sPE.h, 1)
            pair_q_done[p] = sPE.inc(1)
            @self.on("scalar")
            def _(E, p=p, t=pair_q_done[p]):
                E.wait_ge(sPE.h, t)
                E.activation(qp, psA[:, 0:2048], AF.Identity)
                E.activation(qsq, psA[:, 0:2048], AF.Square).then_inc(sACT.h, 1)
            a_qev = sACT.inc(1)
            @self.on("tensor")
            def _(E, p=p, wk_t=wk_t):
                self.wpl(E, wk_t)
                sl = wslot[(p % 2) + 2]
                for c in range(DC):
                    for nt in range(NT):
                        mi = E.matmul(psB[:, nt * 512:(nt + 1) * 512], sl[:, c * 128:(c + 1) * 128],
                                      xn[:, c * T + nt * 512:c * T + (nt + 1) * 512],
                                      start=(c == 0), stop=(c == DC - 1))
                mi.then_inc(sPE.h, 1)
            pair_k_done[p] = sPE.inc(1)
            @self.on("scalar")
            def _(E, p=p, t=pair_k_done[p]):
                E.wait_ge(sPE.h, t)
                E.activation(kp, psB[:, 0:2048], AF.Identity)
                E.activation(ksq, psB[:, 0:2048], AF.Square).then_inc(sACT.h, 1)
            a_kev = sACT.inc(1)
            @self.on("tensor")
            def _(E, a_qev=a_qev, a_kev=a_kev):
                E.wait_ge(sACT.h, a_qev)
                for nt in range(NT):
                    mi = E.matmul(psA[0:2, nt * 512:(nt + 1) * 512], ind2[:, :],
                                  qsq[:, nt * 512:(nt + 1) * 512], start=True, stop=True)
                E.wait_ge(sACT.h, a_kev)
                for nt in range(NT):
                    mi = E.matmul(psB[0:2, nt * 512:(nt + 1) * 512], ind2[:, :],
                                  ksq[:, nt * 512:(nt + 1) * 512], start=True, stop=True)
                mi.then_inc(sPE.h, 1)
            p_norm = sPE.inc(1)
            @self.on("vector")
            def _(E, p_norm=p_norm):
                E.wait_ge(sPE.h, p_norm)
                E.reduce_max(smax[:, 0:1], psA[0:2, 0:2048], AX.X)
                E.reduce_max(smax[:, 1:2], psB[0:2, 0:2048], AX.X)
                self.dfence(E)
                E.tensor_mul(smax[:, 2:3], smax[:, 0:1], smax[:, 1:2])
                self.dfence(E)
            d_norm = sDVE.inc(2)
            @self.on("scalar")
            def _(E, d_norm=d_norm):
                E.wait_ge(sDVE.h, d_norm)
                E.activation(smax[:, 3:4], smax[:, 2:3], AF.Sqrt).then_inc(sACT.h, 1)
                E.wait_ge(sACT.h, sACT.n + 1)
            a_sq1 = 0
            q = self.q["scalar"].pop()
            a_sq1 = sACT.inc(1)
            @self.on("scalar")
            def _(E, d_norm=d_norm, a_sq1=a_sq1):
                E.wait_ge(sDVE.h, d_norm)
                E.activation(smax[:, 3:4], smax[:, 2:3], AF.Sqrt).then_inc(sACT.h, 1)
                E.wait_ge(sACT.h, a_sq1)
                E.activation(smaxb[:, 0:1], smax[:, 3:4], AF.Copy, scale=-0.125).then_inc(sACT.h, 1)
            a_negc = sACT.inc(1)
            @self.on("tensor")
            def _(E, a_negc=a_negc, d_norm=d_norm):
                E.wait_ge(sACT.h, a_negc)
                E.matmul(psA[:, 0:1], ind2t[:, 0:128], smaxb[0:2, 0:1], start=True, stop=True)
                mi = E.matmul(psA[:, 512:513], ind2t[:, 128:256], smaxb[0:2, 0:1], start=True, stop=True)
                mi.then_inc(sPE.h, 1)
            p_negc = sPE.inc(1)
            @self.on("vector")
            def _(E, p_negc=p_negc):
                E.wait_ge(sPE.h, p_negc)
                E.tensor_copy(negCb[:, 0:1], psA[:, 0:1])
                E.tensor_copy(negCb[:, 1:2], psA[:, 512:513]).then_inc(sDVE.h, 1)
            d_negc = sDVE.inc(1)
            @self.on("vector")
            def _(E):
                E.memset(vbufs[0][:, :], 1.0)
                E.memset(vbufs[1][:, :], 1.0).then_inc(sDVE.h, 1)
            d_vmem = sDVE.inc(1)
            @self.on("tensor")
            def _(E, p=p, wv_t=wv_t, d_norm=d_norm):
                self.wpl(E, wv_t)
                E.wait_ge(sDVE.h, d_norm)
                sl = wslot[(p % 2) + 4]
                for c in range(DC):
                    for nt in range(NT):
                        mi = E.matmul(psB[:, nt * 512:(nt + 1) * 512], sl[:, c * 128:(c + 1) * 128],
                                      xn[:, c * T + nt * 512:c * T + (nt + 1) * 512],
                                      start=(c == 0), stop=(c == DC - 1))
                mi.then_inc(sPE.h, 1)
            p_vfm = sPE.inc(1)
            @self.on("scalar")
            def _(E, p=p, t=p_vfm):
                E.wait_ge(sPE.h, t)
                E.activation(fs2[:, 0:2048], psB[:, 0:2048], AF.Identity).then_inc(sACT.h, 1)
            a_vfm = sACT.inc(1)
            vtr = []
            @self.on("tensor")
            def _(E, a_vfm=a_vfm, base=sPE.n, dve_base=sDVE.n, d_vmem=d_vmem, d_negc=d_negc):
                E.wait_ge(sACT.h, a_vfm)
                E.wait_ge(sDVE.h, d_negc)
                for tc in range(16):
                    if tc >= 2:
                        E.wait_ge(sDVE.h, dve_base + (tc - 1) - 1 + 1)
                    mi = E.transpose(psA[:, (tc % 2) * 512:(tc % 2) * 512 + 128],
                                     fs2[:, tc * 128:(tc + 1) * 128], self.eye[:, :])
                    mi.then_inc(sPE.h, 1)
            for tc in range(16):
                vtr.append(sPE.inc(1))
            pair_vend[p] = vtr[-1]
            @self.on("vector")
            def _(E, vtr=tuple(vtr), d_vmem=d_vmem):
                E.wait_ge(sDVE.h, d_vmem)
                for tc in range(16):
                    E.wait_ge(sPE.h, vtr[tc])
                    pv = psA[:, (tc % 2) * 512:(tc % 2) * 512 + 128]
                    E.tensor_copy(vbufs[0][:, tc * 65:tc * 65 + 64], pv[:, 0:64])
                    E.tensor_copy(vbufs[1][:, tc * 65:tc * 65 + 64],
                                  pv[:, 64:128]).then_inc(sDVE.h, 1)
            vev = []
            for tc in range(16):
                vev.append(sDVE.inc(1))
            a_vdone = 0
            d_vdone = vev[-1]

            for h01 in range(2):
                r0 = 64 * h01
                vX = vbufs[h01]
                pieces = []
                for kc in range(16):
                    qs = 128 * kc
                    first = True
                    while qs < T:
                        qe = min(T, (qs // 1024 + 1) * 1024)
                        pieces.append((kc, qs, qe, first))
                        qs = qe
                        first = False
                npc = len(pieces)
                s_cnt = [0] * npc
                av_cnt = [0] * npc
                e_cnt = [0] * npc
                m_cnt = {}
                order = []
                for i in range(npc):
                    order.append(("s", i))
                    if i >= 1:
                        order.append(("av", i - 1))
                order.append(("av", npc - 1))
                cnt = sPE.n
                for kind, i in order:
                    cnt += 1
                    if kind == "s":
                        s_cnt[i] = cnt
                    else:
                        av_cnt[i] = cnt
                sPE.n = cnt
                a_exp0 = sACT.n
                for i in range(npc):
                    e_cnt[i] = a_exp0 + (i + 1)
                sACT.n = a_exp0 + npc
                d_m0 = sDVE.n
                nd = 0
                for i in range(npc):
                    if pieces[i][3]:
                        nd += 1
                        m_cnt[i] = d_m0 + nd
                sDVE.n = d_m0 + nd
                @self.on("tensor")
                def _(E, pieces=tuple(pieces), order=tuple(order), s_cnt=tuple(s_cnt),
                      e_cnt=tuple(e_cnt), m_cnt=dict(m_cnt), r0=r0, vX=vX, p=p,
                      d_vdone=d_vdone, d_negc=d_negc, hev=prev_head_evac[0]):
                    E.wait_ge(sDVE.h, d_vdone)
                    E.wait_ge(sDVE.h, d_negc)
                    if hev:
                        E.wait_ge(sDVE.h, hev)
                    for kind, i in order:
                        kc, qs, qe, diag = pieces[i]
                        w = qe - qs
                        if kind == "s":
                            if i >= 2:
                                E.wait_ge(sACT.h, e_cnt[i - 2])
                            ring = i % 2
                            for o in range(0, w, 512):
                                ww = min(512, w - o)
                                mi = E.matmul(psA[:, ring * 1024 + o:ring * 1024 + o + ww],
                                              kp[r0:r0 + 64, kc * 128:(kc + 1) * 128],
                                              qp[r0:r0 + 64, qs + o:qs + o + ww],
                                              start=True, stop=True)
                        else:
                            if i in m_cnt:
                                E.wait_ge(sDVE.h, m_cnt[i])
                            else:
                                E.wait_ge(sACT.h, e_cnt[i])
                            for o in range(0, w, 512):
                                ww = min(512, w - o)
                                mi = E.matmul(psB[0:65, qs + o:qs + o + ww],
                                              vX[:, kc * 65:(kc + 1) * 65],
                                              eslot[i % 3][:, o:o + ww], start=(kc == 0),
                                              stop=(kc == 15))
                        mi.then_inc(sPE.h, 1)
                @self.on("scalar")
                def _(E, pieces=tuple(pieces), s_cnt=tuple(s_cnt), av_cnt=tuple(av_cnt),
                      h01=h01):
                    for i in range(len(pieces)):
                        kc, qs, qe, diag = pieces[i]
                        w = qe - qs
                        ring = i % 2
                        E.wait_ge(sPE.h, s_cnt[i])
                        if i >= 3:
                            E.wait_ge(sPE.h, av_cnt[i - 3])
                        E.activation(eslot[i % 3][:, 0:w], psA[:, ring * 1024:ring * 1024 + w],
                                     AF.Exp, bias=negCb[:, h01:h01 + 1],
                                     scale=0.125).then_inc(sACT.h, 1)
                @self.on("vector")
                def _(E, pieces=tuple(pieces), e_cnt=tuple(e_cnt)):
                    for i in range(len(pieces)):
                        if pieces[i][3]:
                            E.wait_ge(sACT.h, e_cnt[i])
                            E.tensor_mul(eslot[i % 3][:, 0:128], eslot[i % 3][:, 0:128],
                                         tri[:, :]).then_inc(sDVE.h, 1)
                av_last = av_cnt[npc - 1]
                a_lnl = sACT.inc(1)
                a_linv = sACT.inc(1)
                @self.on("scalar")
                def _(E, av_last=av_last, a_lnl=a_lnl):
                    E.wait_ge(sPE.h, av_last)
                    E.activation(fs1[0:1, 0:2048], psB[64:65, 0:2048],
                                 AF.Ln).then_inc(sACT.h, 1)
                    E.wait_ge(sACT.h, a_lnl)
                    E.activation(fs1[0:1, 0:2048], fs1[0:1, 0:2048], AF.Exp,
                                 scale=-1.0).then_inc(sACT.h, 1)
                @self.on("tensor")
                def _(E, a_linv=a_linv, e_last=e_cnt[npc - 1]):
                    E.wait_ge(sACT.h, a_linv)
                    E.wait_ge(sACT.h, e_last)
                    for nt in range(NT):
                        mi = E.matmul(psA[0:64, nt * 512:(nt + 1) * 512], onesf[0:1, 0:64],
                                      fs1[0:1, nt * 512:(nt + 1) * 512], start=True, stop=True)
                    mi.then_inc(sPE.h, 1)
                p_lb = sPE.inc(1)
                @self.on("vector")
                def _(E, p_lb=p_lb, r0=r0, p=p):
                    E.wait_ge(sPE.h, p_lb)
                    E.tensor_copy(fs2[0:64, 0:2048], psA[0:64, 0:2048])
                    self.dfence(E)
                    E.tensor_mul(big1[r0:r0 + 64, p * 2048:(p + 1) * 2048],
                                 psB[0:64, 0:2048], fs2[0:64, 0:2048]).then_inc(sDVE.h, 1)
                sDVE.inc(1)
                prev_head_evac[0] = sDVE.inc(1)
            pair_all_done[p] = prev_head_evac[0]

        xadd_done = [0] * 8
        wo_pe = [0] * 8
        for j in range(8):
            wo_t = self.cload(d["wo_h"][j], wslot[4 + (j % 2)][:, :],
                              pe_war=wo_pe[j - 2] if j >= 2 else pair_vend[6 + (j % 2)])
            ps = psA if (j % 2 == 0) else psB
            war2 = xadd_done[j - 2] if j >= 2 else pair_all_done[7]
            @self.on("tensor")
            def _(E, j=j, wo_t=wo_t, ps=ps, war2=war2):
                self.wpl(E, wo_t)
                if war2:
                    E.wait_ge(sDVE.h, war2)
                sl = wslot[4 + (j % 2)]
                for c in range(DC):
                    for nt in range(NT):
                        mi = E.matmul(ps[:, nt * 512:(nt + 1) * 512], sl[:, c * 128:(c + 1) * 128],
                                      big1[:, c * 2048 + nt * 512:c * 2048 + (nt + 1) * 512],
                                      start=(c == 0), stop=(c == DC - 1))
                mi.then_inc(sPE.h, 1)
            p_j = sPE.inc(1)
            wo_pe[j] = p_j
            @self.on("vector")
            def _(E, j=j, p_j=p_j, ps=ps):
                E.wait_ge(sPE.h, p_j)
                for nt in range(NT):
                    ti = E.tensor_add(x[:, j * 2048 + nt * 512:j * 2048 + (nt + 1) * 512],
                                      x[:, j * 2048 + nt * 512:j * 2048 + (nt + 1) * 512],
                                      ps[:, nt * 512:(nt + 1) * 512])
                ti.then_inc(sDVE.h, 1)
            xadd_done[j] = sDVE.inc(1)
        self.maybe_flush("attn" + tag)
        return wo_pe[7], xadd_done[7]

    def _shift_mix(self, mu, d_after_pe=0):
        xn, big1 = self.xn, self.big1
        sDVE, sPE = self.sDVE, self.sPE
        @self.on("vector")
        def _(E, mu=mu, d_after_pe=d_after_pe):
            if d_after_pe:
                E.wait_ge(sPE.h, d_after_pe)
            for c in range(DC):
                E.tensor_sub(big1[:, c * T + 1:c * T + T], xn[:, c * T + 1:c * T + T],
                             xn[:, c * T:c * T + T - 1])
                self.dfence(E)
                E.scalar_tensor_tensor(big1[:, c * T + 1:c * T + T], big1[:, c * T + 1:c * T + T],
                                       mu[:, c:c + 1], xn[:, c * T:c * T + T - 1],
                                       ALU.mult, ALU.add)
                E.tensor_scalar(big1[:, c * T:c * T + 1], xn[:, c * T:c * T + 1],
                                mu[:, c:c + 1], None, ALU.mult)
            self.dfence(E)
        return sDVE.inc(DC + 1)

    def ffn0(self, tag="", entry_pe=0):
        x, xn, big1, big2, wring = self.x_sb, self.xn, self.big1, self.big2, self.wring
        fs1 = self.fs1
        psA, psB = self.psA, self.psB
        sPE, sACT, sDVE, sW = self.sPE, self.sACT, self.sDVE, self.sW
        sPL = self.sPL
        d = self.d
        wslot = [wring[:, i * 1024:(i + 1) * 1024] for i in range(7)]
        st_buf = wring[:, 6 * 1024:7 * 1024]

        d_xk = self._shift_mix(self.mu_k, d_after_pe=entry_pe)
        evac_dve = [0] * 8
        evac_act = [0] * 8
        kk_pe = [0] * 8
        for j in range(8):
            wk_t = self.cload(d["wkk_h"][j], wslot[j % 2][:, :],
                              pe_war=kk_pe[j - 2] if j >= 2 else entry_pe)
            ps = psA if j % 2 == 0 else psB
            @self.on("tensor")
            def _(E, j=j, wk_t=wk_t, ps=ps, d_xk=d_xk, war=evac_act[j - 2] if j >= 2 else 0):
                self.wpl(E, wk_t)
                if j == 0:
                    E.wait_ge(sDVE.h, d_xk)
                if war:
                    E.wait_ge(sACT.h, war)
                sl = wslot[j % 2]
                for c in range(DC):
                    for nt in range(NT):
                        mi = E.matmul(ps[:, nt * 512:(nt + 1) * 512], sl[:, c * 128:(c + 1) * 128],
                                      big1[:, c * T + nt * 512:c * T + (nt + 1) * 512],
                                      start=(c == 0), stop=(c == DC - 1))
                mi.then_inc(sPE.h, 1)
            p_j = sPE.inc(1)
            kk_pe[j] = p_j
            a_half = [sACT.inc(1), sACT.inc(1)]
            d_half = [sDVE.inc(1), sDVE.inc(1)]
            @self.on("scalar")
            def _(E, j=j, p_j=p_j, ps=ps, war=evac_dve[j - 1] if j >= 1 else 0,
                  d_half=tuple(d_half)):
                E.wait_ge(sPE.h, p_j)
                if war:
                    E.wait_ge(sDVE.h, war)
                E.activation(st_buf, ps[:, 0:1024], AF.Relu).then_inc(sACT.h, 1)
                E.wait_ge(sDVE.h, d_half[0])
                E.activation(st_buf, ps[:, 1024:2048], AF.Relu).then_inc(sACT.h, 1)
            @self.on("vector")
            def _(E, j=j, a_half=tuple(a_half)):
                E.wait_ge(sACT.h, a_half[0])
                E.tensor_mul(big2[:, j * 2048:j * 2048 + 1024], st_buf, st_buf).then_inc(sDVE.h, 1)
                E.wait_ge(sACT.h, a_half[1])
                E.tensor_mul(big2[:, j * 2048 + 1024:(j + 1) * 2048], st_buf, st_buf).then_inc(sDVE.h, 1)
            evac_act[j] = a_half[1]
            evac_dve[j] = d_half[1]
        d_xr = self._shift_mix(self.mu_r, d_after_pe=sPE.n)
        r_act = [0] * 8
        x_dve = [0] * 8
        rv_pe = [0] * 8
        for j in range(8):
            wr_t = self.cload(d["wr_h"][j], wslot[2 + (j % 2)][:, :],
                              pe_war=rv_pe[j - 2] if j >= 2 else entry_pe)
            wv_t = self.cload(d["wvv_h"][j], wslot[4 + (j % 2)][:, :],
                              pe_war=rv_pe[j - 2] if j >= 2 else entry_pe)
            @self.on("tensor")
            def _(E, j=j, wr_t=wr_t, wv_t=wv_t, d_xr=d_xr,
                  war_a=r_act[j - 1] if j >= 1 else 0,
                  war_d=x_dve[j - 1] if j >= 1 else 0):
                self.wpl(E, wr_t)
                if j == 0:
                    E.wait_ge(sDVE.h, d_xr)
                if war_a:
                    E.wait_ge(sACT.h, war_a)
                sl = wslot[2 + (j % 2)]
                for c in range(DC):
                    for nt in range(NT):
                        mi = E.matmul(psA[:, nt * 512:(nt + 1) * 512], sl[:, c * 128:(c + 1) * 128],
                                      big1[:, c * T + nt * 512:c * T + (nt + 1) * 512],
                                      start=(c == 0), stop=(c == DC - 1))
                mi.then_inc(sPE.h, 1)
                self.wpl(E, wv_t)
                if war_d:
                    E.wait_ge(sDVE.h, war_d)
                sl = wslot[4 + (j % 2)]
                for c in range(DC):
                    for nt in range(NT):
                        mi = E.matmul(psB[:, nt * 512:(nt + 1) * 512], sl[:, c * 128:(c + 1) * 128],
                                      big2[:, c * T + nt * 512:c * T + (nt + 1) * 512],
                                      start=(c == 0), stop=(c == DC - 1))
                mi.then_inc(sPE.h, 1)
            p_r = sPE.inc(1)
            p_v = sPE.inc(1)
            rv_pe[j] = p_v
            ra_half = [sACT.inc(1), sACT.inc(1)]
            xd_half = [0, 0]
            sDVE.inc(2)
            xd_half[0] = sDVE.inc(1)
            sDVE.inc(2)
            xd_half[1] = sDVE.inc(1)
            @self.on("scalar")
            def _(E, j=j, p_r=p_r, war=x_dve[j - 1] if j >= 1 else 0,
                  xd_half=tuple(xd_half)):
                E.wait_ge(sPE.h, p_r)
                if war:
                    E.wait_ge(sDVE.h, war)
                E.activation(st_buf, psA[:, 0:1024], AF.Sigmoid).then_inc(sACT.h, 1)
                E.wait_ge(sDVE.h, xd_half[0])
                E.activation(st_buf, psA[:, 1024:2048], AF.Sigmoid).then_inc(sACT.h, 1)
            @self.on("vector")
            def _(E, j=j, p_v=p_v, ra_half=tuple(ra_half)):
                E.wait_ge(sPE.h, p_v)
                for h2 in range(2):
                    E.wait_ge(sACT.h, ra_half[h2])
                    for nt in (2 * h2, 2 * h2 + 1):
                        E.tensor_mul(fs1[:, 0:512], psB[:, nt * 512:(nt + 1) * 512],
                                     st_buf[:, (nt % 2) * 512:(nt % 2) * 512 + 512])
                        self.dfence(E)
                        ti = E.tensor_add(x[:, j * 2048 + nt * 512:j * 2048 + (nt + 1) * 512],
                                          x[:, j * 2048 + nt * 512:j * 2048 + (nt + 1) * 512],
                                          fs1[:, 0:512])
                    ti.then_inc(sDVE.h, 1)
            r_act[j] = ra_half[1]
            x_dve[j] = xd_half[1]
        self.maybe_flush("ffn0" + tag)
        return rv_pe[7], x_dve[7]

    def fold(self):
        big1, big2, wring, fs1 = self.big1, self.big2, self.wring, self.fs1
        psA, psB = self.psA, self.psB
        sPE, sACT, sDVE, sW, sX = self.sPE, self.sACT, self.sDVE, self.sW, self.sX
        sPL = self.sPL
        d, l1w, l1wb, cvec = self.d, self.l1w, self.l1wb, self.cvec
        wslot = [wring[:, i * 1024:(i + 1) * 1024] for i in range(7)]
        sl_ = lambda i: l1w[:, i * 16:(i + 1) * 16]

        aN = [sACT.inc(1) for _ in range(4)]
        a2 = aN[3]
        @self.on("scalar")
        def _(E, aN=tuple(aN)):
            E.activation(sl_(5), sl_(2), AF.Exp).then_inc(sACT.h, 1)
            E.activation(sl_(6), sl_(3), AF.Exp).then_inc(sACT.h, 1)
            E.wait_ge(sACT.h, aN[0])
            E.activation(sl_(2), sl_(4), AF.Exp).then_inc(sACT.h, 1)
            E.wait_ge(sACT.h, aN[2])
            E.activation(sl_(5), sl_(5), AF.Exp, scale=-1.0).then_inc(sACT.h, 1)
        d1 = sDVE.inc(1)
        @self.on("vector")
        def _(E, a2=a2, d1=d1):
            E.wait_ge(sACT.h, a2)
            E.tensor_scalar(sl_(7), sl_(6), -1.0, 1.5707963267948966, ALU.mult, ALU.add)
            E.tensor_scalar(sl_(9), sl_(2), 2.0, None, ALU.mult)
            E.tensor_scalar(sl_(10), sl_(2), -2.0, None, ALU.mult)
            self.dfence(E)
        a3 = sACT.inc(1)
        @self.on("scalar")
        def _(E, d1=d1, a3=a3):
            E.wait_ge(sDVE.h, d1)
            E.activation(sl_(7), sl_(7), AF.Sin)
            E.activation(sl_(8), sl_(6), AF.Sin).then_inc(sACT.h, 1)
        d2 = sDVE.inc(1); d3 = sDVE.inc(1); d4 = sDVE.inc(1); d_vecs = sDVE.inc(1)
        @self.on("vector")
        def _(E, a3=a3, d2=d2, d3=d3, d4=d4, d_vecs=d_vecs):
            E.wait_ge(sACT.h, a3)
            E.tensor_mul(sl_(3), sl_(5), sl_(7))
            E.tensor_mul(sl_(4), sl_(5), sl_(8)).then_inc(sDVE.h, 1)
            E.wait_ge(sDVE.h, d2)
            E.tensor_mul(sl_(6), sl_(3), sl_(0))
            E.tensor_mul(sl_(7), sl_(4), sl_(1))
            E.tensor_mul(sl_(8), sl_(3), sl_(1))
            E.tensor_mul(sl_(11), sl_(4), sl_(0)).then_inc(sDVE.h, 1)
            E.wait_ge(sDVE.h, d3)
            E.tensor_sub(sl_(6), sl_(6), sl_(7))
            E.tensor_add(sl_(8), sl_(8), sl_(11)).then_inc(sDVE.h, 1)
            E.wait_ge(sDVE.h, d4)
            E.tensor_scalar(l1wb[:, 0:16], sl_(6), 2.0, None, ALU.mult)
            E.tensor_scalar(l1wb[:, 16:32], sl_(8), -2.0, None, ALU.mult)
            self.dfence(E)
        self.flush("foldv")

        pe0 = sPE.n
        ctg = []
        for sc in range(SC):
            war = pe0 + sc - 1 if pe0 + sc - 1 > 0 else 0
            t1 = self.cload(d["cre_h"][:, sc * 1024:(sc + 1) * 1024], wslot[sc % 2][:, :], pe_war=war)
            t2 = self.cload(d["cim_h"][:, sc * 1024:(sc + 1) * 1024], wslot[2 + sc % 2][:, :], pe_war=war)
            ctg.append(t2)
        @self.on("tensor")
        def _(E, ctg=tuple(ctg), d_vecs=d_vecs, pe0=pe0):
            E.wait_ge(sDVE.h, d_vecs)
            for sc in range(SC):
                self.wpl(E, ctg[sc])
                for jc in range(8):
                    ps = psA if jc < 4 else psB
                    col = (jc % 4) * 512
                    E.matmul(ps[:, col:col + 1], wslot[sc % 2][:, jc * 128:(jc + 1) * 128],
                             l1wb[:, sc:sc + 1], start=(sc == 0), stop=False)
                    mi = E.matmul(ps[:, col:col + 1], wslot[2 + sc % 2][:, jc * 128:(jc + 1) * 128],
                                  l1wb[:, 16 + sc:16 + sc + 1], start=False, stop=(sc == SC - 1))
                mi.then_inc(sPE.h, 1)
        for sc in range(SC):
            sPE.inc(1)
        p_cvec = sPE.n
        @self.on("vector")
        def _(E, p_cvec=p_cvec):
            E.wait_ge(sPE.h, p_cvec)
            for jc in range(8):
                ps = psA if jc < 4 else psB
                col = (jc % 4) * 512
                ti = E.tensor_copy(cvec[:, jc:jc + 1], ps[:, col:col + 1])
            ti.then_inc(sDVE.h, 1)
        d_cvec = sDVE.inc(1)
        self.flush("foldc")

        for c8 in range(8):
            dt_t = self.cload(d["dt_h"][:, c8 * 1024:(c8 + 1) * 1024],
                              big2[:, 8192 + c8 * 1024:8192 + (c8 + 1) * 1024])
        self._dt_t = dt_t
        mm_done = {}
        for g in range(2):
            for ri, (bh, gsl) in enumerate((("bre_h", 9), ("bim_h", 10))):
                pidx = g * 2 + ri
                dve0 = sDVE.n
                ch = "cre_h" if ri == 0 else "cim_h"
                pe0 = sPE.n
                btg = []
                ctg = []
                for sc in range(SC):
                    bwar = dve0 + sc - 1 if dve0 + sc - 1 > 0 else 0
                    btg.append(self.cload(d[bh][:, sc * 1024:(sc + 1) * 1024],
                                          wslot[sc % 2][:, :], dve_war=bwar))
                    war = pe0 + sc - 1 if pe0 + sc - 1 > 0 else 0
                    ctg.append(self.cload(d[ch][:, sc * 1024:(sc + 1) * 1024],
                                          wslot[4 + sc % 2][:, :], pe_war=war))
                @self.on("vector")
                def _(E, btg=tuple(btg), gsl=gsl, pidx=pidx, mmp=dict(mm_done)):
                    for sc in range(SC):
                        self.wpl(E, btg[sc])
                        if pidx > 0:
                            E.wait_ge(sPE.h, mmp[(pidx - 1, sc)])
                        E.tensor_scalar(big1[:, sc * 1024:(sc + 1) * 1024],
                                        wslot[sc % 2][:, :],
                                        l1w[:, gsl * 16 + sc:gsl * 16 + sc + 1], None,
                                        ALU.mult)
                        self.dfence(E)
                sDVE.inc(SC)
                @self.on("tensor")
                def _(E, g=g, ctg=tuple(ctg), dve0=dve0, pidx=pidx, pe0=pe0, ri=ri):
                    for sc in range(SC):
                        self.wpl(E, ctg[sc])
                        E.wait_ge(sDVE.h, dve0 + sc + 1)
                        for djl in range(4):
                            dj = g * 4 + djl
                            for jh in range(2):
                                ps = psA if djl < 2 else psB
                                col = ((djl % 2) * 2 + jh) * 512
                                mi = E.matmul(ps[:, col:col + 512],
                                              big1[:, sc * 1024 + dj * 128:sc * 1024 + (dj + 1) * 128],
                                              wslot[4 + sc % 2][:, jh * 512:(jh + 1) * 512],
                                              start=(ri == 0 and sc == 0),
                                              stop=(ri == 1 and sc == SC - 1))
                        mi.then_inc(sPE.h, 1)
                for sc in range(SC):
                    mm_done[(pidx, sc)] = sPE.inc(1)
                if getattr(self, "_fold_debug", False) and pidx == 0:
                    @self.on("vector")
                    def _(E, t=mm_done[(0, SC - 1)]):
                        E.wait_ge(sPE.h, t)
                        E.tensor_copy(fs1[:, 0:512], psA[:, 0:512])
                        E.tensor_copy(fs1[:, 512:1024], psA[:, 512:1024]).then_inc(sDVE.h, 1)
                    sDVE.inc(1)
                    dbg = sDVE.n
                    @self.on("tensor")
                    def _(E, dbg=dbg):
                        E.wait_ge(sDVE.h, dbg)
            @self.on("vector")
            def _(E, g=g, t=mm_done[(g * 2 + 1, SC - 1)], dtt=self._dt_t):
                E.wait_ge(sPE.h, t)
                self.wpl_all(E, dtt)
                for djl in range(4):
                    dj = g * 4 + djl
                    for jh in range(2):
                        ps = psA if djl < 2 else psB
                        col = ((djl % 2) * 2 + jh) * 512
                        E.scalar_tensor_tensor(
                            big2[:, dj * 1024 + jh * 512:dj * 1024 + jh * 512 + 512],
                            ps[:, col:col + 512], 1.0,
                            big2[:, 8192 + dj * 1024 + jh * 512:8192 + dj * 1024 + jh * 512 + 512],
                            ALU.mult, ALU.add)
                self.dfence(E)
            sDVE.inc(1)
            if g == 0:
                mm_done[(1, -1)] = None
                ev = sDVE.n
                @self.on("tensor")
                def _(E, ev=ev):
                    E.wait_ge(sDVE.h, ev)
        ev_all = sDVE.n
        prev = sX.n
        sX.inc(16)
        @self.on("sync")
        def _(E, ev_all=ev_all, prev=prev):
            E.wait_ge(sDVE.h, ev_all)
            if prev:
                E.wait_ge(sX.h, prev)
            E.dma_start(out=d["mtot_scr"], in_=big2[:, 0:8192]).then_inc(sX.h, 16)
        self.flush("foldm")

    def l1apply(self, tag="", entry_dve=0, entry_pe=0):
        x, xn, big2 = self.x_sb, self.xn, self.big2
        psA, psB = self.psA, self.psB
        sPE, sDVE, sX = self.sPE, self.sDVE, self.sX
        d, cvec = self.d, self.cvec
        mt = sX.inc(16)
        @self.on("sync")
        def _(E, mt=mt, ep=entry_pe):
            if ep:
                E.wait_ge(sPE.h, ep)
            E.wait_ge(sX.h, mt - 16)
            E.dma_start(out=big2[:, 0:8192], in_=d["mtot_scr"]).then_inc(sX.h, 16)
        x_dve = [0] * 8
        for jc in range(8):
            ps = psA if jc % 2 == 0 else psB
            @self.on("tensor")
            def _(E, jc=jc, ps=ps, mt=mt, war=x_dve[jc - 2] if jc >= 2 else 0,
                  ed=entry_dve if jc == 0 else 0):
                if jc == 0:
                    E.wait_ge(sX.h, mt)
                if ed:
                    E.wait_ge(sDVE.h, ed)
                if war:
                    E.wait_ge(sDVE.h, war)
                for c in range(DC):
                    for nt in range(NT):
                        mi = E.matmul(ps[:, nt * 512:(nt + 1) * 512],
                                      big2[:, c * 1024 + jc * 128:c * 1024 + jc * 128 + 128],
                                      xn[:, c * T + nt * 512:c * T + (nt + 1) * 512],
                                      start=(c == 0), stop=(c == DC - 1))
                mi.then_inc(sPE.h, 1)
            p_j = sPE.inc(1)
            @self.on("vector")
            def _(E, jc=jc, p_j=p_j, ps=ps):
                E.wait_ge(sPE.h, p_j)
                for nt in range(NT):
                    E.scalar_tensor_tensor(
                        x[:, jc * 2048 + nt * 512:jc * 2048 + (nt + 1) * 512],
                        ps[:, nt * 512:(nt + 1) * 512], cvec[:, jc:jc + 1],
                        x[:, jc * 2048 + nt * 512:jc * 2048 + (nt + 1) * 512],
                        ALU.add, ALU.add)
                self.dfence(E)
            x_dve[jc] = sDVE.inc(1)
        self.maybe_flush("l1a" + tag)
        return p_j, x_dve[7]

    def gating(self, tag="", entry_dve=0):
        xn, gtmp, scT = self.xn, self.gtmp, self.scT
        psA, psB = self.psA, self.psB
        sPE, sACT, sDVE = self.sPE, self.sACT, self.sDVE
        gw, ebias, eye, w2rows = self.gwsb, self.ebias, self.eye, self.w2rows
        @self.on("tensor")
        def _(E, ed=entry_dve):
            if ed:
                E.wait_ge(sDVE.h, ed)
            for tt in range(16):
                for c in range(DC):
                    mi = E.matmul(psA[:, tt * 8:(tt + 1) * 8],
                                  xn[:, c * T + tt * 128:c * T + (tt + 1) * 128],
                                  gw[:, c * 8:(c + 1) * 8],
                                  start=(c == 0), stop=(c == DC - 1))
            mi.then_inc(sPE.h, 1)
        p_sc = sPE.inc(1)
        @self.on("scalar")
        def _(E, p_sc=p_sc):
            E.wait_ge(sPE.h, p_sc)
            E.activation(scT[:, 0:128], psA[:, 0:128], AF.Sigmoid).then_inc(sACT.h, 1)
        a_sc = sACT.inc(1)
        biased = gtmp[:, 0:128]
        m1 = gtmp[:, 128:144]
        mask1 = gtmp[:, 144:272]
        cnt = gtmp[:, 272:288]
        b2 = gtmp[:, 288:416]
        m2 = gtmp[:, 416:432]
        ge2 = gtmp[:, 432:448]
        dm = gtmp[:, 448:464]
        t8 = gtmp[:, 464:480]
        thr = gtmp[:, 480:496]
        w2t = b2
        d_base = sDVE.n
        d_w2 = sDVE.inc(11)
        @self.on("vector")
        def _(E, a_sc=a_sc, d_base=d_base):
            _b = bass
            def r3(ap):
                return ap.rearrange("p (t e) -> p t e", e=8)
            def bcast(big3, small):
                a, b2_ = _b.broadcast_tensor_aps(big3, small.rearrange("p (t e) -> p t e", e=1))
                return a, b2_
            E.wait_ge(sACT.h, a_sc)
            E.tensor_add(biased, scT[:, 0:128], ebias[:, 0:128])
            self.dfence(E)
            E.tensor_reduce(m1, r3(biased), AX.X, ALU.max)
            self.dfence(E)
            a, b = bcast(r3(biased), m1)
            E.tensor_tensor(r3(mask1), a, b, ALU.is_ge)
            self.dfence(E)
            E.tensor_reduce(cnt, r3(mask1), AX.X, ALU.add)
            E.scalar_tensor_tensor(b2, mask1, -1e30, biased, ALU.mult, ALU.add)
            self.dfence(E)
            E.tensor_reduce(m2, r3(b2), AX.X, ALU.max)
            E.tensor_scalar(ge2, cnt, 1.5, None, ALU.is_ge)
            self.dfence(E)
            E.tensor_sub(dm, m1, m2)
            self.dfence(E)
            E.tensor_mul(t8, dm, ge2)
            self.dfence(E)
            E.tensor_add(thr, t8, m2)
            self.dfence(E)
            a, b = bcast(r3(biased), thr)
            E.tensor_tensor(r3(mask1), a, b, ALU.is_ge)
            self.dfence(E)
            E.tensor_mul(mask1, scT[:, 0:128], mask1)
            self.dfence(E)
            E.tensor_mul(w2t, mask1, mask1).then_inc(sDVE.h, 1)
        if getattr(self, "_gate_half", False):
            self.flush("gate" + tag)
            return
        tr_done = []
        @self.on("tensor")
        def _(E, d_w2=d_w2):
            E.wait_ge(sDVE.h, d_w2)
            for tt in range(16):
                if tt >= 4:
                    E.wait_ge(sDVE.h, d_w2 + tt - 3)
                mi = E.transpose(psB[0:8, (tt % 4) * 512:(tt % 4) * 512 + 128],
                                 w2t[:, tt * 8:(tt + 1) * 8], eye[:, :])
                mi.then_inc(sPE.h, 1)
        tr0 = sPE.n
        for tt in range(16):
            tr_done.append(sPE.inc(1))
        @self.on("vector")
        def _(E, tr=tuple(tr_done)):
            for tt in range(16):
                E.wait_ge(sPE.h, tr[tt])
                E.tensor_copy(w2rows[0:8, tt * 128:(tt + 1) * 128],
                              psB[0:8, (tt % 4) * 512:(tt % 4) * 512 + 128]).then_inc(sDVE.h, 1)
        gate_dve = 0
        for tt in range(16):
            gate_dve = sDVE.inc(1)
        self.maybe_flush("gate" + tag)
        return gate_dve

    def moe(self, tag="", entry_pe=0, entry_dve=0):
        x, xn, big1, big2, wring = self.x_sb, self.xn, self.big1, self.big2, self.wring
        psA, psB = self.psA, self.psB
        sPE, sACT, sDVE = self.sPE, self.sACT, self.sDVE
        sPL = self.sPL
        d = self.d
        wslot = [wring[:, i * 1024:(i + 1) * 1024] for i in range(7)]
        w2b = wslot[4]
        s_sl = wslot[5]
        t_sl = wslot[6]
        wu_k = [0]
        wu_pe = {}
        prev_down_pe = [entry_pe]
        prev_xadd = [entry_dve]
        prev_upevac = [0]
        first_pe = entry_pe
        wu_tgts = {}

        def emit_wu(e, h, f):
            k = wu_k[0]; wu_k[0] += 1
            wu_tgts[(e, h, f)] = (self.cload(d["wu_h"][e, f], wslot[k % 4][:, :],
                                             pe_war=wu_pe.get(k - 4, first_pe)), k)
        for e in range(NE):
            wd_t = 0
            for fc in range(FC):
                wd_t = self.cload(d["wd_h"][e][:, fc * 1024:(fc + 1) * 1024],
                                  big2[:, fc * 1024:(fc + 1) * 1024],
                                  pe_war=prev_down_pe[0])
            for h in range(2):
                for f in range(4):
                    if (e, h, f) not in wu_tgts:
                        emit_wu(e, h, f)
                @self.on("tensor")
                def _(E, e=e, h=h, dwar=prev_xadd[0]):
                    if dwar:
                        E.wait_ge(sDVE.h, dwar)
                    for hnt in range(2):
                        mi = E.matmul(psB[:, hnt * 512:(hnt + 1) * 512],
                                      self.ind8[0:8, e * 128:(e + 1) * 128],
                                      self.w2rows[0:8, h * 1024 + hnt * 512:h * 1024 + (hnt + 1) * 512],
                                      start=True, stop=True)
                    mi.then_inc(sPE.h, 1)
                p_bc = sPE.inc(1)
                @self.on("scalar")
                def _(E, p_bc=p_bc, uwar=prev_upevac[0], fp=first_pe):
                    E.wait_ge(sPE.h, p_bc)
                    if uwar:
                        E.wait_ge(sDVE.h, uwar)
                    elif fp:
                        E.wait_ge(sPE.h, fp)
                    E.activation(w2b, psB[:, 0:1024], AF.Identity).then_inc(sACT.h, 1)
                a_w2b = sACT.inc(1)
                up_act = [0] * FC
                up_dve = [0] * FC
                for f in range(FC):
                    if (e, h, f) not in wu_tgts:
                        emit_wu(e, h, f)
                    wu_t, k = wu_tgts[(e, h, f)]
                    @self.on("tensor")
                    def _(E, e=e, h=h, f=f, k=k, wu_t=wu_t,
                          awar=up_act[f - 2] if f >= 2 else 0):
                        self.wpl(E, wu_t)
                        if awar:
                            E.wait_ge(sACT.h, awar)
                        sl = wslot[k % 4]
                        for c in range(DC):
                            for ntu in range(2):
                                mi = E.matmul(
                                    psA[:, (f % 2) * 1024 + ntu * 512:(f % 2) * 1024 + (ntu + 1) * 512],
                                    sl[:, c * 128:(c + 1) * 128],
                                    xn[:, c * T + h * TH + ntu * 512:c * T + h * TH + (ntu + 1) * 512],
                                    start=(c == 0), stop=(c == DC - 1))
                        mi.then_inc(sPE.h, 1)
                    p_f = sPE.inc(1)
                    wu_pe[k] = p_f
                    @self.on("scalar")
                    def _(E, f=f, p_f=p_f, a_w2b=a_w2b,
                          dwar=up_dve[f - 1] if f >= 1 else 0,
                          fp=first_pe if (e == 0 and h == 0 and f == 0) else 0):
                        E.wait_ge(sPE.h, p_f)
                        E.wait_ge(sACT.h, a_w2b)
                        if dwar:
                            E.wait_ge(sDVE.h, dwar)
                        elif fp:
                            E.wait_ge(sPE.h, fp)
                        E.activation(s_sl, psA[:, (f % 2) * 1024:(f % 2) * 1024 + 1024],
                                     AF.Relu).then_inc(sACT.h, 1)
                    up_act[f] = sACT.inc(1)
                    @self.on("vector")
                    def _(E, f=f, a_f=up_act[f], pwar=prev_down_pe[0]):
                        E.wait_ge(sACT.h, a_f)
                        if pwar and f < 2:
                            E.wait_ge(sPE.h, pwar)
                        E.tensor_mul(t_sl, s_sl, w2b)
                        self.dfence(E)
                        E.tensor_mul(big1[:, f * 1024:(f + 1) * 1024], t_sl,
                                     s_sl).then_inc(sDVE.h, 1)
                    sDVE.inc(1)
                    up_dve[f] = sDVE.inc(1)
                prev_upevac[0] = up_dve[FC - 1]
                if h == 0:
                    for f in range(4):
                        emit_wu(e, 1, f)
                elif e + 1 < NE:
                    for f in range(4):
                        emit_wu(e + 1, 0, f)
                for j in range(8):
                    @self.on("tensor")
                    def _(E, e=e, h=h, j=j, hdone=up_dve[FC - 1], wd_t=wd_t,
                          dwar=prev_xadd[0] if j >= 2 else 0,
                          jwar=0):
                        E.wait_ge(sDVE.h, hdone)
                        self.wpl_all(E, wd_t)
                        if dwar:
                            E.wait_ge(sDVE.h, dwar)
                        for f in range(FC):
                            for ntd in range(2):
                                mi = E.matmul(
                                    psB[:, (j % 2) * 1024 + ntd * 512:(j % 2) * 1024 + (ntd + 1) * 512],
                                    big2[:, f * 1024 + j * 128:f * 1024 + j * 128 + 128],
                                    big1[:, f * 1024 + ntd * 512:f * 1024 + (ntd + 1) * 512],
                                    start=(f == 0), stop=(f == FC - 1))
                        mi.then_inc(sPE.h, 1)
                    p_j = sPE.inc(1)
                    prev_down_pe[0] = p_j
                    @self.on("vector")
                    def _(E, h=h, j=j, p_j=p_j):
                        E.wait_ge(sPE.h, p_j)
                        for ntd in range(2):
                            ti = E.tensor_add(
                                x[:, j * 2048 + h * TH + ntd * 512:j * 2048 + h * TH + (ntd + 1) * 512],
                                x[:, j * 2048 + h * TH + ntd * 512:j * 2048 + h * TH + (ntd + 1) * 512],
                                psB[:, (j % 2) * 1024 + ntd * 512:(j % 2) * 1024 + (ntd + 1) * 512])
                        ti.then_inc(sDVE.h, 1)
                    prev_xadd[0] = sDVE.inc(1)
        self.maybe_flush("moe" + tag)
        return prev_xadd[0]



def build_full(nc, stack, taps=()):
    b = Builder(nc, stack)
    b.load_consts()
    b.fold()
    b.seq_merge = MERGE_SEQ
    prev_dve = b.sDVE.n
    prev_pe = b.sPE.n
    for s in range(S):
        b.load_x(s)
        ln1_dve = b.ln(f"1s{s}", entry_dve=prev_dve)
        wo_pe, attn_dve = b.attn(f"s{s}", entry_dve=ln1_dve, entry_pe=prev_pe)
        rv_pe, ffn_dve = b.ffn0(f"s{s}", entry_pe=wo_pe)
        ln2_dve = b.ln(f"2s{s}", entry_dve=ffn_dve)
        l1_pe, l1_dve = b.l1apply(f"s{s}", entry_dve=ln2_dve, entry_pe=rv_pe)
        gate_dve = b.gating(f"s{s}", entry_dve=l1_dve)
        moe_dve = b.moe(f"s{s}", entry_pe=l1_pe, entry_dve=gate_dve)
        b.store_x(s, entry_dve=moe_dve)
        prev_dve = moe_dve
        prev_pe = b.sPE.n
    b.seq_merge = False
    b.flush("seqs")
    for name, buf, shape, dtype in taps:
        b.tap(name, buf(b), shape, dtype)
    b.flush_taps()
    b.finish()
    return b


def numpy_ref(inp, x_slice):
    def ln(v):
        m = v.mean(-1, keepdims=True)
        s = v.var(-1, keepdims=True)
        return (v - m) / np.sqrt(s + 1e-5)
    out = []
    wq = np.asarray(inp["L0_attn_wq"]); wk = np.asarray(inp["L0_attn_wk"])
    wv = np.asarray(inp["L0_attn_wv"]); wo = np.asarray(inp["L0_attn_wo"])
    Wr = np.asarray(inp["L0_ffn_W_r"]); Wk = np.asarray(inp["L0_ffn_W_k"])
    Wv = np.asarray(inp["L0_ffn_W_v"])
    mur = np.asarray(inp["L0_ffn_mu_r"]); muk = np.asarray(inp["L0_ffn_mu_k"])
    nu = np.asarray(inp["L1_attn_nu_log"]); th = np.asarray(inp["L1_attn_theta_log"])
    ga = np.asarray(inp["L1_attn_gamma_log"])
    xre = np.asarray(inp["L1_attn_x_re"]); xim = np.asarray(inp["L1_attn_x_im"])
    Bre = np.asarray(inp["L1_attn_B_re"]); Bim = np.asarray(inp["L1_attn_B_im"])
    Cre = np.asarray(inp["L1_attn_C_re"]); Cim = np.asarray(inp["L1_attn_C_im"])
    Dm = np.asarray(inp["L1_attn_D"])
    gwm = np.asarray(inp["L1_ffn_gate_w"]); wu = np.asarray(inp["L1_ffn_w_up"])
    wd = np.asarray(inp["L1_ffn_w_down"]); eb = np.asarray(inp["L1_ffn_expert_bias"])
    lam = np.exp(-np.exp(nu)); lre = lam * np.cos(np.exp(th)); lim = lam * np.sin(np.exp(th))
    gam = np.exp(ga)
    cre = lre * xre - lim * xim
    cim = lre * xim + lim * xre
    for xs in x_slice:
        x = xs.astype(np.float32)
        xn = ln(x)
        att = np.zeros_like(x)
        for hh in range(16):
            q = xn @ wq[hh].T; k = xn @ wk[hh].T; v = xn @ wv[hh].T
            sco = (q @ k.T) / 8.0
            mask = np.tril(np.ones((T, T), dtype=bool))
            sco = np.where(mask, sco, -np.inf)
            m = sco.max(-1, keepdims=True)
            eo = np.exp(sco - m)
            w = eo / eo.sum(-1, keepdims=True)
            att[:, hh * 64:(hh + 1) * 64] = w @ v
        x = x + att @ wo.T
        xprev = np.vstack([np.zeros((1, D), np.float32), xn[:-1]])
        xr = mur * xn + (1 - mur) * xprev
        xk = muk * xn + (1 - muk) * xprev
        r = 1 / (1 + np.exp(-(xr @ Wr.T)))
        kk = np.maximum(xk @ Wk.T, 0) ** 2
        x = x + r * (kk @ Wv.T)
        xn2 = ln(x)
        Bu_re = xn2 @ Bre.T; Bu_im = xn2 @ Bim.T
        zr = lre * xre - lim * xim + gam * Bu_re
        zi = lre * xim + lim * xre + gam * Bu_im
        y = 2 * (zr @ Cre.T - zi @ Cim.T) + xn2 @ Dm.T
        x = x + y
        sc = 1 / (1 + np.exp(-(xn2 @ gwm.T)))
        bi = sc + eb
        sel = np.argsort(-bi, axis=-1, kind="stable")[:, :2]
        outm = np.zeros_like(x)
        for ee in range(8):
            wt = np.where((sel == ee).any(-1), np.take_along_axis(sc, sel, -1)[
                np.arange(sc.shape[0]), (sel == ee).argmax(-1)] * (sel == ee).any(-1), 0.0)
            xe = xn2 * wt[:, None]
            hdd = np.maximum(xe @ wu[ee].T, 0) ** 2
            outm += hdd @ wd[ee].T
        x = x + outm
        out.append(x)
    return np.stack(out)



from concourse.bass_utils import run_bass_kernel_spmd


def kernel(**inputs):
    common, xs = host_prep(inputs, n_cores=8)
    nc = bass.Bass("TRN2", target_bir_lowering=False)
    with ExitStack() as stack:
        build_full(nc, stack)
    in_maps = []
    for ci in range(8):
        m = dict(common)
        m["x_in"] = xs[ci]
        in_maps.append(m)
    res = run_bass_kernel_spmd(nc, in_maps, core_ids=list(range(8)))
    out = np.concatenate([unprep_x(res.results[ci]["x_out"]) for ci in range(8)], axis=0)
    return np.ascontiguousarray(out.astype(np.float32))
